# version 05
# speedup vs baseline: 1.2768x; 1.0106x over previous
import math
from contextlib import ExitStack

import numpy as np
import ml_dtypes

import concourse.bass as bass
import concourse.mybir as mybir
from concourse.bass_utils import run_bass_kernel_spmd

F32 = mybir.dt.float32
BF16 = mybir.dt.bfloat16
AF = mybir.ActivationFunctionType
ALU = mybir.AluOpType
AXL = mybir.AxisListType
BF16NP = ml_dtypes.bfloat16

N_CORES = 8
D = 1024
NCH = 8
NEG = -1.0e30


class Geom:
    def __init__(self, t_own=2048, pad_l=1280, pad_r=128):
        assert t_own % 256 == 0 and pad_l % 128 == 0 and pad_r % 128 == 0
        self.t_own = t_own
        self.pad_l = pad_l
        self.pad_r = pad_r
        self.t_buf = pad_l + t_own + pad_r
        self.seq = 4 * t_own
        self.n_tiles = self.t_buf // 128
        self.own_lo = pad_l
        self.own_hi = pad_l + t_own


GEOM_FULL = Geom()


ENGINES = ("tensor", "vector", "scalar", "gpsimd", "sync")


class _OpRec:
    __slots__ = ("engine", "kind", "sem", "val", "waits", "clock", "emit", "tag")

    def __init__(self, engine, kind, sem, val, waits, clock, emit, tag):
        self.engine = engine
        self.kind = kind
        self.sem = sem
        self.val = val
        self.waits = waits
        self.clock = clock
        self.emit = emit
        self.tag = tag


class Sched:

    DMA_ROT = 8

    def __init__(self):
        self.ops = {e: [] for e in ENGINES}
        self.counts = {}
        self.last_writer = {}
        self.readers = {}
        self.eng_clock = {e: {} for e in ENGINES}
        self.dma_rot = {}

    def _sem_of(self, engine, kind):
        if kind == "dma":
            j = self.dma_rot.get(engine, 0)
            self.dma_rot[engine] = j + 1
            return f"{engine}_dma{j % self.DMA_ROT}"
        if kind == "cc":
            return "cc"
        return engine

    def op(self, engine, emit, reads=(), writes=(), kind="compute",
           accum=False, tag=""):
        semkey = self._sem_of(engine, kind)
        inc = 16 if kind == "dma" else 1
        val = self.counts.get(semkey, 0) + inc
        self.counts[semkey] = val

        deps = []
        for r in reads:
            w = self.last_writer.get(r)
            if w is not None:
                deps.append(w)
        for w in writes:
            lw = self.last_writer.get(w)
            if lw is not None and not (accum and lw.engine == engine == "tensor"):
                deps.append(lw)
            deps.extend(self.readers.get(w, ()))

        clk = self.eng_clock[engine]
        wd = {}
        for d in deps:
            if clk.get(d.sem, 0) >= d.val:
                continue
            wd[d.sem] = max(wd.get(d.sem, 0), d.val)
            for k, v in d.clock.items():
                if clk.get(k, 0) < v:
                    clk[k] = v
            if clk.get(d.sem, 0) < d.val:
                clk[d.sem] = d.val
        if kind in ("dma", "cc") and val > inc and clk.get(semkey, 0) < val - inc:
            wd[semkey] = max(wd.get(semkey, 0), val - inc)
            clk[semkey] = max(clk.get(semkey, 0), val - inc)
        waits = sorted(wd.items())

        myclock = dict(clk)
        myclock[semkey] = val
        rec = _OpRec(engine, kind, semkey, val, waits, myclock, emit, tag)
        self.ops[engine].append(rec)

        for w in writes:
            self.last_writer[w] = rec
            self.readers[w] = []
        for r in reads:
            self.readers.setdefault(r, []).append(rec)
        return rec


class Arena:

    GRAIN = 512

    def __init__(self, t, name, n):
        self.t = t
        self.name = name
        self.n = n
        self.free_list = [(0, n)]
        self.hwm = 0

    def alloc(self, n, align=128, side="lo"):
        order = range(len(self.free_list))
        if side == "hi":
            order = reversed(list(order))
        for i in order:
            s, l = self.free_list[i]
            if side == "hi":
                a = (s + l - n) // align * align
                if a < s:
                    continue
            else:
                a = (s + align - 1) // align * align
                if a + n > s + l:
                    continue
            pre = (s, a - s)
            post = (a + n, s + l - (a + n))
            repl = [p for p in (pre, post) if p[1] > 0]
            self.free_list[i:i + 1] = repl
            self.hwm = max(self.hwm, a + n)
            return a
        used = self.n - sum(l for _, l in self.free_list)
        raise MemoryError(
            f"arena {self.name}: cannot alloc {n} (cap {self.n}, in use "
            f"{used}, free_list {self.free_list})")

    def free(self, off, n):
        self.free_list.append((off, n))
        self.free_list.sort()
        out = []
        for s, l in self.free_list:
            if out and out[-1][0] + out[-1][1] == s:
                out[-1] = (out[-1][0], out[-1][1] + l)
            else:
                out.append((s, l))
        self.free_list = [tuple(x) for x in out]


class Buf:

    def __init__(self, arena, base, nch, width, lead=0, name=""):
        self.arena = arena
        self.t = arena.t
        self.base = base
        self.nch = nch
        self.width = width
        self.lead = lead
        self.stride = width + lead
        self.name = name

    def ap(self, c, lo, hi, dp=0, dn=128):
        off = self.base + c * self.stride + self.lead
        return self.t[dp:dp + dn, off + lo: off + hi]

    def keys(self, c, lo, hi):
        g = Arena.GRAIN
        off = self.base + c * self.stride + self.lead
        return [(self.arena.name, i)
                for i in range((off + lo) // g, (off + hi + g - 1) // g)]

    def kall(self):
        g = Arena.GRAIN
        n = self.nch * self.stride
        return [(self.arena.name, i)
                for i in range(self.base // g, (self.base + n + g - 1) // g)]

    def full_ap(self):
        return self.t[:, self.base: self.base + self.nch * self.stride]


def dkey(name, *idx):
    return ("dram:" + name,) + idx



def _chunk_cols(vec, n):
    v = np.asarray(vec, np.float32).reshape(n, 128).T
    return np.ascontiguousarray(v)


def _t2(a):
    return np.ascontiguousarray(np.asarray(a, np.float32).T)


def stage_inputs(inputs, geom):
    g = geom
    x = np.asarray(inputs["L0_attn_x"], np.float32)
    Bb, T, _ = x.shape
    assert T == g.seq and Bb == 2

    shared = {}
    shared["wq1t"] = _t2(inputs["L0_attn_Wq1"])
    shared["wk1t"] = _t2(inputs["L0_attn_Wk1"])
    shared["wk2t"] = _t2(inputs["L0_attn_Wk2"])
    shared["wprojt"] = _t2(inputs["L0_attn_Wproj"])
    shared["bq1c"] = _chunk_cols(inputs["L0_attn_bq1"], NCH)
    shared["bk1c"] = _chunk_cols(inputs["L0_attn_bk1"], NCH)
    shared["bk2c"] = _chunk_cols(inputs["L0_attn_bk2"], NCH)
    shared["bprojc"] = _chunk_cols(inputs["L0_attn_bproj"], NCH)

    ident = np.eye(128, dtype=np.float32)
    shared["identf"] = ident
    shared["identb"] = ident.astype(BF16NP)
    r = np.arange(128)
    tri_first_s = np.where(r[:, None] > r[None, :], NEG, 0.0).astype(np.float32)
    tri_last_s = np.where(r[:, None] < r[None, :], NEG, 0.0).astype(np.float32)
    shared["tri_first_s"] = tri_first_s.astype(BF16NP)
    shared["tri_last_s"] = tri_last_s.astype(BF16NP)
    shared["tri_first_st"] = np.ascontiguousarray(tri_first_s.T).astype(BF16NP)
    shared["tri_last_st"] = np.ascontiguousarray(tri_last_s.T).astype(BF16NP)
    shared["pexprep"] = np.zeros((128, 8), np.float32)

    per_core = []
    for core in range(N_CORES):
        b, j = core // 4, core % 4
        a = j * g.t_own
        lo, hi = a - g.pad_l, a + g.t_own + g.pad_r
        xs = np.zeros((g.t_buf, D), np.float32)
        s_lo, s_hi = max(lo, 0), min(hi, T)
        xs[s_lo - lo: s_hi - lo] = x[b, s_lo:s_hi]
        m = {"xfm": np.ascontiguousarray(xs.T)}

        pm = np.zeros(8, np.float32)
        pexp = np.zeros(8, np.float32)
        for i in range(4):
            if i < j:
                pm[4 * b + i] = 1.0
                pexp[4 * b + i] = float(j - 1 - i)
        m["pmask16"] = np.ascontiguousarray(
            np.broadcast_to(np.repeat(pm, 16)[None, :], (128, 128))).astype(np.float32)
        m["pmask8"] = np.ascontiguousarray(
            np.broadcast_to(np.repeat(pm, 8)[None, :], (128, 64))).astype(np.float32)
        m["pmaskrep"] = np.ascontiguousarray(
            np.broadcast_to(pm[None, :], (128, 8))).astype(np.float32)
        m["pexprep"] = np.ascontiguousarray(
            np.broadcast_to(pexp[None, :], (128, 8))).astype(np.float32)
        m["edge0"] = np.full((128, 1), 0.0 if j == 0 else 1.0, np.float32)
        m["edge3"] = np.full((128, 1), 0.0 if j == 3 else 1.0, np.float32)

        dead = np.zeros((128, g.n_tiles), np.float32)
        for tile in range(g.n_tiles):
            tok0 = lo + tile * 128
            for p in range(128):
                tok = tok0 + p
                if tok < 0 or tok >= T:
                    dead[p, tile] = NEG
        m["deadcol"] = dead
        per_core.append(m)

    HEADS, DH, WIN = 8, 64, 512
    INNER = HEADS * DH
    HID = 2730
    HIDP = 2816
    wqkv = np.asarray(inputs["L0_ffn_wqkv"], np.float32)
    wo_ = np.asarray(inputs["L0_ffn_wo"], np.float32)
    w1_ = np.asarray(inputs["L0_ffn_w1"], np.float32)
    b1_ = np.asarray(inputs["L0_ffn_b1"], np.float32)
    w2_ = np.asarray(inputs["L0_ffn_w2"], np.float32)
    b2_ = np.asarray(inputs["L0_ffn_b2"], np.float32)
    qs_ = np.asarray(inputs["L0_ffn_q_scale"], np.float32)
    ks_ = np.asarray(inputs["L0_ffn_k_scale"], np.float32)
    lg_ = np.asarray(inputs["L0_ffn_ln_g"], np.float32)
    lb_ = np.asarray(inputs["L0_ffn_ln_b"], np.float32)
    for li in range(2):
        shared[f"wqkvqk{li}"] = np.ascontiguousarray(wqkv[li][:, :2 * INNER])
        shared[f"wqkvv{li}"] = np.ascontiguousarray(wqkv[li][:, 2 * INNER:])
        shared[f"wo{li}"] = np.ascontiguousarray(wo_[li])
        w1a = np.zeros((D, HIDP), np.float32)
        w1g = np.zeros((D, HIDP), np.float32)
        w1a[:, :HID] = w1_[li][:, :HID]
        w1g[:, :HID] = w1_[li][:, HID:]
        w1pb = np.zeros((22, D, 256), np.float32)
        for i in range(22):
            w1pb[i, :, :128] = w1a[:, i * 128:(i + 1) * 128]
            w1pb[i, :, 128:] = w1g[:, i * 128:(i + 1) * 128]
        shared[f"w1pb{li}"] = np.ascontiguousarray(
            w1pb.reshape(22, 8, 128, 256).transpose(0, 2, 1, 3)).reshape(
                22 * 128, 2048)
        b1a = np.zeros((HIDP,), np.float32)
        b1g = np.zeros((HIDP,), np.float32)
        b1a[:HID] = b1_[li][:HID]
        b1g[:HID] = b1_[li][HID:]
        shared[f"b1a{li}"] = _chunk_cols(b1a, 22)
        shared[f"b1g{li}"] = _chunk_cols(b1g, 22)
        w2p = np.zeros((HIDP, D), np.float32)
        w2p[:HID] = w2_[li]
        w2pcb = np.zeros((4, HIDP, 256), np.float32)
        for pr in range(4):
            w2pcb[pr] = w2p[:, pr * 256:(pr + 1) * 256]
        shared[f"w2pcb{li}"] = np.ascontiguousarray(
            w2pcb.reshape(4, 22, 128, 256).transpose(0, 2, 1, 3)).reshape(
                512, 5632)
        shared[f"b2c{li}"] = _chunk_cols(b2_[li], NCH)
        qsind = np.zeros((16, 4 * 128), np.float32)
        ksind = np.zeros((16, 4 * 128), np.float32)
        hind = np.zeros((16, 4 * 128), np.float32)
        for c in range(4):
            for p in range(128):
                hq = 2 * c + p // 64
                qsind[hq, c * 128 + p] = 8.0 * qs_[li][p % 64]
                ksind[8 + hq, c * 128 + p] = ks_[li][p % 64]
                hind[hq, c * 128 + p] = 1.0
        shared[f"qsind{li}"] = qsind.astype(BF16NP)
        shared[f"ksind{li}"] = ksind.astype(BF16NP)
        if li == 0:
            shared["hind16"] = hind.astype(BF16NP)
            osel = np.zeros((16, 8 * 128), np.float32)
            for h in range(HEADS):
                osel[h, h * 128:(h + 1) * 128] = 1.0
            shared["osel"] = osel.astype(BF16NP)
            hsel = np.zeros((128, 8 * 16), np.float32)
            for oc in range(8):
                for p in range(128):
                    hh = 2 * (oc % 4) + p // 64
                    col = hh if oc < 4 else 8 + hh
                    hsel[p, oc * 16 + col] = 1.0
            shared["hsel"] = hsel.astype(BF16NP)
        lgb = np.zeros((2, NCH * 128), np.float32)
        lgb[0] = lg_[li]
        lgb[1] = lb_[li]
        shared[f"lgb{li}"] = lgb.astype(BF16NP)
        shared[f"lgc{li}"] = _chunk_cols(lg_[li], NCH)

    shared["wkt"] = _t2(inputs["L1_attn_Wk"])
    shared["wvt"] = _t2(inputs["L1_attn_Wv"])
    shared["wrt"] = _t2(inputs["L1_attn_Wr"])
    shared["wot"] = _t2(inputs["L1_attn_Wo"])
    shared["sdc"] = _chunk_cols(inputs["L1_attn_spatial_decay"], NCH)
    shared["sfc"] = _chunk_cols(inputs["L1_attn_spatial_first"], NCH)
    shared["mixkc"] = _chunk_cols(inputs["L1_attn_mix_k"], NCH)
    shared["mixvc"] = _chunk_cols(inputs["L1_attn_mix_v"], NCH)
    shared["mixrc"] = _chunk_cols(inputs["L1_attn_mix_r"], NCH)
    wfcT = _t2(inputs["L1_ffn_W_fc"])
    shared["wfctb"] = np.ascontiguousarray(
        wfcT.reshape(8, 128, 32, 128).transpose(2, 1, 0, 3)).reshape(
            32 * 128, 1024)
    shared["wprojT2"] = _t2(inputs["L1_ffn_W_proj"])
    shared["lnwc"] = _chunk_cols(inputs["L1_ffn_ln_w"], 32)
    shared["lnbc"] = _chunk_cols(inputs["L1_ffn_ln_b"], 32)

    half = np.arange(0, DH, 2, np.float32)
    inv_freq = 1.0 / 10000.0 ** (half / DH)
    base = (half + 0.4 * DH) / (1.4 * DH)
    seq = g.seq
    for core, m in enumerate(per_core):
        b, j = core // 4, core % 4
        lo = j * g.t_own - g.pad_l
        t_abs = np.clip(np.arange(lo, lo + g.t_buf, dtype=np.float32), 0, seq - 1)
        freqs = t_abs[None, :] * inv_freq[:, None]
        power = (t_abs - seq // 2) / float(WIN // 2)
        scale = base[:, None] ** power[None, :]
        cos = np.cos(freqs)
        sin = np.sin(freqs)
        def expand(tab32, signed=False):
            t64 = np.concatenate([tab32, tab32], axis=0)
            if signed:
                t64 = np.concatenate([-tab32, tab32], axis=0)
            return np.concatenate([t64, t64], axis=0)
        m["rotq_cos"] = np.ascontiguousarray(expand(cos * scale)).astype(BF16NP)
        m["rotq_sin"] = np.ascontiguousarray(
            expand(sin * scale, signed=True)).astype(BF16NP)
        m["rotk_cos"] = np.ascontiguousarray(expand(cos / scale)).astype(BF16NP)
        m["rotk_sin"] = np.ascontiguousarray(
            expand(sin / scale, signed=True)).astype(BF16NP)

    in_maps = [dict(shared, **pc) for pc in per_core]
    return in_maps


def unstage_output(results, geom):
    g = geom
    out = np.empty((2, g.seq, D), np.float32)
    for core, r in enumerate(results):
        b, j = core // 4, core % 4
        out[b, j * g.t_own:(j + 1) * g.t_own] = r["out_fm"].T
    return out



class Program:
    def __init__(self, geom, taps=()):
        self.g = geom
        self.taps = set(taps)
        self.nc = bass.Bass("TRN2", target_bir_lowering=False, debug=False)
        self.S = Sched()
        self.stack = ExitStack()
        self.sbuf_bytes = 0
        self._tap_outputs = []

    def din(self, name, shape, dtype=F32):
        return self.nc.dram_tensor(name, list(shape), dtype,
                                   kind="ExternalInput").ap()

    def dout(self, name, shape, dtype=F32):
        return self.nc.dram_tensor(name, list(shape), dtype,
                                   kind="ExternalOutput").ap()

    def dram(self, name, shape, dtype=F32):
        return self.nc.dram_tensor(name, list(shape), dtype).ap()

    def sbuf(self, name, shape, dtype=F32):
        t = self.stack.enter_context(self.nc.sbuf_tensor(name, list(shape), dtype))
        self.sbuf_bytes += shape[1] * mybir.dt.size(dtype)
        return t

    def mk_arenas(self, f32_elems, bf16_elems):
        self.af32 = Arena(self.sbuf("af32", [128, f32_elems]), "af32", f32_elems)
        self.ab16 = Arena(self.sbuf("ab16", [128, bf16_elems], BF16), "ab16",
                          bf16_elems)

    def abuf(self, name, nch, width, dtype=F32, lead=0, side="lo"):
        ar = self.af32 if dtype == F32 else self.ab16
        base = ar.alloc(nch * (width + lead), side=side)
        return Buf(ar, base, nch, width, lead=lead, name=name)

    def bfree(self, buf):
        buf.arena.free(buf.base, buf.nch * buf.stride)

    def psum(self, name, cols=512):
        return self.stack.enter_context(
            self.nc.psum_tensor(name, [128, cols], F32))

    def dma(self, q, out_ap, in_ap, reads, writes, tag="", slow_ok=False):
        self.S.op(q, lambda e, o=out_ap, i=in_ap, sl=slow_ok:
                      e.dma_start(out=o, in_=i, allow_slow_non_contiguous=sl),
                  reads=reads, writes=writes, kind="dma", tag=tag)

    def mm_group(self, psname, ops, tag=""):
        banks = [c0 // 512 for (_, c0, _, _, _, _) in ops]
        first = {}
        last = {}
        for i, b in enumerate(banks):
            first.setdefault(b, i)
            last[b] = i
        for i, (out_ap, c0, lhsT, rhs, lkeys, rkeys) in enumerate(ops):
            b = banks[i]
            self.mm(psname, out_ap, lhsT, rhs, first[b] == i, last[b] == i,
                    lkeys, rkeys, tag=tag)

    def mm_groupb(self, entries, tag=""):
        banks = [(e[0], e[2] // 512) for e in entries]
        first = {}
        last = {}
        for i, b in enumerate(banks):
            first.setdefault(b, i)
            last[b] = i
        for i, (psn, out_ap, c0, lhsT, rhs, lk, rk) in enumerate(entries):
            b = banks[i]
            self.mm(psn, out_ap, lhsT, rhs, first[b] == i, last[b] == i,
                    lk, rk, tag=tag)

    def mm(self, psname, out_ap, lhsT, rhs, start, stop, lkeys, rkeys, tag=""):
        self.S.op("tensor",
                  lambda e, o=out_ap, l=lhsT, r=rhs, st=start, sp=stop:
                      e.matmul(o, l, r, start=st, stop=sp),
                  reads=list(lkeys) + list(rkeys), writes=[("ps", psname)],
                  accum=not start, tag=tag)

    def cbias(self, value):
        if not hasattr(self, "_cpool"):
            self._cpool = self.sbuf("cpool", [128, 32])
            self._cpool_map = {}
        if value not in self._cpool_map:
            i = len(self._cpool_map)
            assert i < 32
            self.memset(self._cpool[:, i:i + 1], value, [("sc", "cpool")],
                        tag="cpool")
            self._cpool_map[value] = i
        i = self._cpool_map[value]
        return self._cpool[:, i:i + 1]

    def act(self, out_ap, in_ap, func, rkeys, wkeys, bias=0.0, scale=1.0,
            eng="scalar", tag=""):
        if isinstance(bias, float) and bias != 0.0:
            bias = self.cbias(bias)[0:out_ap.shape[0], :]
            rkeys = list(rkeys) + [("sc", "cpool")]
        elif not isinstance(bias, float) and bias.shape[0] != out_ap.shape[0]:
            bias = bias[0:out_ap.shape[0], :]
        self.S.op(eng,
                  lambda e, o=out_ap, i=in_ap, f=func, b=bias, s=scale:
                      e.activation(o, i, f, bias=b, scale=s),
                  reads=rkeys, writes=wkeys, tag=tag)

    def tt(self, out_ap, a_ap, b_ap, op, rkeys, wkeys, eng="vector", tag=""):
        self.S.op(eng,
                  lambda e, o=out_ap, a=a_ap, b=b_ap, p=op:
                      e.tensor_tensor(o, a, b, p),
                  reads=rkeys, writes=wkeys, tag=tag)

    def ts(self, out_ap, in_ap, s1, op0, rkeys, wkeys, s2=None, op1=ALU.bypass,
           eng="vector", tag=""):
        self.S.op(eng,
                  lambda e, o=out_ap, i=in_ap, a=s1, b=s2, p0=op0, p1=op1:
                      e.tensor_scalar(o, i, a, b, p0, p1),
                  reads=rkeys, writes=wkeys, tag=tag)

    def stt(self, out_ap, in0_ap, scalar, in1_ap, op0, op1, rkeys, wkeys,
            eng="vector", tag=""):
        self.S.op(eng,
                  lambda e, o=out_ap, i0=in0_ap, s=scalar, i1=in1_ap, p0=op0,
                         p1=op1:
                      e.scalar_tensor_tensor(o, i0, s, i1, p0, p1),
                  reads=rkeys, writes=wkeys, tag=tag)

    def copy(self, out_ap, in_ap, rkeys, wkeys, eng="vector", tag=""):
        if eng == "scalar":
            self.S.op(eng, lambda e, o=out_ap, i=in_ap: e.copy(o, i),
                      reads=rkeys, writes=wkeys, tag=tag)
        else:
            self.S.op(eng, lambda e, o=out_ap, i=in_ap: e.tensor_copy(o, i),
                      reads=rkeys, writes=wkeys, tag=tag)

    def red(self, out_ap, in_ap, op, rkeys, wkeys, negate=False, eng="vector",
            tag=""):
        self.S.op(eng,
                  lambda e, o=out_ap, i=in_ap, p=op, n=negate:
                      e.tensor_reduce(o, i, AXL.X, p, negate=n),
                  reads=rkeys, writes=wkeys, tag=tag)

    def scan(self, out_ap, d0_ap, d1_ap, init, rkeys, wkeys, op0=ALU.mult,
             op1=ALU.add, tag=""):
        self.S.op("vector",
                  lambda e, o=out_ap, a=d0_ap, b=d1_ap, i=init, p0=op0, p1=op1:
                      e.tensor_tensor_scan(o, a, b, i, p0, p1),
                  reads=rkeys, writes=wkeys, tag=tag)

    def memset(self, ap, const, wkeys, eng="vector", tag=""):
        self.S.op(eng, lambda e, a=ap, c=const: e.memset(a, c),
                  reads=(), writes=wkeys, tag=tag)

    def recip(self, out_ap, in_ap, rkeys, wkeys, tag=""):
        self.S.op("vector", lambda e, o=out_ap, i=in_ap: e.reciprocal(o, i),
                  reads=rkeys, writes=wkeys, tag=tag)

    def transpose(self, out_psname, out_ap, in_ap, rkeys, tag=""):
        self.S.op("tensor",
                  lambda e, o=out_ap, i=in_ap, d=self.sb_identf[:]:
                      e.transpose(o, i, d),
                  reads=list(rkeys) + [("sc", "identf")],
                  writes=[("ps", out_psname)], tag=tag)

    def tap(self, name, ap, shape, rkeys, dtype=F32):
        if name not in self.taps:
            return
        out = self.dout("tap_" + name, list(shape), dtype)
        self.dma("gpsimd", out[:, :], ap, rkeys, [dkey("tap_" + name)],
                 tag="tap")

    def declare(self):
        g = self.g
        if g.t_own <= 512:
            self.mk_arenas(10240, 65536)
        else:
            self.mk_arenas(8448, 72192)
        self.xfm = self.din("xfm", [D, g.t_buf])
        self.wq1t = self.din("wq1t", [D, D])
        self.wk1t = self.din("wk1t", [D, D])
        self.wk2t = self.din("wk2t", [D, D])
        self.wprojt = self.din("wprojt", [D, D])
        self.sb_bias = {}
        for n in ("bq1c", "bk1c", "bk2c", "bprojc"):
            ap = self.din(n, [128, NCH])
            t = self.sbuf("sb_" + n, [128, NCH])
            self.sb_bias[n] = t
            self.dma("sync", t[:], ap[:, :], [dkey(n)], [("sc", n)])
        for n, sh in (("identf", [128, 128]), ("pmask16", [128, 128]),
                      ("pmask8", [128, 64]), ("pmaskrep", [128, 8]),
                      ("pexprep", [128, 8]), ("edge0", [128, 1]),
                      ("edge3", [128, 1]), ("deadcol", [128, g.n_tiles])):
            ap = self.din(n, sh)
            t = self.sbuf("sb_" + n, sh)
            setattr(self, "sb_" + n, t)
            self.dma("sync", t[:], ap[:, :], [dkey(n)], [("sc", n)])

        for li in range(2):
            for n, sh in ((f"wqkvqk{li}", [D, 1024]), (f"wqkvv{li}", [D, 512]),
                          (f"wo{li}", [512, D]), (f"w1pb{li}", [22 * 128, 2048]),
                          (f"w2pcb{li}", [4 * 128, 22 * 256]),
                          (f"lgb{li}", [2, NCH * 128]),
                          (f"lgc{li}", [128, NCH])):
                setattr(self, n, self.din(n, sh,
                                          BF16 if n.startswith("lgb") else F32))
            for n in (f"b1a{li}", f"b1g{li}"):
                ap = self.din(n, [128, 22])
                t = self.sbuf("sb_" + n, [128, 22])
                self.sb_bias[n] = t
                self.dma("sync", t[:], ap[:, :], [dkey(n)], [("sc", n)])
            ap = self.din(f"b2c{li}", [128, NCH])
            t = self.sbuf(f"sb_b2c{li}", [128, NCH])
            self.sb_bias[f"b2c{li}"] = t
            self.dma("sync", t[:], ap[:, :], [dkey(f"b2c{li}")],
                     [("sc", f"b2c{li}")])
            for n in (f"qsind{li}", f"ksind{li}"):
                ap = self.din(n, [16, 512], BF16)
                t = self.sbuf("sb_" + n, [16, 512], BF16)
                setattr(self, "sb_" + n, t)
                self.dma("sync", t[:], ap[:, :], [dkey(n)], [("sc", n)])
        for n, sh in (("hind16", [16, 512]), ("osel", [16, 1024]),
                      ("hsel", [128, 128]), ("tri_first_s", [128, 128]),
                      ("tri_last_s", [128, 128]), ("tri_first_st", [128, 128]),
                      ("tri_last_st", [128, 128]), ("identb", [128, 128])):
            ap = self.din(n, sh, BF16)
            t = self.sbuf("sb_" + n, sh, BF16)
            setattr(self, "sb_" + n, t)
            self.dma("sync", t[:], ap[:, :], [dkey(n)], [("sc", n)])
        for n in ("rotq_cos", "rotq_sin", "rotk_cos", "rotk_sin"):
            setattr(self, n, self.din(n, [128, g.t_buf], BF16))

        self.onesrow_b = self.sbuf("onesrow_b", [1, 128], BF16)
        self.onescol_b = self.sbuf("onescol_b", [128, 1], BF16)
        self.ones128_b = self.sbuf("ones128_b", [128, 128], BF16)
        self.memset(self.onesrow_b[:], 1.0, [("sc", "onesrow_b")])
        self.memset(self.onescol_b[:], 1.0, [("sc", "onescol_b")])
        self.memset(self.ones128_b[:], 1.0, [("sc", "ones128_b")])

        for n in ("wkt", "wvt", "wrt", "wot"):
            setattr(self, n, self.din(n, [D, D]))
        self.wfctb = self.din("wfctb", [32 * 128, 1024])
        self.wprojT2 = self.din("wprojT2", [4096, D])
        for n in ("sdc", "sfc", "mixkc", "mixvc", "mixrc"):
            ap = self.din(n, [128, NCH])
            t = self.sbuf("sb_" + n, [128, NCH])
            setattr(self, "sb_" + n, t)
            self.dma("sync", t[:], ap[:, :], [dkey(n)], [("sc", n)])
        for n in ("lnwc", "lnbc"):
            ap = self.din(n, [128, 32])
            t = self.sbuf("sb_" + n, [128, 32])
            setattr(self, "sb_" + n, t)
            self.dma("sync", t[:], ap[:, :], [dkey(n)], [("sc", n)])

        self.out_fm = self.dout("out_fm", [D, g.t_own])

        self.psA = self.psum("psA")
        self.psB = self.psum("psB")
        self.psC = self.psum("psC")
        self.psD = self.psum("psD")
        self.psBC = self.psum("psBC")
        self.psBC2 = self.psum("psBC2")
        self.psSC = self.psum("psSC", 1024)

        self.ones_col = self.sbuf("ones_col", [128, 1])
        self.ones128 = self.sbuf("ones128", [128, 128])
        self.sc_small = self.sbuf("sc_small", [128, 256])
        self.memset(self.ones_col[:], 1.0, [("sc", "ones_col")])
        self.memset(self.ones128[:], 1.0, [("sc", "ones128")])

    def wfull(self, name, dram_ap, rows=D):
        cols = dram_ap.shape[1]
        nk = rows // 128
        w = self.abuf("w_" + name, nk, cols, BF16)
        self.dma("gpsimd",
                 w.full_ap().rearrange("p (a c) -> p a c", a=nk),
                 dram_ap.rearrange("(a p) c -> p a c", p=128),
                 [dkey(name)], w.kall(), tag="wload_" + name)
        return w

    def mm_fm(self, w, rhs_buf, rhs_lo, rhs_hi, evac, rhs_shift=0,
              psnames=("psA", "psB"), pss=None, nk=None, out_chunks=None,
              tag=""):
        if pss is None:
            pss = (self.psA, self.psB)
        if nk is None:
            nk = w.nch
        if out_chunks is None:
            out_chunks = range(w.width // 128)
        npan = (rhs_hi - rhs_lo + 511) // 512
        i = 0
        for oc in out_chunks:
            for p in range(npan):
                lo = rhs_lo + 512 * p
                hi = min(lo + 512, rhs_hi)
                psn, ps = psnames[i % 2], pss[i % 2]
                i += 1
                for kc in range(nk):
                    self.mm(psn, ps[:, 0:hi - lo],
                            w.ap(kc, oc * 128, oc * 128 + 128),
                            rhs_buf.ap(kc, lo + rhs_shift, hi + rhs_shift),
                            kc == 0, kc == nk - 1,
                            w.keys(kc, oc * 128, oc * 128 + 128),
                            rhs_buf.keys(kc, lo + rhs_shift, hi + rhs_shift),
                            tag=tag + "mm")
                evac(oc, lo, hi, psn, ps)

    def phase_xn(self):
        g = self.g
        self.xn_spill = self.dram("xn_spill", [D, g.t_buf + 1], BF16)
        zc = self.sbuf("zcol", [128, 1], BF16)
        self.memset(zc[:], 0.0, [("sc", "zcol")], tag="zc")
        for c in range(NCH):
            self.dma("scalar", self.xn_spill[c * 128:(c + 1) * 128, 0:1],
                     zc[:], [("sc", "zcol")], [dkey("xn_spill", c, -1)],
                     tag="xnz", slow_ok=True)

        xpan = self.abuf("xpan", NCH, 256)
        xsq = self.abuf("xsq", NCH, 256)
        xnp = self.abuf("xnp", NCH, 256, BF16)
        rstd = self.sbuf("xn_rstd", [128, 512])

        n_pan = (g.t_buf + 255) // 256
        for p in range(n_pan):
            lo = p * 256
            hi = min(lo + 256, g.t_buf)
            w = hi - lo
            for c in range(NCH):
                self.dma("sync", xpan.ap(c, 0, w),
                         self.xfm[c * 128:(c + 1) * 128, lo:hi],
                         [dkey("xfm")], xpan.keys(c, 0, w), tag="xload")
                self.act(xsq.ap(c, 0, w), xpan.ap(c, 0, w), AF.Square,
                         xpan.keys(c, 0, w), xsq.keys(c, 0, w), tag="xsq")
            for c in range(NCH):
                self.mm("psBC", self.psBC[:, 0:w], self.ones128[:],
                        xpan.ap(c, 0, w), c == 0, c == NCH - 1,
                        [("sc", "ones128")], xpan.keys(c, 0, w), tag="xs1")
                self.mm("psBC2", self.psBC2[:, 0:w], self.ones128[:],
                        xsq.ap(c, 0, w), c == 0, c == NCH - 1,
                        [("sc", "ones128")], xsq.keys(c, 0, w), tag="xs2")
            self.act(rstd[:, 0:w], self.psBC[:, 0:w], AF.Square,
                     [("ps", "psBC")], [("sc", "xn_rstd")], tag="s2")
            self.stt(rstd[:, 0:w], rstd[:, 0:w], -1.0 / D, self.psBC2[:, 0:w],
                     ALU.mult, ALU.add, [("sc", "xn_rstd"), ("ps", "psBC2")],
                     [("sc", "xn_rstd")], tag="var")
            self.act(rstd[:, 0:w], rstd[:, 0:w], AF.Ln, [("sc", "xn_rstd")],
                     [("sc", "xn_rstd")], bias=1e-5, scale=1.0 / D, tag="ln")
            self.act(rstd[:, 0:w], rstd[:, 0:w], AF.Exp, [("sc", "xn_rstd")],
                     [("sc", "xn_rstd")], scale=-0.5, tag="exp")
            for c in range(NCH):
                self.stt(xsq.ap(c, 0, w), self.psBC[:, 0:w], -1.0 / D,
                         xpan.ap(c, 0, w), ALU.mult, ALU.add,
                         [("ps", "psBC")] + xpan.keys(c, 0, w),
                         xsq.keys(c, 0, w), tag="xcen")
                self.tt(xnp.ap(c, 0, w), xsq.ap(c, 0, w), rstd[:, 0:w],
                        ALU.mult, xsq.keys(c, 0, w) + [("sc", "xn_rstd")],
                        xnp.keys(c, 0, w), tag="xnorm")
                self.dma("scalar", self.xn_spill[c * 128:(c + 1) * 128,
                                                 1 + lo:1 + hi],
                         xnp.ap(c, 0, w), xnp.keys(c, 0, w),
                         [dkey("xn_spill", c, lo // 512)], tag="xnsp")
        self.bfree(xpan)
        self.bfree(xsq)
        self.bfree(xnp)

    def load_xn(self, buf, c, lo, hi, tag="xnld"):
        self.dma("sync", buf.ap(c, 0, hi - lo),
                 self.xn_spill[c * 128:(c + 1) * 128, 1 + lo:1 + hi],
                 [dkey("xn_spill", c, max(lo, 0) // 512),
                  dkey("xn_spill", c, max(hi - 1, 0) // 512),
                  dkey("xn_spill", c, -1)],
                 buf.keys(c, 0, hi - lo), tag=tag)

    def load_spill(self, buf, c, lo, hi, src_ap, name, off, tag="spld"):
        c0, c1 = off + lo, off + hi
        self.dma("sync", buf.ap(c, 0, hi - lo),
                 src_ap[c * 128:(c + 1) * 128, c0:c1],
                 [dkey(name, c, c0 // 512), dkey(name, c, (c1 - 1) // 512),
                  dkey(name, c, -1)],
                 buf.keys(c, 0, hi - lo), tag=tag)

    def allgather(self, idx, in_sbuf, ncols, rkeys):
        cin = self.dram(f"cc{idx}_in", [128, ncols])
        cout = self.dram(f"cc{idx}_out", [8 * 128, ncols])
        self.dma("gpsimd", cin[:, :], in_sbuf[:, 0:ncols], rkeys,
                 [dkey(f"cc{idx}_in")], tag=f"ag{idx}w")
        self.S.op("gpsimd",
                  lambda e, ci=cin, co=cout: e.collective_compute(
                      "AllGather", ALU.bypass,
                      replica_groups=[list(range(N_CORES))],
                      ins=[ci.opt()], outs=[co.opt()]),
                  reads=[dkey(f"cc{idx}_in")], writes=[dkey(f"cc{idx}_out")],
                  kind="cc", tag=f"ag{idx}")
        agg = self.sbuf(f"agg{idx}", [128, ncols * 8])
        for r in range(8):
            self.dma("gpsimd", agg[:, r * ncols:(r + 1) * ncols],
                     cout[r * 128:(r + 1) * 128, :],
                     [dkey(f"cc{idx}_out")], [("sc", f"agg{idx}")],
                     tag=f"ag{idx}r")
        return agg

    def carry_cols(self, agg, ncols, wname, weights=None):
        carry = self.sbuf(wname + "_c", [128, ncols])
        masked = self.sbuf(wname + "_m", [128, ncols * 8])
        kagg = ("sc", agg.name)
        km = ("sc", wname + "_m")
        if weights is None:
            pmsk = self.sb_pmask16 if ncols == 16 else self.sb_pmask8
            pk = "pmask16" if ncols == 16 else "pmask8"
            assert ncols in (8, 16)
            self.tt(masked[:], agg[:], pmsk[:, 0:ncols * 8],
                    ALU.mult, [kagg, ("sc", pk)], [km], tag=wname)
            self.red(carry[:].rearrange("p (one c) -> p c one", one=1),
                     masked[:].rearrange("p (r c) -> p c r", r=8), ALU.add,
                     [km], [("sc", wname + "_c")], tag=wname)
        else:
            w_ap, wk = weights
            cm = agg[:].rearrange("p (r c) -> p c r", r=8)
            w3 = w_ap.rearrange("p (c r) -> p c r", r=8)
            self.tt(masked[:, 0:64].rearrange("p (c r) -> p c r", r=8),
                    cm[:, 0:8, :], w3, ALU.mult, [kagg, wk], [km], tag=wname)
            self.tt(masked[:, 64:128].rearrange("p (c r) -> p c r", r=8),
                    cm[:, 8:16, :], w3, ALU.mult, [kagg, wk], [km], tag=wname)
            self.red(carry[:].rearrange("p (one c) -> p c one", one=1),
                     masked[:].rearrange("p (c r) -> p c r", r=8), ALU.add,
                     [km], [("sc", wname + "_c")], tag=wname)
        return carry

    def phase_l0_attn(self):
        g = self.g
        SC_LO, SC_HI = g.own_lo, g.t_buf
        NSC = SC_HI - SC_LO
        P2N = g.t_own + 2
        P3N = g.t_own + 1
        NOS = g.t_own + 2

        W2 = g.t_own + 2
        self.k_spill = self.dram("k_spill", [D, W2], BF16)
        K2f = self.abuf("l0_K2f", NCH, W2, BF16)
        Q2f = self.abuf("l0_Q2f", NCH, W2 + 2, BF16)
        self.s_spill = self.dram("s_spill", [D, NOS], BF16)

        PW = 256
        xnps = [self.abuf(f"l0_xnp{k}", NCH, PW + 1, BF16) for k in range(2)]
        xnp = xnps[0]
        fA = self.abuf("l0_fA", NCH, PW + 1)
        fB = self.abuf("l0_fB", NCH, PW + 1)
        fC = self.abuf("l0_fC", NCH, PW + 1)
        Qp = self.abuf("l0_Qp", NCH, PW, BF16)
        Kpans = [self.abuf(f"l0_Kp{k}", NCH, PW, BF16) for k in range(2)]
        Kpan = Kpans[0]
        OARp = self.abuf("l0_OARp", NCH, PW, BF16, lead=1)
        OMAp = self.abuf("l0_OMAp", NCH, PW, BF16)
        sp_pan = self.abuf("l0_sp", NCH, PW, BF16)

        chains = self.sbuf("l0_chains", [128, 64])
        kch = lambda c: chains[:, c:c + 1]
        kvch = lambda c: chains[:, 8 + c:9 + c]
        s2ch = lambda c: chains[:, 16 + c:17 + c]
        oarl = lambda c: chains[:, 24 + c:25 + c]
        runsum = chains[:, 32:48]
        ag2in = chains[:, 48:56]
        KCH = ("sc", "l0_chains")
        qb = self.sbuf("l0_qb", [128, 16])
        KQB = ("sc", "l0_qb")
        self.memset(runsum, 0.0, [KCH], tag="rs0")

        wk1 = self.wfull("wk1t", self.wk1t)
        wk2 = self.wfull("wk2t", self.wk2t)
        self.k2bias = self.sbuf("k2bias", [128, NCH])
        self.ts(self.k2bias[:], self.sb_bias["bk2c"][:], 0.000625, ALU.mult,
                [("sc", "bk2c")], [("sc", "k2bias")], tag="k2b")
        n_own_pan = (g.t_own + PW - 1) // PW
        n_pan1 = (W2 + PW - 1) // PW
        for p in range(n_pan1):
            lo = SC_LO + p * PW
            hi = min(lo + PW, SC_LO + W2)
            w = hi - lo
            xnp = xnps[p % 2]
            Kpan = Kpans[p % 2]
            for c in range(NCH):
                self.load_xn(xnp, c, lo - 1, hi)
            for oc in range(NCH):
                psn, ps = ("psA", self.psA) if oc % 2 == 0 else ("psB", self.psB)
                for kc in range(NCH):
                    self.mm(psn, ps[:, 0:w], wk1.ap(kc, oc * 128, oc * 128 + 128),
                            xnp.ap(kc, 1, 1 + w), kc == 0, kc == NCH - 1,
                            wk1.keys(kc, oc * 128, oc * 128 + 128),
                            xnp.keys(kc, 1, 1 + w), tag="k1mm")
                self.act(Kpan.ap(oc, 0, w), ps[:, 0:w], AF.Exp,
                         [("ps", psn), ("sc", "bk1c")],
                         Kpan.keys(oc, 0, w),
                         bias=self.sb_bias["bk1c"][:, oc:oc + 1], tag="evk")
                i0_ = lo - SC_LO
                self.dma("scalar", self.k_spill[oc * 128:(oc + 1) * 128,
                                                i0_:i0_ + w],
                         Kpan.ap(oc, 0, w), Kpan.keys(oc, 0, w),
                         [dkey("k_spill", oc, i0_ // 512)], tag="ksp",
                         slow_ok=True)
            for oc in range(NCH):
                psn, ps = ("psC", self.psC) if oc % 2 == 0 else ("psD", self.psD)
                for kc in range(NCH):
                    self.mm(psn, ps[:, 0:w], wk2.ap(kc, oc * 128, oc * 128 + 128),
                            xnp.ap(kc, 0, w), kc == 0, kc == NCH - 1,
                            wk2.keys(kc, oc * 128, oc * 128 + 128),
                            xnp.keys(kc, 0, w), tag="k2mm")
                self.act(K2f.ap(oc, lo - SC_LO, hi - SC_LO), ps[:, 0:w],
                         AF.Sigmoid, [("ps", psn), ("sc", "k2bias")],
                         K2f.keys(oc, lo - SC_LO, hi - SC_LO),
                         bias=self.k2bias[:, oc:oc + 1], scale=0.000625,
                         tag="evk2")
            if lo < g.own_hi:
                ws = min(hi, g.own_hi) - lo
                for c in range(NCH):
                    self.tt(fA.ap(c, 0, ws), Kpan.ap(c, 0, ws),
                            xnp.ap(c, 1, 1 + ws), ALU.mult,
                            Kpan.keys(c, 0, ws) +
                            xnp.keys(c, 1, 1 + ws), fA.keys(c, 0, ws), tag="kv1")
                    self.red(fB.ap(c, 0, 1), fA.ap(c, 0, ws), ALU.add,
                             fA.keys(c, 0, ws), fB.keys(c, 0, 1), tag="rkv")
                    self.red(fC.ap(c, 0, 1), Kpan.ap(c, 0, ws),
                             ALU.add, Kpan.keys(c, 0, ws),
                             fC.keys(c, 0, 1), tag="rk")
                    self.tt(runsum[:, c:c + 1], runsum[:, c:c + 1],
                            fC.ap(c, 0, 1), ALU.add, [KCH] + fC.keys(c, 0, 1),
                            [KCH], tag="accK")
                    self.tt(runsum[:, 8 + c:9 + c], runsum[:, 8 + c:9 + c],
                            fB.ap(c, 0, 1), ALU.add, [KCH] + fB.keys(c, 0, 1),
                            [KCH], tag="accKV")
        self.bfree(wk2)
        self.bfree(wk1)

        agg1 = self.allgather(1, runsum, 16, [KCH])
        car1 = self.carry_cols(agg1, 16, "car1")
        k_car1 = ("sc", "car1_c")

        wq1 = self.wfull("wq1t", self.wq1t)
        xnp = xnps[0]
        for c in range(NCH):
            self.load_xn(xnp, c, g.own_lo - 2, g.own_lo)
        psn, ps = "psA", self.psA
        for oc in range(NCH):
            for kc in range(NCH):
                self.mm(psn, ps[:, 0:2], wq1.ap(kc, oc * 128, oc * 128 + 128),
                        xnp.ap(kc, 0, 2), kc == 0, kc == NCH - 1,
                        wq1.keys(kc, oc * 128, oc * 128 + 128),
                        xnp.keys(kc, 0, 2), tag="qbmm")
            self.act(qb[:, 2 * oc:2 * oc + 2], ps[:, 0:2], AF.Sigmoid,
                     [("ps", psn), ("sc", "bq1c")], [KQB],
                     bias=self.sb_bias["bq1c"][:, oc:oc + 1], tag="qbev")
        sc = self.sc_small
        ksc = ("sc", "sc_small")
        for c in range(NCH):
            self.copy(kch(c), car1[:, c:c + 1], [k_car1], [KCH], tag="ci1")
            self.copy(kvch(c), car1[:, 8 + c:9 + c], [k_car1], [KCH], tag="ci2")
            self.memset(s2ch(c), 0.0, [KCH], tag="ci3")
            self.ts(sc[:, c:c + 1], car1[:, c:c + 1], 1e-6, ALU.add,
                    [k_car1], [ksc], tag="rcA")
            self.recip(sc[:, 8 + c:9 + c], sc[:, c:c + 1], [ksc], [ksc],
                       tag="rcB")
            self.tt(sc[:, 16 + c:17 + c], car1[:, 8 + c:9 + c],
                    sc[:, 8 + c:9 + c], ALU.mult, [k_car1, ksc], [ksc],
                    tag="rcC")
            self.tt(oarl(c), sc[:, 16 + c:17 + c], qb[:, 2 * c + 1:2 * c + 2],
                    ALU.mult, [ksc, KQB], [KCH], tag="rcD")
            self.ts(fA.ap(c, 0, 2), qb[:, 2 * c:2 * c + 2], 0.03125, ALU.mult,
                    [KQB], fA.keys(c, 0, 2), tag="q2ba")
            self.stt(fA.ap(c, 0, 2), qb[:, 2 * c:2 * c + 2], 0.000625,
                     fA.ap(c, 0, 2), ALU.mult, ALU.min, [KQB] + fA.keys(c, 0, 2),
                     fA.keys(c, 0, 2), tag="q2bb")
            self.ts(Q2f.ap(c, 0, 2), fA.ap(c, 0, 2), self.sb_edge0[:, 0:1],
                    ALU.mult, fA.keys(c, 0, 2) + [("sc", "edge0")],
                    Q2f.keys(c, 0, 2), tag="q2bz")

        wproj = self.wfull("wprojt", self.wprojt)
        onesb = self.ones_col
        n_pan2 = (P2N + PW - 1) // PW
        for p in range(n_pan2):
            lo = SC_LO + p * PW
            hi = min(lo + PW, SC_LO + P2N)
            w = hi - lo
            i0 = lo - SC_LO
            xnp = xnps[p % 2]
            Kpan = Kpans[p % 2]
            for c in range(NCH):
                self.load_xn(xnp, c, lo - 1, hi)
            for oc in range(NCH):
                psn, ps = ("psA", self.psA) if oc % 2 == 0 else ("psB", self.psB)
                for kc in range(NCH):
                    self.mm(psn, ps[:, 0:w], wq1.ap(kc, oc * 128, oc * 128 + 128),
                            xnp.ap(kc, 1, 1 + w), kc == 0, kc == NCH - 1,
                            wq1.keys(kc, oc * 128, oc * 128 + 128),
                            xnp.keys(kc, 1, 1 + w), tag="qmm")
                self.act(Qp.ap(oc, 0, w), ps[:, 0:w], AF.Sigmoid,
                         [("ps", psn), ("sc", "bq1c")], Qp.keys(oc, 0, w),
                         bias=self.sb_bias["bq1c"][:, oc:oc + 1], tag="evq")
            for c in range(NCH):
                self.dma("sync", Kpan.ap(c, 0, w),
                         self.k_spill[c * 128:(c + 1) * 128, i0:i0 + w],
                         [dkey("k_spill", c, i0 // 512),
                          dkey("k_spill", c, (i0 + w) // 512)],
                         Kpan.keys(c, 0, w), tag="kld", slow_ok=True)
                kKf = Kpan.keys(c, 0, w)
                self.scan(fA.ap(c, 0, w), onesb[:].broadcast_to([128, w]),
                          Kpan.ap(c, 0, w), kch(c),
                          kKf + [KCH, ("sc", "ones_col")], fA.keys(c, 0, w),
                          tag="scK")
                self.copy(kch(c), fA.ap(c, w - 1, w), fA.keys(c, w - 1, w),
                          [KCH], tag="upK")
                self.act(fA.ap(c, 0, w), fA.ap(c, 0, w), AF.Ln,
                         fA.keys(c, 0, w), fA.keys(c, 0, w), bias=1e-6,
                         tag="lnK")
                self.act(fA.ap(c, 0, w), fA.ap(c, 0, w), AF.Exp,
                         fA.keys(c, 0, w), fA.keys(c, 0, w), scale=-1.0,
                         tag="rcK")
                self.tt(fB.ap(c, 0, w), Kpan.ap(c, 0, w),
                        xnp.ap(c, 1, 1 + w), ALU.mult,
                        kKf + xnp.keys(c, 1, 1 + w), fB.keys(c, 0, w),
                        tag="kv")
                self.scan(fB.ap(c, 0, w), onesb[:].broadcast_to([128, w]),
                          fB.ap(c, 0, w), kvch(c),
                          fB.keys(c, 0, w) + [KCH, ("sc", "ones_col")],
                          fB.keys(c, 0, w), tag="scKV")
                self.copy(kvch(c), fB.ap(c, w - 1, w), fB.keys(c, w - 1, w),
                          [KCH], tag="upKV")
                self.tt(fB.ap(c, 0, w), fB.ap(c, 0, w), fA.ap(c, 0, w),
                        ALU.mult, fB.keys(c, 0, w) + fA.keys(c, 0, w),
                        fB.keys(c, 0, w), tag="S")
                self.copy(OARp.ap(c, -1, 0), oarl(c), [KCH],
                          OARp.keys(c, -1, 0), tag="oarl")
                self.tt(OARp.ap(c, 0, w), fB.ap(c, 0, w), Qp.ap(c, 0, w),
                        ALU.mult, fB.keys(c, 0, w) + Qp.keys(c, 0, w),
                        OARp.keys(c, 0, w), tag="oar")
                self.copy(oarl(c), OARp.ap(c, w - 1, w),
                          OARp.keys(c, w - 1, w), [KCH], tag="oarU")
                self.tt(fA.ap(c, 0, w), xnp.ap(c, 1, 1 + w),
                        OARp.ap(c, -1, w - 1), ALU.subtract,
                        xnp.keys(c, 1, 1 + w) + OARp.keys(c, -1, w - 1),
                        fA.keys(c, 0, w), tag="E")
                if p == 0:
                    self.ts(fA.ap(c, 0, 1), fA.ap(c, 0, 1),
                            self.sb_edge0[:, 0:1], ALU.mult,
                            fA.keys(c, 0, 1) + [("sc", "edge0")],
                            fA.keys(c, 0, 1), tag="E0")
                self.tt(fA.ap(c, 0, w), K2f.ap(c, i0, i0 + w), fA.ap(c, 0, w),
                        ALU.mult, K2f.keys(c, i0, i0 + w) + fA.keys(c, 0, w),
                        fA.keys(c, 0, w), tag="k2e")
                self.scan(fC.ap(c, 0, w), onesb[:].broadcast_to([128, w]),
                          fA.ap(c, 0, w), s2ch(c),
                          fA.keys(c, 0, w) + [KCH, ("sc", "ones_col")],
                          fC.keys(c, 0, w), tag="scS2")
                self.copy(s2ch(c), fC.ap(c, w - 1, w), fC.keys(c, w - 1, w),
                          [KCH], tag="upS2")
                self.copy(K2f.ap(c, i0, i0 + w), fC.ap(c, 0, w),
                          fC.keys(c, 0, w), K2f.keys(c, i0, i0 + w), tag="s2w")
                q2lo = 2 + i0
                q2hi = min(2 + i0 + w, W2 + 2)
                if q2lo < q2hi:
                    ww = q2hi - q2lo
                    self.ts(Q2f.ap(c, q2lo, q2hi), Qp.ap(c, 0, ww), 0.03125,
                            ALU.mult, Qp.keys(c, 0, ww), Q2f.keys(c, q2lo, q2hi),
                            tag="q2a")
                    self.stt(Q2f.ap(c, q2lo, q2hi), Qp.ap(c, 0, ww), 0.000625,
                             Q2f.ap(c, q2lo, q2hi), ALU.mult, ALU.min,
                             Qp.keys(c, 0, ww) + Q2f.keys(c, q2lo, q2hi),
                             Q2f.keys(c, q2lo, q2hi), tag="q2b")
            if hi == g.own_hi:
                for c in range(NCH):
                    self.copy(ag2in[:, c:c + 1], s2ch(c), [KCH], [KCH],
                              tag="ag2p")
            for oc in range(NCH):
                psn, ps = ("psC", self.psC) if oc % 2 == 0 else ("psD", self.psD)
                for kc in range(NCH):
                    self.mm(psn, ps[:, 0:w], wproj.ap(kc, oc * 128, oc * 128 + 128),
                            OARp.ap(kc, -1, w - 1), kc == 0, kc == NCH - 1,
                            wproj.keys(kc, oc * 128, oc * 128 + 128),
                            OARp.keys(kc, -1, w - 1), tag="pjmm")
                self.act(sp_pan.ap(oc, 0, w), ps[:, 0:w], AF.Identity,
                         [("ps", psn), ("sc", "bprojc")],
                         sp_pan.keys(oc, 0, w),
                         bias=self.sb_bias["bprojc"][:, oc:oc + 1], tag="evs")
                self.dma("scalar", self.s_spill[oc * 128:(oc + 1) * 128,
                                                i0:i0 + w],
                         sp_pan.ap(oc, 0, w), sp_pan.keys(oc, 0, w),
                         [dkey("s_spill", oc, i0 // 512)], tag="ssp", slow_ok=True)
        self.bfree(wq1)
        self.bfree(Qp)

        agg2 = self.allgather(2, ag2in, 8, [KCH])
        car2 = self.carry_cols(agg2, 8, "car2")
        k_car2 = ("sc", "car2_c")

        n_pan3 = (P3N + PW - 1) // PW
        for p in range(n_pan3):
            lo = SC_LO + p * PW
            hi = min(lo + PW, SC_LO + P3N)
            w = hi - lo
            i0 = lo - SC_LO
            for c in range(NCH):
                self.stt(OMAp.ap(c, 0, w), K2f.ap(c, i0, i0 + w),
                         car2[:, c:c + 1], Q2f.ap(c, i0 + 1, i0 + 1 + w),
                         ALU.add, ALU.mult,
                         K2f.keys(c, i0, i0 + w) + [k_car2] +
                         Q2f.keys(c, i0 + 1, i0 + 1 + w),
                         OMAp.keys(c, 0, w), tag="oma")
            for oc in range(NCH):
                psn, ps = ("psA", self.psA) if oc % 2 == 0 else ("psB", self.psB)
                for kc in range(NCH):
                    self.mm(psn, ps[:, 0:w],
                            wproj.ap(kc, oc * 128, oc * 128 + 128),
                            OMAp.ap(kc, 0, w),
                            kc == 0, kc == NCH - 1,
                            wproj.keys(kc, oc * 128, oc * 128 + 128),
                            OMAp.keys(kc, 0, w), tag="pj2mm")
                self.dma("scalar", sp_pan.ap(oc, 0, w),
                         self.s_spill[oc * 128:(oc + 1) * 128,
                                      i0 + 1:i0 + 1 + w],
                         [dkey("s_spill", oc, i0 // 512),
                          dkey("s_spill", oc, (i0 + w) // 512)],
                         sp_pan.keys(oc, 0, w), tag="sld", slow_ok=True)
                self.tt(sp_pan.ap(oc, 0, w), sp_pan.ap(oc, 0, w),
                        ps[:, 0:w], ALU.add,
                        sp_pan.keys(oc, 0, w) + [("ps", psn)],
                        sp_pan.keys(oc, 0, w), tag="sadd")
                self.dma("scalar", self.s_spill[oc * 128:(oc + 1) * 128,
                                                i0 + 1:i0 + 1 + w],
                         sp_pan.ap(oc, 0, w), sp_pan.keys(oc, 0, w),
                         [dkey("s_spill", oc, i0 // 512),
                          dkey("s_spill", oc, (i0 + w) // 512)], tag="sst", slow_ok=True)
        self.bfree(wproj)
        self.bfree(sp_pan)
        self.bfree(Kpans[0]); self.bfree(Kpans[1])
        self.bfree(K2f)
        self.bfree(Q2f)
        self.bfree(xnps[0]); self.bfree(xnps[1])
        self.bfree(fA)
        self.bfree(fB)
        self.bfree(fC)
        self.bfree(OARp)
        self.bfree(OMAp)

    def row_scatter(self, src_ap, n_rows, dst_rows, row, width, rkeys, wkeys,
                    scname, tag=""):
        d = self.dram(scname, [n_rows * width], BF16)
        self.dma("gpsimd", d.rearrange("(p f) -> p f", p=n_rows), src_ap,
                 rkeys, [dkey(scname)], tag=tag + "a")
        self.dma("gpsimd", dst_rows, d.rearrange("(o f) -> o f", o=1),
                 [dkey(scname)], wkeys, tag=tag + "b")

    def phase_l0_ffn(self):
        g = self.g
        xab0 = self.ffn_layer(0, None)
        QLO0 = g.own_lo - 768
        QN0 = (g.own_hi + 128) - QLO0
        self.xab0_spill = self.dram("xab0_spill", [D, QN0 + 1], BF16)
        for c in range(NCH):
            self.dma("scalar", self.xab0_spill[c * 128:(c + 1) * 128, :],
                     xab0.ap(c, -1, QN0), xab0.keys(c, -1, QN0),
                     [dkey("xab0_spill", c, -1)], tag="xab0sp")
        self.bfree(xab0)
        self.xab1 = self.ffn_layer(1, "spill")

    def ffn_layer(self, li, xin):
        g = self.g
        QLO = g.own_lo - (768 if li == 0 else 128)
        QLO_prev = g.own_lo - 768
        QHI = g.own_hi + 128
        KLO = QLO - 512
        QN = QHI - QLO
        KN = QHI - KLO
        n_qt = QN // 128
        n_kt = KN // 128
        L = str(li)

        wqk = self.wfull(f"wqkvqk{li}", getattr(self, f"wqkvqk{li}"))
        wv = self.wfull(f"wqkvv{li}", getattr(self, f"wqkvv{li}"))
        Vtok = self.abuf("vtok" + L, n_kt, 512, BF16)
        qhat = self.abuf("qhat" + L, 4, QN, BF16)
        khat = self.abuf("khat" + L, 4, KN, BF16)
        rotpan = self.abuf("frot" + L, 1, 512, BF16)
        tabc = self.abuf("ftabc" + L, 1, 512, BF16)
        tabs = self.abuf("ftabs" + L, 1, 512, BF16)
        f1 = self.abuf("ff1" + L, 1, 512, BF16)
        f2 = self.abuf("ff2" + L, 1, 512, BF16)
        rrow = self.abuf("frr" + L, 1, 512, BF16)
        KRS = rrow.keys(0, 0, 512)
        xpans = [self.abuf(f"fxpan{k}" + L, NCH, 513, BF16) for k in range(2)]
        sqpan = self.abuf("fsq" + L, NCH, 512, BF16)

        pan_lo = [0]
        pan_buf = [xpans[0]]
        nld = [0]

        def ld_panel(lo, hi):
            pan_lo[0] = lo
            pan_buf[0] = xpans[nld[0] % 2]
            nld[0] += 1
            xpan = pan_buf[0]
            for c in range(NCH):
                if xin is None:
                    self.load_xn(xpan, c, lo - 1, hi, tag="fxl")
                else:
                    self.load_spill(xpan, c, lo - 1, hi, self.xab0_spill,
                                    "xab0_spill", 1 - QLO_prev, tag="fxl")

        def xap(c, lo, hi):
            return pan_buf[0].ap(c, lo - (pan_lo[0] - 1), hi - (pan_lo[0] - 1))

        def xkeys(c, lo, hi):
            return pan_buf[0].keys(c, lo - (pan_lo[0] - 1),
                                   hi - (pan_lo[0] - 1))

        def qk_panel(oc, lo, hi):
            w = hi - lo
            side = 0 if oc < 4 else 1
            psn, ps = ("psA", self.psA) if oc % 2 == 0 else ("psB", self.psB)
            for kc in range(NCH):
                sh = 0 if kc < 4 else -1
                self.mm(psn, ps[:, 0:w], wqk.ap(kc, oc * 128, oc * 128 + 128),
                        xap(kc, lo + sh, hi + sh), kc == 0, kc == NCH - 1,
                        wqk.keys(kc, oc * 128, oc * 128 + 128),
                        xkeys(kc, lo + sh, hi + sh), tag="qkmm")
            hat = qhat if side == 0 else khat
            base = QLO if side == 0 else KLO
            c = oc % 4
            self.copy(hat.ap(c, lo - base, hi - base), ps[:, 0:w],
                      [("ps", psn)], hat.keys(c, lo - base, hi - base),
                      eng="scalar", tag="qkev")
            self.act(sqpan.ap(oc, 0, w), ps[:, 0:w], AF.Square,
                     [("ps", psn)], sqpan.keys(oc, 0, w), tag="qksq")

        def rot_chunk(oc, lo, hi):
            w = hi - lo
            side = 0 if oc < 4 else 1
            c = oc % 4
            hat = qhat if side == 0 else khat
            base = QLO if side == 0 else KLO
            i0 = lo - base
            sn = ("qsind" if side == 0 else "ksind") + L
            sind = getattr(self, "sb_" + sn)
            ctab = self.rotq_cos if side == 0 else self.rotk_cos
            stab = self.rotq_sin if side == 0 else self.rotk_sin
            ckey = "rotq_cos" if side == 0 else "rotk_cos"
            skey = "rotq_sin" if side == 0 else "rotk_sin"
            self.mm("psBC", self.psBC[:, 0:w], sind[:, c * 128:(c + 1) * 128],
                    rrow.ap(0, 0, w, dp=0, dn=16), True, True,
                    [("sc", sn)], KRS, tag="bcmm")
            self.tt(hat.ap(c, i0, i0 + w), hat.ap(c, i0, i0 + w),
                    self.psBC[:, 0:w], ALU.mult,
                    hat.keys(c, i0, i0 + w) + [("ps", "psBC")],
                    hat.keys(c, i0, i0 + w), tag="qn")
            for b0, b1 in ((0, 32), (32, 0), (64, 96), (96, 64)):
                self.copy(rotpan.ap(0, 0, w, dp=b0, dn=32),
                          hat.ap(c, i0, i0 + w, dp=b1, dn=32),
                          hat.keys(c, i0, i0 + w),
                          rotpan.keys(0, 0, w), eng="scalar", tag="rotc")
            self.dma("sync", tabc.ap(0, 0, w), ctab[:, lo:hi],
                     [dkey(ckey)], tabc.keys(0, 0, w), tag="tabl")
            self.dma("sync", tabs.ap(0, 0, w), stab[:, lo:hi],
                     [dkey(skey)], tabs.keys(0, 0, w), tag="tabl")
            self.tt(f1.ap(0, 0, w), hat.ap(c, i0, i0 + w), tabc.ap(0, 0, w),
                    ALU.mult, hat.keys(c, i0, i0 + w) + tabc.keys(0, 0, w),
                    f1.keys(0, 0, w), tag="rt1")
            self.tt(f2.ap(0, 0, w), rotpan.ap(0, 0, w), tabs.ap(0, 0, w),
                    ALU.mult, rotpan.keys(0, 0, w) + tabs.keys(0, 0, w),
                    f2.keys(0, 0, w), tag="rt2")
            self.tt(hat.ap(c, i0, i0 + w), f1.ap(0, 0, w), f2.ap(0, 0, w),
                    ALU.add, f1.keys(0, 0, w) + f2.keys(0, 0, w),
                    hat.keys(c, i0, i0 + w), tag="rt3")

        n_panA = (KN + 511) // 512
        for p in range(n_panA):
            lo = KLO + p * 512
            hi = min(lo + 512, QHI)
            w = hi - lo
            ld_panel(lo, hi)
            for tt_ in range(w // 128):
                kt = p * 4 + tt_
                psn, ps = ("psC", self.psC) if tt_ % 2 == 0 else ("psD", self.psD)
                t0 = lo + tt_ * 128
                for kc in range(NCH):
                    sh = 0 if kc < 4 else -1
                    self.mm(psn, ps[:, 0:512],
                            xap(kc, t0 + sh, t0 + sh + 128),
                            wv.ap(kc, 0, 512), kc == 0, kc == NCH - 1,
                            xkeys(kc, t0 + sh, t0 + sh + 128),
                            wv.keys(kc, 0, 512), tag="vmm")
                self.copy(Vtok.ap(kt, 0, 512), ps[:, 0:512], [("ps", psn)],
                          Vtok.keys(kt, 0, 512), eng="scalar", tag="vev")
            ocs = [4, 5, 6, 7] + ([0, 1, 2, 3] if lo >= QLO else [])
            for oc in ocs:
                qk_panel(oc, lo, hi)
            for i, oc in enumerate(ocs):
                self.mm("psBC2", self.psBC2[0:16, 0:w],
                        self.sb_hsel[:, oc * 16:(oc + 1) * 16],
                        sqpan.ap(oc, 0, w), i == 0, i == len(ocs) - 1,
                        [("sc", "hsel")], sqpan.keys(oc, 0, w), tag="ssq")
            self.act(rrow.ap(0, 0, w, dp=0, dn=16),
                     self.psBC2[0:16, 0:w], AF.Ln,
                     [("ps", "psBC2")], KRS, bias=1e-24, tag="lnq")
            self.act(rrow.ap(0, 0, w, dp=0, dn=16),
                     rrow.ap(0, 0, w, dp=0, dn=16), AF.Exp,
                     KRS, KRS, scale=-0.5, tag="rsq")
            for oc in ocs:
                rot_chunk(oc, lo, hi)
        self.bfree(wqk)
        self.bfree(wv)
        self.bfree(sqpan)
        self.bfree(rotpan); self.bfree(tabc); self.bfree(tabs)
        self.bfree(f1); self.bfree(f2); self.bfree(rrow)
        self.bfree(xpans[0]); self.bfree(xpans[1])

        self.tap(f"qhat{li}", qhat.full_ap(), [128, 4 * QN], qhat.kall(),
                 dtype=BF16)
        self.tap(f"khat{li}", khat.full_ap(), [128, 4 * KN], khat.kall(),
                 dtype=BF16)
        self.tap(f"vtok{li}", Vtok.full_ap(), [128, n_kt * 512], Vtok.kall(),
                 dtype=BF16)
        negmc = self.abuf("negmc" + L, 1, 256)
        sumsc = self.abuf("sumsc" + L, 1, 256)
        tstage = self.abuf("tstage" + L, 1, 128, BF16, side="hi")
        mrows = self.abuf("mrows" + L, 1, QN, BF16, side="hi")
        rrows = self.abuf("rrows" + L, 1, QN, BF16, side="hi")
        PT = self.abuf("pt" + L, 6, 640, BF16, side="hi")
        obuf = self.abuf("obuf" + L, 4, QN, BF16)
        for p0 in range(0, QN, 512):
            p1 = min(p0 + 512, QN)
            self.memset(mrows.ap(0, p0, p1, dp=0, dn=16), 0.0,
                        mrows.keys(0, p0, p1), tag="mrz")
            self.memset(rrows.ap(0, p0, p1, dp=0, dn=16), 0.0,
                        rrows.keys(0, p0, p1), tag="rrz")
        for sl in range(6):
            self.memset(PT.ap(sl, 0, 640), 0.0, PT.keys(sl, 0, 640),
                        tag="ptz")
        KNM = negmc.keys(0, 0, 256)
        KSM = sumsc.keys(0, 0, 256)
        KTS = tstage.keys(0, 0, 128)

        for h in range(8):
            ch, rb = h // 2, (h % 2) * 64
            for qt in range(n_qt):
                kq = qhat.keys(ch, qt * 128, qt * 128 + 128)

                def sseg(c0, c1):
                    if qt % 2 == 0:
                        return ("psSC", self.psSC[:, c0:c1], c0)
                    if c0 >= 512:
                        return ("psBC2", self.psBC2[:, c0 - 512:c1 - 512],
                                c0 - 512)
                    return ("psBC", self.psBC[:, c0:c1], c0)

                e1 = sseg(0, 512)
                e2 = sseg(512, 640)
                e3 = sseg(0, 128)
                e4 = sseg(512, 640)
                self.mm_groupb([
                    (e1[0], e1[1], e1[2],
                     qhat.ap(ch, qt * 128, qt * 128 + 128, dp=rb, dn=64),
                     khat.ap(ch, qt * 128, qt * 128 + 512, dp=rb, dn=64),
                     kq, khat.keys(ch, qt * 128, qt * 128 + 512)),
                    (e2[0], e2[1], e2[2],
                     qhat.ap(ch, qt * 128, qt * 128 + 128, dp=rb, dn=64),
                     khat.ap(ch, qt * 128 + 512, qt * 128 + 640, dp=rb, dn=64),
                     kq, khat.keys(ch, qt * 128 + 512, qt * 128 + 640)),
                    (e3[0], e3[1], e3[2], self.sb_identb[:],
                     self.sb_tri_first_s[:], [("sc", "identb")],
                     [("sc", "tri_first_s")]),
                    (e4[0], e4[1], e4[2], self.sb_identb[:],
                     self.sb_tri_last_s[:], [("sc", "identb")],
                     [("sc", "tri_last_s")]),
                ], tag="smm")
                if qt % 2 == 0:
                    self.red(negmc.ap(0, qt, qt + 1), self.psSC[:, 0:640],
                             ALU.max, [("ps", "psSC")], KNM, negate=True,
                             tag="smax")
                else:
                    sc = self.sc_small
                    ksc = ("sc", "sc_small")
                    self.red(sc[:, 32:33], self.psBC[:, 0:512],
                             ALU.max, [("ps", "psBC")], [ksc], negate=True,
                             tag="smax")
                    self.red(sc[:, 33:34], self.psBC2[:, 0:128],
                             ALU.max, [("ps", "psBC2")], [ksc], negate=True,
                             tag="smax")
                    self.tt(negmc.ap(0, qt, qt + 1), sc[:, 32:33],
                            sc[:, 33:34], ALU.min, [ksc], KNM, tag="smax")
            self.transpose("psD", self.psD[0:n_qt, 0:128],
                           negmc.ap(0, 0, n_qt), KNM, tag="mtr")
            self.copy(tstage.ap(0, 0, 128, dp=0, dn=n_qt),
                      self.psD[0:n_qt, 0:128], [("ps", "psD")], KTS,
                      eng="vector", tag="mcp")
            self.row_scatter(tstage.ap(0, 0, 128, dp=0, dn=n_qt), n_qt,
                             mrows.ap(0, 0, n_qt * 128, dp=h, dn=1), h,
                             128, KTS, mrows.keys(0, 0, QN),
                             f"mrow_sc{L}_{h}", tag="mrs")

            def do_pv(qt):
                pv_ops = []
                sum_ops = []
                for r in range(5):
                    ktr = qt + r
                    slot = ktr % 6
                    qtlo_r = max(0, ktr - 4)
                    off = (qt - qtlo_r) * 128
                    pv_ops.append((self.psA[0:64, 0:128], 0,
                                   Vtok.ap(ktr, h * 64, h * 64 + 64),
                                   PT.ap(slot, off, off + 128),
                                   Vtok.keys(ktr, h * 64, h * 64 + 64),
                                   PT.keys(slot, off, off + 128)))
                    sum_ops.append((self.psB[:, 0:1], 0,
                                    PT.ap(slot, off, off + 128),
                                    self.onescol_b[:],
                                    PT.keys(slot, off, off + 128),
                                    [("sc", "onescol_b")]))
                self.mm_group("psA", pv_ops, tag="pvmm")
                self.mm_group("psB", sum_ops, tag="summ")
                self.copy(obuf.ap(ch, qt * 128, qt * 128 + 128, dp=rb, dn=64),
                          self.psA[0:64, 0:128], [("ps", "psA")],
                          obuf.keys(ch, qt * 128, qt * 128 + 128),
                          eng="vector", tag="oev")
                self.copy(sumsc.ap(0, qt, qt + 1), self.psB[:, 0:1],
                          [("ps", "psB")], KSM, eng="vector", tag="sev")

            for kt in range(n_kt):
                qtlo = max(0, kt - 4)
                qthi = min(kt + 1, n_qt)
                w_st = (qthi - qtlo) * 128

                def tseg(c0, c1):
                    if kt % 2 == 0:
                        return ("psSC", self.psSC[:, c0:c1], c0)
                    if c0 >= 512:
                        return ("psBC2", self.psBC2[:, c0 - 512:c1 - 512],
                                c0 - 512)
                    return ("psBC", self.psBC[:, c0:c1], c0)

                ops = []
                kk = khat.keys(ch, kt * 128, kt * 128 + 128)
                for c0 in range(0, w_st, 512):
                    c1 = min(c0 + 512, w_st)
                    e = tseg(c0, c1)
                    ops.append((e[0], e[1], e[2],
                                khat.ap(ch, kt * 128, kt * 128 + 128,
                                        dp=rb, dn=64),
                                qhat.ap(ch, qtlo * 128 + c0, qtlo * 128 + c1,
                                        dp=rb, dn=64),
                                kk,
                                qhat.keys(ch, qtlo * 128 + c0,
                                          qtlo * 128 + c1)))
                for c0 in range(0, w_st, 512):
                    c1 = min(c0 + 512, w_st)
                    e = tseg(c0, c1)
                    ops.append((e[0], e[1], e[2],
                                self.sb_osel[:, h * 128:(h + 1) * 128],
                                mrows.ap(0, qtlo * 128 + c0, qtlo * 128 + c1,
                                         dp=0, dn=16),
                                [("sc", "osel")],
                                mrows.keys(0, qtlo * 128 + c0,
                                           qtlo * 128 + c1)))
                if kt < n_qt:
                    off = (kt - qtlo) * 128
                    e = tseg(off, off + 128)
                    ops.append((e[0], e[1], e[2],
                                self.sb_identb[:], self.sb_tri_first_st[:],
                                [("sc", "identb")], [("sc", "tri_first_st")]))
                if 0 <= kt - 4 < n_qt:
                    off = (kt - 4 - qtlo) * 128
                    e = tseg(off, off + 128)
                    ops.append((e[0], e[1], e[2],
                                self.sb_identb[:], self.sb_tri_last_st[:],
                                [("sc", "identb")], [("sc", "tri_last_st")]))
                self.mm_groupb(ops, tag="stmm")
                slot = kt % 6
                dcb = self.sb_deadcol[:, (KLO // 128) + kt:
                                      (KLO // 128) + kt + 1]
                if kt % 2 == 0:
                    self.act(PT.ap(slot, 0, w_st), self.psSC[:, 0:w_st],
                             AF.Exp, [("ps", "psSC"), ("sc", "deadcol")],
                             PT.keys(slot, 0, w_st), bias=dcb, tag="stexp")
                else:
                    wlo = min(w_st, 512)
                    self.act(PT.ap(slot, 0, wlo), self.psBC[:, 0:wlo],
                             AF.Exp, [("ps", "psBC"), ("sc", "deadcol")],
                             PT.keys(slot, 0, wlo), bias=dcb, tag="stexp")
                    if w_st > 512:
                        self.act(PT.ap(slot, 512, w_st),
                                 self.psBC2[:, 0:w_st - 512], AF.Exp,
                                 [("ps", "psBC2"), ("sc", "deadcol")],
                                 PT.keys(slot, 512, w_st), bias=dcb,
                                 tag="stexp")
                if 0 <= kt - 4 < n_qt:
                    do_pv(kt - 4)
            self.ts(sumsc.ap(0, 0, n_qt), sumsc.ap(0, 0, n_qt), 1e-30,
                    ALU.add, KSM, KSM, tag="seps")
            self.recip(sumsc.ap(0, 0, n_qt), sumsc.ap(0, 0, n_qt), KSM, KSM,
                       tag="srec")
            self.transpose("psD", self.psD[0:n_qt, 0:128],
                           sumsc.ap(0, 0, n_qt), KSM, tag="rtr")
            self.copy(tstage.ap(0, 0, 128, dp=0, dn=n_qt),
                      self.psD[0:n_qt, 0:128], [("ps", "psD")], KTS,
                      eng="vector", tag="rcp")
            self.row_scatter(tstage.ap(0, 0, 128, dp=0, dn=n_qt), n_qt,
                             rrows.ap(0, 0, n_qt * 128, dp=h, dn=1), h,
                             128, KTS, rrows.keys(0, 0, QN),
                             f"rrow_sc{L}_{h}", tag="rrs")
            if h % 2 == 1:
                for p0 in range(0, QN, 512):
                    p1 = min(p0 + 512, QN)
                    self.mm("psBC", self.psBC[:, 0:p1 - p0],
                            self.sb_hind16[:, ch * 128:(ch + 1) * 128],
                            rrows.ap(0, p0, p1, dp=0, dn=16), True, True,
                            [("sc", "hind16")], rrows.keys(0, p0, p1),
                            tag="obc")
                    self.tt(obuf.ap(ch, p0, p1), obuf.ap(ch, p0, p1),
                            self.psBC[:, 0:p1 - p0], ALU.mult,
                            obuf.keys(ch, p0, p1) + [("ps", "psBC")],
                            obuf.keys(ch, p0, p1), tag="osc")
        self.bfree(negmc); self.bfree(sumsc); self.bfree(tstage)
        self.bfree(mrows); self.bfree(rrows); self.bfree(PT)
        self.bfree(khat); self.bfree(qhat); self.bfree(Vtok)

        self.tap(f"obuf{li}", obuf.full_ap(), [128, 4 * QN], obuf.kall(),
                 dtype=BF16)
        self.tap(f"mrows{li}", mrows.full_ap(), [128, QN], mrows.kall(),
                 dtype=BF16)
        self.tap(f"rrows{li}", rrows.full_ap(), [128, QN], rrows.kall(),
                 dtype=BF16)
        xpans[0] = self.abuf(f"fxpan0b" + L, NCH, 513, BF16)
        xpans[1] = self.abuf(f"fxpan1b" + L, NCH, 513, BF16)
        wo = self.wfull(f"wo{li}", getattr(self, f"wo{li}"), rows=512)
        xab = self.abuf("xab" + L, NCH, QN, BF16, lead=1, side="hi")
        for c in range(NCH):
            self.memset(xab.ap(c, -1, 0), 0.0, xab.keys(c, -1, 0), tag="xabz")
        for p0 in range(0, QN, 512):
            p1 = min(p0 + 512, QN)
            w = p1 - p0
            ld_panel(QLO + p0, QLO + p1)
            for oc in range(NCH):
                psn, ps = ("psA", self.psA) if oc % 2 == 0 else ("psB", self.psB)
                for kc in range(4):
                    self.mm(psn, ps[:, 0:w], wo.ap(kc, oc * 128, oc * 128 + 128),
                            obuf.ap(kc, p0, p1), kc == 0, kc == 3,
                            wo.keys(kc, oc * 128, oc * 128 + 128),
                            obuf.keys(kc, p0, p1), tag="womm")
                self.tt(xab.ap(oc, p0, p1), xap(oc, QLO + p0, QLO + p1),
                        ps[:, 0:w], ALU.add,
                        xkeys(oc, QLO + p0, QLO + p1) + [("ps", psn)],
                        xab.keys(oc, p0, p1), tag="worid")
        self.bfree(wo)
        self.bfree(obuf)
        for b_ in xpans:
            self.bfree(b_)
        self.tap(f"xa{li}", xab.full_ap(), [128, NCH * (QN + 1)],
                 xab.kall(), dtype=BF16)
        self.ffn_mlp(li, xab, QN)
        if li == 0:
            bi = (g.own_lo - 1) - QLO
            for c in range(NCH):
                self.ts(xab.ap(c, bi, bi + 1), xab.ap(c, bi, bi + 1),
                        self.sb_edge0[:, 0:1], ALU.mult,
                        xab.keys(c, bi, bi + 1) + [("sc", "edge0")],
                        xab.keys(c, bi, bi + 1), tag="xabe0")
        self.tap(f"xab{li}", xab.full_ap(), [128, NCH * (QN + 1)],
                 xab.kall(), dtype=BF16)
        return xab

    def ffn_mlp(self, li, xab, FR):
        g = self.g
        L = str(li)
        assert FR % 4 == 0
        ql = FR // 4
        prodb = self.abuf("prod" + L, 22, ql, BF16)
        w1blk = [self.abuf(f"w1b{k}" + L, NCH, 256, BF16) for k in range(2)]
        w2pair = [self.abuf(f"w2p{k}" + L, 22, 256, BF16) for k in range(2)]
        fgg2 = [self.abuf(f"fgg{k}" + L, 1, 512, BF16) for k in range(2)]
        htil = self.abuf("htil" + L, NCH, ql, BF16)
        stash = self.sbuf(f"ffnstash{li}", [128, 8])
        KST = [("sc", f"ffnstash{li}")]
        lnrowA = self.abuf("lnA" + L, 1, FR, BF16)
        lnrowB = self.abuf("lnB" + L, 1, FR, BF16)
        sqp = self.abuf("msq" + L, NCH, 512, BF16)
        if not hasattr(self, "_mrow1"):
            self._mrow1 = self.sbuf("mrow1", [1, 512])
            self._mrow2 = self.sbuf("mrow2", [1, 512])
        r1, r2 = self._mrow1, self._mrow2
        KR1 = [("sc", "mrow1")]
        KR2 = [("sc", "mrow2")]
        lgb = self.sbuf(f"sb_lgb{li}", [2, NCH * 128], BF16)
        self.dma("sync", lgb[:], getattr(self, f"lgb{li}")[:, :],
                 [dkey(f"lgb{li}")], [("sc", f"lgb{li}")], tag="lgb")
        KLGB = [("sc", f"lgb{li}")]
        if not hasattr(self, "_onerow"):
            self._onerow = self.sbuf("onerow", [1, 512], BF16)
            self.memset(self._onerow[:], 1.0, [("sc", "onerow")], tag="or0")
        for p0 in range(0, FR, 512):
            p1 = min(p0 + 512, FR)
            self.dma("sync", lnrowB.ap(0, p0, p1, dp=1, dn=1),
                     self._onerow[:, 0:p1 - p0], [("sc", "onerow")],
                     lnrowB.keys(0, p0, p1), tag="onerow")
        lgc = self.sbuf(f"sb_lgc{li}", [128, NCH])
        self.dma("sync", lgc[:], getattr(self, f"lgc{li}")[:, :],
                 [dkey(f"lgc{li}")], [("sc", f"lgc{li}")], tag="lgc")

        def shx(c, p0, p1):
            sh = 0 if c < 4 else -1
            return xab.ap(c, p0 + sh, p1 + sh)

        def shxk(c, p0, p1):
            sh = 0 if c < 4 else -1
            return xab.keys(c, p0 + sh, p1 + sh)

        for p0 in range(0, FR, 512):
            p1 = min(p0 + 512, FR)
            w = p1 - p0
            for c in range(NCH):
                self.act(sqp.ap(c, 0, w), shx(c, p0, p1), AF.Square,
                         shxk(c, p0, p1), sqp.keys(c, 0, w),
                         tag="msq")
            for c in range(NCH):
                self.mm("psBC", self.psBC[:, 0:w], self.ones128_b[:],
                        shx(c, p0, p1), c == 0, c == NCH - 1,
                        [("sc", "ones128_b")], shxk(c, p0, p1),
                        tag="ms1")
                self.mm("psBC2", self.psBC2[:, 0:w], self.ones128_b[:],
                        sqp.ap(c, 0, w), c == 0, c == NCH - 1,
                        [("sc", "ones128_b")], sqp.keys(c, 0, w), tag="ms2")
            self.act(r1[:, 0:w], self.psBC[0:1, 0:w], AF.Square,
                     [("ps", "psBC")], KR1, tag="mr1")
            self.stt(r1[:, 0:w], r1[:, 0:w], -1.0 / D, self.psBC2[0:1, 0:w],
                     ALU.mult, ALU.add, KR1 + [("ps", "psBC2")], KR1,
                     tag="mr2")
            self.act(r1[:, 0:w], r1[:, 0:w], AF.Ln, KR1, KR1, bias=1e-5,
                     scale=1.0 / D, tag="mr3")
            self.act(lnrowA.ap(0, p0, p1, dp=0, dn=1), r1[:, 0:w], AF.Exp,
                     KR1, lnrowA.keys(0, p0, p1), scale=-0.5, tag="mr4")
            self.ts(r2[:, 0:w], self.psBC[0:1, 0:w], -1.0 / D, ALU.mult,
                    [("ps", "psBC")], KR2, tag="mr5")
            self.tt(lnrowB.ap(0, p0, p1, dp=0, dn=1), r2[:, 0:w],
                    lnrowA.ap(0, p0, p1, dp=0, dn=1), ALU.mult,
                    KR2 + lnrowA.keys(0, p0, p1), lnrowB.keys(0, p0, p1),
                    tag="mr6")
        self.bfree(sqp)

        w1d = getattr(self, f"w1pb{li}")
        w2d = getattr(self, f"w2pcb{li}")
        for qf in range(4):
            H0 = qf * ql
            if qf < 3:
                for c in range(4, NCH):
                    sg = (qf % 2) * 4 + (c - 4)
                    self.copy(stash[:, sg:sg + 1],
                              xab.ap(c, H0 + ql - 1, H0 + ql),
                              xab.keys(c, H0 + ql - 1, H0 + ql), KST,
                              tag="stsh")
            for p0 in range(H0, H0 + ql, 512):
                p1 = min(p0 + 512, H0 + ql)
                w = p1 - p0
                self.mm("psBC", self.psBC[:, 0:w], self.onesrow_b[:],
                        lnrowA.ap(0, p0, p1, dp=0, dn=1), True, True,
                        [("sc", "onesrow_b")], lnrowA.keys(0, p0, p1),
                        tag="hbc1")
                for c in range(NCH):
                    self.tt(htil.ap(c, p0 - H0, p1 - H0),
                            shx(c, p0, p1), self.psBC[:, 0:w],
                            ALU.mult,
                            shxk(c, p0, p1) + [("ps", "psBC")],
                            htil.keys(c, p0 - H0, p1 - H0), tag="ht1")
                    if qf > 0 and p0 == H0 and c >= 4:
                        sg = ((qf - 1) % 2) * 4 + (c - 4)
                        self.tt(htil.ap(c, 0, 1), stash[:, sg:sg + 1],
                                self.psBC[:, 0:1], ALU.mult,
                                KST + [("ps", "psBC")],
                                htil.keys(c, 0, 1), tag="ht1s")
                for c in range(NCH):
                    self.mm("psBC2", self.psBC2[:, 0:w],
                            lgb[0:2, c * 128:(c + 1) * 128],
                            lnrowB.ap(0, p0, p1, dp=0, dn=2), True, True,
                            KLGB, lnrowB.keys(0, p0, p1), tag="hbc2")
                    self.stt(htil.ap(c, p0 - H0, p1 - H0),
                             htil.ap(c, p0 - H0, p1 - H0), lgc[:, c:c + 1],
                             self.psBC2[:, 0:w], ALU.mult, ALU.add,
                             htil.keys(c, p0 - H0, p1 - H0) +
                             [("sc", f"lgc{li}"), ("ps", "psBC2")],
                             htil.keys(c, p0 - H0, p1 - H0), tag="ht2")
            for i in range(22):
                wb = w1blk[i % 2]
                fgg = fgg2[i % 2]
                self.dma("gpsimd", wb.full_ap(),
                         w1d[i * 128:(i + 1) * 128, :],
                         [dkey(f"w1pb{li}")], wb.kall(), tag="w1l")
                psan, psa = ("psA", self.psA) if i % 2 == 0 else \
                    ("psC", self.psC)
                psgn, psg = ("psB", self.psB) if i % 2 == 0 else \
                    ("psD", self.psD)
                for p0 in range(0, ql, 512):
                    p1 = min(p0 + 512, ql)
                    w = p1 - p0
                    for kc in range(NCH):
                        self.mm(psan, psa[:, 0:w], wb.ap(kc, 0, 128),
                                htil.ap(kc, p0, p1), kc == 0, kc == NCH - 1,
                                wb.keys(kc, 0, 128), htil.keys(kc, p0, p1),
                                tag="w1a")
                        self.mm(psgn, psg[:, 0:w], wb.ap(kc, 128, 256),
                                htil.ap(kc, p0, p1), kc == 0, kc == NCH - 1,
                                wb.keys(kc, 128, 256), htil.keys(kc, p0, p1),
                                tag="w1g")
                    self.act(fgg.ap(0, 0, w), psg[:, 0:w], AF.Gelu,
                             [("ps", psgn), ("sc", f"b1g{li}")],
                             fgg.keys(0, 0, w),
                             bias=self.sb_bias[f"b1g{li}"][:, i:i + 1],
                             tag="gev")
                    self.stt(prodb.ap(i, p0, p1), psa[:, 0:w],
                             self.sb_bias[f"b1a{li}"][:, i:i + 1],
                             fgg.ap(0, 0, w), ALU.add, ALU.mult,
                             [("ps", psan), ("sc", f"b1a{li}")] +
                             fgg.keys(0, 0, w),
                             prodb.keys(i, p0, p1), tag="prod")
            npan = (ql + 511) // 512
            assert npan <= 3
            bankset = [("psA", self.psA), ("psB", self.psB),
                       ("psC", self.psC), ("psD", self.psD),
                       ("psBC", self.psBC), ("psBC2", self.psBC2)]
            for pr in range(4):
                wp = w2pair[pr % 2]
                self.dma("gpsimd", wp.full_ap(),
                         w2d[pr * 128:(pr + 1) * 128, :],
                         [dkey(f"w2pcb{li}")],
                         wp.kall(), tag="w2l")
                for i in range(22):
                    for o2 in range(2):
                        for sp in range(npan):
                            p0 = sp * 512
                            p1 = min(p0 + 512, ql)
                            psn, ps = bankset[o2 * npan + sp]
                            self.mm(psn, ps[:, 0:p1 - p0],
                                    wp.ap(i, o2 * 128, o2 * 128 + 128),
                                    prodb.ap(i, p0, p1), i == 0, i == 21,
                                    wp.keys(i, o2 * 128, o2 * 128 + 128),
                                    prodb.keys(i, p0, p1), tag="w2mm")
                for o2 in range(2):
                    oc = pr * 2 + o2
                    for sp in range(npan):
                        p0 = sp * 512
                        p1 = min(p0 + 512, ql)
                        psn, ps = bankset[o2 * npan + sp]
                        self.stt(xab.ap(oc, H0 + p0, H0 + p1),
                                 ps[:, 0:p1 - p0],
                                 self.sb_bias[f"b2c{li}"][:, oc:oc + 1],
                                 xab.ap(oc, H0 + p0, H0 + p1),
                                 ALU.add, ALU.add,
                                 [("ps", psn), ("sc", f"b2c{li}")] +
                                 xab.keys(oc, H0 + p0, H0 + p1),
                                 xab.keys(oc, H0 + p0, H0 + p1), tag="w2ev")
        self.bfree(lnrowA); self.bfree(lnrowB); self.bfree(htil)
        self.bfree(prodb)
        for b in w1blk + w2pair:
            self.bfree(b)
        self.bfree(fgg2[0]); self.bfree(fgg2[1])

    def phase_combine(self):
        g = self.g
        NOS = g.t_own + 2
        self.x1mod_spill = self.dram("x1mod_spill", [D, NOS], F32)
        self.xnL1 = self.abuf("xnL1", NCH, NOS, BF16)
        xnL1 = self.xnL1
        QLO1 = g.own_lo - 128
        gpan = self.abuf("cgp", NCH, 256, BF16)
        xp = self.abuf("cxp", NCH, 256)
        sp = self.abuf("csp", NCH, 256, BF16)
        ops2 = [self.abuf(f"cop{k}", NCH, 256) for k in range(2)]
        xsq = self.abuf("cxq", NCH, 256)
        rstd = self.sbuf("l1_rstd", [128, 256])
        KRS1 = [("sc", "l1_rstd")]
        for ip, p0 in enumerate(range(0, NOS, 256)):
            p1 = min(p0 + 256, NOS)
            w = p1 - p0
            op = ops2[ip % 2]
            t0 = g.own_lo - 1 + p0
            for c in range(NCH):
                self.act(gpan.ap(c, 0, w), self.xab1.ap(c, t0 - QLO1, t0 - QLO1 + w),
                         AF.Silu, self.xab1.keys(c, t0 - QLO1, t0 - QLO1 + w),
                         gpan.keys(c, 0, w), tag="silu")
                self.dma("sync", xp.ap(c, 0, w),
                         self.xfm[c * 128:(c + 1) * 128, t0:t0 + w],
                         [dkey("xfm")], xp.keys(c, 0, w), tag="cx")
                self.dma("scalar", sp.ap(c, 0, w),
                         self.s_spill[c * 128:(c + 1) * 128, p0:p1],
                         [dkey("s_spill", c)], sp.keys(c, 0, w), tag="cs")
                self.tt(sp.ap(c, 0, w), sp.ap(c, 0, w), gpan.ap(c, 0, w),
                        ALU.mult, sp.keys(c, 0, w) + gpan.keys(c, 0, w),
                        sp.keys(c, 0, w), tag="sg")
                self.tt(op.ap(c, 0, w), xp.ap(c, 0, w), sp.ap(c, 0, w),
                        ALU.add, xp.keys(c, 0, w) + sp.keys(c, 0, w),
                        op.keys(c, 0, w), tag="x1m")
                self.dma("scalar", self.x1mod_spill[c * 128:(c + 1) * 128, p0:p1],
                         op.ap(c, 0, w), op.keys(c, 0, w),
                         [dkey("x1mod_spill", c, p0 // 512)], tag="x1sp")
            for c in range(NCH):
                self.act(xsq.ap(c, 0, w), op.ap(c, 0, w), AF.Square,
                         op.keys(c, 0, w), xsq.keys(c, 0, w), tag="l1sq")
            for c in range(NCH):
                self.mm("psBC", self.psBC[:, 0:w], self.ones128[:],
                        op.ap(c, 0, w), c == 0, c == NCH - 1,
                        [("sc", "ones128")], op.keys(c, 0, w), tag="l1s1")
                self.mm("psBC2", self.psBC2[:, 0:w], self.ones128[:],
                        xsq.ap(c, 0, w), c == 0, c == NCH - 1,
                        [("sc", "ones128")], xsq.keys(c, 0, w), tag="l1s2")
            self.act(rstd[:, 0:w], self.psBC[:, 0:w], AF.Square,
                     [("ps", "psBC")], KRS1, tag="l1v1")
            self.stt(rstd[:, 0:w], rstd[:, 0:w], -1.0 / D, self.psBC2[:, 0:w],
                     ALU.mult, ALU.add, KRS1 + [("ps", "psBC2")], KRS1,
                     tag="l1v2")
            self.act(rstd[:, 0:w], rstd[:, 0:w], AF.Ln, KRS1, KRS1,
                     bias=1e-5, scale=1.0 / D, tag="l1v3")
            self.act(rstd[:, 0:w], rstd[:, 0:w], AF.Exp, KRS1, KRS1,
                     scale=-0.5, tag="l1v4")
            for c in range(NCH):
                self.stt(xsq.ap(c, 0, w), self.psBC[:, 0:w], -1.0 / D,
                         op.ap(c, 0, w), ALU.mult, ALU.add,
                         [("ps", "psBC")] + op.keys(c, 0, w),
                         xsq.keys(c, 0, w), tag="l1c")
                self.tt(xnL1.ap(c, p0, p1), xsq.ap(c, 0, w), rstd[:, 0:w],
                        ALU.mult, xsq.keys(c, 0, w) + KRS1,
                        xnL1.keys(c, p0, p1), tag="l1n")
        self.bfree(gpan); self.bfree(xp); self.bfree(sp)
        self.bfree(ops2[0]); self.bfree(ops2[1]); self.bfree(xsq)
        self.bfree(self.xab1)


    def phase_l1(self):
        g = self.g
        NOS = g.t_own + 2
        TO = g.t_own
        PW1 = 256

        xnL1 = self.xnL1
        for c in range(NCH):
            self.ts(xnL1.ap(c, 0, 1), xnL1.ap(c, 0, 1), self.sb_edge0[:, 0:1],
                    ALU.mult, xnL1.keys(c, 0, 1) + [("sc", "edge0")],
                    xnL1.keys(c, 0, 1), tag="l1e0")
            self.ts(xnL1.ap(c, NOS - 1, NOS), xnL1.ap(c, NOS - 1, NOS),
                    self.sb_edge3[:, 0:1], ALU.mult,
                    xnL1.keys(c, NOS - 1, NOS) + [("sc", "edge3")],
                    xnL1.keys(c, NOS - 1, NOS), tag="l1e3")

        self.tap("xnL1", xnL1.full_ap(), [128, NCH * NOS], xnL1.kall(),
                 dtype=BF16)
        l1c = self.sbuf("l1cols", [128, 48])
        KL1C = [("sc", "l1cols")]
        ewc = lambda c: l1c[:, c:c + 1]
        euc = lambda c: l1c[:, 8 + c:9 + c]
        ach = lambda c: l1c[:, 16 + c:17 + c]
        bch = lambda c: l1c[:, 24 + c:25 + c]
        ag3in = l1c[:, 32:48]
        for c in range(NCH):
            self.act(ewc(c), self.sb_sdc[:, c:c + 1], AF.Exp,
                     [("sc", "sdc")], KL1C, scale=1.0 / g.seq, tag="ew")
            self.act(euc(c), self.sb_sfc[:, c:c + 1], AF.Exp,
                     [("sc", "sfc")], KL1C, scale=1.0 / g.seq, tag="eu")

        m1 = self.sbuf("l1m1", [128, 24])
        KM1 = [("sc", "l1m1")]
        for i, n in enumerate(("mixkc", "mixvc", "mixrc")):
            self.ts(m1[:, i * 8:(i + 1) * 8], getattr(self, "sb_" + n)[:],
                    -1.0, ALU.mult, [("sc", n)], KM1, s2=1.0, op1=ALU.add,
                    tag="m1")

        mixp = self.abuf("l1mix", 4, PW1, BF16)
        ekf = self.abuf("l1ek", NCH, TO, BF16)
        ekvf = self.abuf("l1ekv", NCH, TO, BF16)
        fE = self.abuf("l1fE", NCH, PW1)
        fV = self.abuf("l1fV", NCH, PW1)
        fA2 = self.abuf("l1fA", NCH, PW1, lead=1)
        fB2 = self.abuf("l1fB", NCH, PW1, lead=1)

        def mk_mix(mi, mname, lo, hi, tag):
            w = hi - lo
            mcol = getattr(self, "sb_" + mname)
            for c in range(4):
                sh = -1 if c < 2 else 1
                self.ts(fE.ap(c, 0, w), xnL1.ap(c, lo, hi), mcol[:, c:c + 1],
                        ALU.mult, xnL1.keys(c, lo, hi) + [("sc", mname)],
                        fE.keys(c, 0, w), tag=tag + "a")
                self.stt(mixp.ap(c, 0, w), xnL1.ap(c, lo + sh, hi + sh),
                         m1[:, mi * 8 + c:mi * 8 + c + 1], fE.ap(c, 0, w),
                         ALU.mult, ALU.add,
                         xnL1.keys(c, lo + sh, hi + sh) + KM1 + fE.keys(c, 0, w),
                         mixp.keys(c, 0, w), tag=tag + "b")

        def l1_sweep(wname, mi, mname, evac):
            wd = getattr(self, wname)
            wts = [self.abuf("w_" + wname, NCH, 512, BF16)]
            for ho in (0, 512):
                wt = wts[0]
                self.dma("gpsimd",
                         wt.full_ap().rearrange("p (a c) -> p a c", a=NCH),
                         wd[:, ho:ho + 512].rearrange("(a p) c -> p a c",
                                                      p=128),
                         [dkey(wname)], wt.kall(), tag="wload_" + wname)
                for p0 in range(0, TO, PW1):
                    p1 = min(p0 + PW1, TO)
                    w = p1 - p0
                    mk_mix(mi, mname, 1 + p0, 1 + p1, tag=wname)
                    for oc2 in range(4):
                        oc = ho // 128 + oc2
                        psn, ps = (("psA", self.psA) if oc % 2 == 0
                                   else ("psB", self.psB))
                        for kc in range(NCH):
                            rhs = (mixp.ap(kc, 0, w) if kc < 4
                                   else xnL1.ap(kc, 1 + p0, 1 + p1))
                            rk = (mixp.keys(kc, 0, w) if kc < 4
                                  else xnL1.keys(kc, 1 + p0, 1 + p1))
                            self.mm(psn, ps[:, 0:w],
                                    wt.ap(kc, oc2 * 128, oc2 * 128 + 128),
                                    rhs, kc == 0, kc == NCH - 1,
                                    wt.keys(kc, oc2 * 128, oc2 * 128 + 128),
                                    rk, tag=wname)
                        evac(oc, p0, p1, psn, ps)
            for w_ in wts:
                self.bfree(w_)

        wk = self.wfull("wkt", self.wkt)
        wv = self.wfull("wvt", self.wvt)
        for p0 in range(0, TO, PW1):
            p1 = min(p0 + PW1, TO)
            w = p1 - p0
            mk_mix(0, "mixkc", 1 + p0, 1 + p1, tag="l1k")
            for oc in range(NCH):
                psn, ps = ("psC", self.psC) if oc % 2 == 0 else ("psD", self.psD)
                for kc in range(NCH):
                    rhs = (mixp.ap(kc, 0, w) if kc < 4
                           else xnL1.ap(kc, 1 + p0, 1 + p1))
                    rk = (mixp.keys(kc, 0, w) if kc < 4
                          else xnL1.keys(kc, 1 + p0, 1 + p1))
                    self.mm(psn, ps[:, 0:w],
                            wk.ap(kc, oc * 128, oc * 128 + 128), rhs,
                            kc == 0, kc == NCH - 1,
                            wk.keys(kc, oc * 128, oc * 128 + 128), rk,
                            tag="l1kmm")
                self.act(ekf.ap(oc, p0, p1), ps[:, 0:w], AF.Exp,
                         [("ps", psn)], ekf.keys(oc, p0, p1), tag="evek")
            mk_mix(1, "mixvc", 1 + p0, 1 + p1, tag="l1v")
            for oc in range(NCH):
                psn, ps = ("psC", self.psC) if oc % 2 == 0 else ("psD", self.psD)
                for kc in range(NCH):
                    rhs = (mixp.ap(kc, 0, w) if kc < 4
                           else xnL1.ap(kc, 1 + p0, 1 + p1))
                    rk = (mixp.keys(kc, 0, w) if kc < 4
                          else xnL1.keys(kc, 1 + p0, 1 + p1))
                    self.mm(psn, ps[:, 0:w],
                            wv.ap(kc, oc * 128, oc * 128 + 128), rhs,
                            kc == 0, kc == NCH - 1,
                            wv.keys(kc, oc * 128, oc * 128 + 128), rk,
                            tag="l1vmm")
                self.tt(ekvf.ap(oc, p0, p1), ekf.ap(oc, p0, p1),
                        ps[:, 0:w], ALU.mult,
                        ekf.keys(oc, p0, p1) + [("ps", psn)],
                        ekvf.keys(oc, p0, p1), tag="evekv")
            for oc in range(NCH):
                if p0 == 0:
                    self.memset(ach(oc), 0.0, KL1C, tag="a0")
                    self.memset(bch(oc), 0.0, KL1C, tag="b0")
                self.scan(fV.ap(oc, 0, w), ewc(oc).broadcast_to([128, w]),
                          ekvf.ap(oc, p0, p1), ach(oc),
                          ekvf.keys(oc, p0, p1) + KL1C, fV.keys(oc, 0, w),
                          tag="asc")
                self.copy(ach(oc), fV.ap(oc, w - 1, w), fV.keys(oc, w - 1, w),
                          KL1C, tag="aup")
                self.scan(fE.ap(oc, 0, w), ewc(oc).broadcast_to([128, w]),
                          ekf.ap(oc, p0, p1), bch(oc),
                          ekf.keys(oc, p0, p1) + KL1C, fE.keys(oc, 0, w),
                          tag="bsc")
                self.copy(bch(oc), fE.ap(oc, w - 1, w), fE.keys(oc, w - 1, w),
                          KL1C, tag="bup")
        self.bfree(wk)
        self.bfree(wv)
        for c in range(NCH):
            self.copy(ag3in[:, c:c + 1], ach(c), KL1C, KL1C, tag="p3a")
            self.copy(ag3in[:, 8 + c:9 + c], bch(c), KL1C, KL1C, tag="p3b")
        agg3 = self.allgather(3, ag3in, 16, KL1C)
        wkvw = self.sbuf("wkvw", [128, 64])
        KWW = [("sc", "wkvw")]
        for c in range(NCH):
            self.ts(wkvw[:, c * 8:(c + 1) * 8], self.sb_pexprep[:],
                    self.sb_sdc[:, c:c + 1], ALU.mult,
                    [("sc", "pexprep"), ("sc", "sdc")], KWW, tag="ww1")
            self.act(wkvw[:, c * 8:(c + 1) * 8], wkvw[:, c * 8:(c + 1) * 8],
                     AF.Exp, KWW, KWW, scale=0.25, tag="ww2")
            self.tt(wkvw[:, c * 8:(c + 1) * 8], wkvw[:, c * 8:(c + 1) * 8],
                    self.sb_pmaskrep[:], ALU.mult,
                    KWW + [("sc", "pmaskrep")], KWW, tag="ww3")
        car3 = self.carry_cols(agg3, 16, "car3",
                               weights=(wkvw[:, 0:64], ("sc", "wkvw")))
        k_car3 = ("sc", "car3_c")

        srf = self.abuf("l1sr", NCH, TO, BF16)

        def evr(oc, p0, p1, psn, ps):
            self.act(srf.ap(oc, p0, p1), ps[:, 0:p1 - p0], AF.Sigmoid,
                     [("ps", psn)], srf.keys(oc, p0, p1), tag="evr")

        l1_sweep("wrt", 2, "mixrc", evr)

        for p0 in range(0, TO, PW1):
            p1 = min(p0 + PW1, TO)
            w = p1 - p0
            for oc in range(NCH):
                if p0 == 0:
                    self.copy(ach(oc), car3[:, oc:oc + 1], [k_car3], KL1C,
                              tag="ai")
                    self.copy(bch(oc), car3[:, 8 + oc:9 + oc], [k_car3], KL1C,
                              tag="bi")
                self.copy(fA2.ap(oc, -1, 0), ach(oc), KL1C,
                          fA2.keys(oc, -1, 0), tag="al")
                self.copy(fB2.ap(oc, -1, 0), bch(oc), KL1C,
                          fB2.keys(oc, -1, 0), tag="bl")
                self.scan(fA2.ap(oc, 0, w), ewc(oc).broadcast_to([128, w]),
                          ekvf.ap(oc, p0, p1), ach(oc),
                          ekvf.keys(oc, p0, p1) + KL1C, fA2.keys(oc, 0, w),
                          tag="asc2")
                self.copy(ach(oc), fA2.ap(oc, w - 1, w), fA2.keys(oc, w - 1, w),
                          KL1C, tag="aup2")
                self.scan(fB2.ap(oc, 0, w), ewc(oc).broadcast_to([128, w]),
                          ekf.ap(oc, p0, p1), bch(oc),
                          ekf.keys(oc, p0, p1) + KL1C, fB2.keys(oc, 0, w),
                          tag="bsc2")
                self.copy(bch(oc), fB2.ap(oc, w - 1, w), fB2.keys(oc, w - 1, w),
                          KL1C, tag="bup2")
                self.stt(fE.ap(oc, 0, w), ekf.ap(oc, p0, p1), euc(oc),
                         fB2.ap(oc, -1, w - 1), ALU.mult, ALU.add,
                         ekf.keys(oc, p0, p1) + KL1C + fB2.keys(oc, -1, w - 1),
                         fE.keys(oc, 0, w), tag="den")
                self.stt(fV.ap(oc, 0, w), ekvf.ap(oc, p0, p1), euc(oc),
                         fA2.ap(oc, -1, w - 1), ALU.mult, ALU.add,
                         ekvf.keys(oc, p0, p1) + KL1C + fA2.keys(oc, -1, w - 1),
                         fV.keys(oc, 0, w), tag="num")
                self.act(fE.ap(oc, 0, w), fE.ap(oc, 0, w), AF.Ln,
                         fE.keys(oc, 0, w), fE.keys(oc, 0, w), tag="dln")
                self.act(fE.ap(oc, 0, w), fE.ap(oc, 0, w), AF.Exp,
                         fE.keys(oc, 0, w), fE.keys(oc, 0, w), scale=-1.0,
                         tag="dex")
                self.tt(fV.ap(oc, 0, w), fV.ap(oc, 0, w), fE.ap(oc, 0, w),
                        ALU.mult, fV.keys(oc, 0, w) + fE.keys(oc, 0, w),
                        fV.keys(oc, 0, w), tag="wkv")
                self.tt(srf.ap(oc, p0, p1), srf.ap(oc, p0, p1),
                        fV.ap(oc, 0, w), ALU.mult,
                        srf.keys(oc, p0, p1) + fV.keys(oc, 0, w),
                        srf.keys(oc, p0, p1), tag="srw")
        self.tap("srwkv", srf.full_ap(), [128, NCH * TO], srf.kall(),
                 dtype=BF16)
        self.tap("ekf", ekf.full_ap(), [128, NCH * TO], ekf.kall(),
                 dtype=BF16)
        self.bfree(ekf)
        self.bfree(ekvf)
        self.bfree(fE); self.bfree(fV); self.bfree(fA2); self.bfree(fB2)
        self.bfree(mixp)

        sL1 = self.abuf("l1s", NCH, TO, BF16)
        wo2 = self.wfull("wot", self.wot)
        for p0 in range(0, TO, PW1):
            p1 = min(p0 + PW1, TO)
            w = p1 - p0
            for oc in range(NCH):
                psn, ps = ("psA", self.psA) if oc % 2 == 0 else ("psB", self.psB)
                for kc in range(NCH):
                    self.mm(psn, ps[:, 0:w], wo2.ap(kc, oc * 128, oc * 128 + 128),
                            srf.ap(kc, p0, p1), kc == 0, kc == NCH - 1,
                            wo2.keys(kc, oc * 128, oc * 128 + 128),
                            srf.keys(kc, p0, p1), tag="l1omm")
                self.copy(sL1.ap(oc, p0, p1), ps[:, 0:w], [("ps", psn)],
                          sL1.keys(oc, p0, p1), eng="scalar", tag="evo")
        self.bfree(wo2)
        self.bfree(srf)

        self.tap("sL1", sL1.full_ap(), [128, NCH * TO], sL1.kall(),
                 dtype=BF16)
        self.l1_ffn(xnL1, sL1)

    def quantize_w(self, src_ap, n_rows, name, lnw=None, lnb=None,
                   colsum_rows=None):
        g = self.g
        cols = src_ap.shape[1]
        dst = self.dram("wq_" + name, [n_rows, cols], BF16)
        pan = self.abuf("qz_" + name, 1, 512)
        pan2 = self.abuf("qz2_" + name, 1, 512)
        panb = self.abuf("qzb_" + name, 1, 512, BF16)
        panb2 = self.abuf("qzb2_" + name, 1, 512, BF16)
        kp = pan.keys(0, 0, 512)
        kp2 = pan2.keys(0, 0, 512)
        kpb = panb.keys(0, 0, 512)
        kpb2 = panb2.keys(0, 0, 512)
        if not hasattr(self, "ones_colb"):
            self.ones_colb = self.sbuf("ones_colb", [128, 1], BF16)
            self.mones_colb = self.sbuf("mones_colb", [128, 1], BF16)
            self.memset(self.ones_colb[:], 1.0, [("sc", "ones_colb")])
            self.memset(self.mones_colb[:], -1.0, [("sc", "mones_col")])
        acc = self.sbuf("qacc_" + name, [128, 2])
        KA = [("sc", "qacc_" + name)]
        self.memset(acc[:, 0:1], 0.0, KA, tag="qz0")
        group = max(1, 512 // cols)
        colpan = max(1, cols // 512)
        npan = (n_rows // (128 * group)) * colpan
        assert group == 1 or (lnw is None and colsum_rows is None)

        def panape(p):
            r0 = (p // colpan) * 128 * group
            c0 = (p % colpan) * 512
            wq = min(512, cols - c0) if group == 1 else cols * group
            return r0, c0, wq

        def src_slice(r0, c0, wq):
            if group == 1:
                return src_ap[r0:r0 + 128, c0:c0 + wq]
            return src_ap[r0:r0 + 128 * group, :].rearrange(
                "(a p) c -> p a c", p=128)

        def dst_slice(r0, c0, wq):
            if group == 1:
                return dst[r0:r0 + 128, c0:c0 + wq]
            return dst[r0:r0 + 128 * group, :].rearrange(
                "(a p) c -> p a c", p=128)

        for p in range(npan):
            r0, c0, wq = panape(p)
            self.dma("sync", pan.ap(0, 0, wq), src_slice(r0, c0, wq),
                     [dkey(name)], kp, tag="qzl")
            self.act(pan2.ap(0, 0, wq), pan.ap(0, 0, wq), AF.Abs, kp, kp2,
                     tag="qza")
            self.red(pan2.ap(0, 0, 1), pan2.ap(0, 0, wq), ALU.add, kp2, kp2,
                     tag="qzr")
            self.tt(acc[:, 0:1], acc[:, 0:1], pan2.ap(0, 0, 1), ALU.add,
                    KA + kp2, KA, tag="qzacc")
        self.mm("psBC", self.psBC[0:1, 0:1], acc[:, 0:1], self.ones_col[:],
                True, True, KA, [("sc", "ones_col")], tag="qzs")
        self.copy(acc[0:1, 1:2], self.psBC[0:1, 0:1], [("ps", "psBC")], KA,
                  tag="qzc")
        self.mm("psBC", self.psBC[:, 0:1], self.ones128[0:1, :],
                acc[0:1, 1:2], True, True, [("sc", "ones128")], KA, tag="qzb")
        self.ts(acc[:, 1:2], self.psBC[:, 0:1],
                0.5 / (n_rows * cols), ALU.mult,
                [("ps", "psBC")], KA, s2=5e-7, op1=ALU.add, tag="qzg")
        for p in range(npan):
            r0, c0, wq = panape(p)
            self.dma("sync", pan.ap(0, 0, wq), src_slice(r0, c0, wq),
                     [dkey(name)], kp, tag="qzl2")
            self.act(pan2.ap(0, 0, wq), pan.ap(0, 0, wq), AF.Abs, kp, kp2,
                     tag="qza2")
            self.ts(pan2.ap(0, 0, wq), pan2.ap(0, 0, wq), acc[:, 1:2],
                    ALU.is_gt, kp2 + KA, kp2, tag="qzm")
            self.act(pan.ap(0, 0, wq), pan.ap(0, 0, wq), AF.Sign, kp, kp,
                     tag="qzs2")
            if lnw is None:
                self.tt(panb.ap(0, 0, wq), pan2.ap(0, 0, wq), pan.ap(0, 0, wq),
                        ALU.mult, kp2 + kp, kpb, tag="qzq")
            else:
                self.tt(pan.ap(0, 0, wq), pan2.ap(0, 0, wq), pan.ap(0, 0, wq),
                        ALU.mult, kp2 + kp, kp, tag="qzq1")
                self.ts(panb.ap(0, 0, wq), pan.ap(0, 0, wq),
                        lnw[:, (r0 // 128):(r0 // 128) + 1], ALU.mult,
                        kp + [("sc", "lnwc")], kpb, tag="qzq2")
            self.dma("scalar", dst_slice(r0, c0, wq), panb.ap(0, 0, wq),
                     kpb, [dkey("wq_" + name)], tag="qzw")
            if colsum_rows is not None:
                self.mm("psSC", self.psSC[0:1, c0:c0 + wq],
                        self.mones_colb[:], panb.ap(0, 0, wq),
                        r0 == 0, r0 == n_rows - 128,
                        [("sc", "mones_col")], kpb, tag="qzn")
                self.ts(panb2.ap(0, 0, wq), pan.ap(0, 0, wq),
                        lnb[:, (r0 // 128):(r0 // 128) + 1], ALU.mult,
                        kp + [("sc", "lnbc")], kpb2, tag="qzlb")
                psn2 = "psA" if c0 == 0 else "psB"
                ps2 = self.psA if c0 == 0 else self.psB
                self.mm(psn2, ps2[0:1, 0:wq], self.ones_colb[:],
                        panb2.ap(0, 0, wq), r0 == 0, r0 == n_rows - 128,
                        [("sc", "ones_colb")], kpb2, tag="qzcs")
        self.bfree(pan)
        self.bfree(pan2)
        self.bfree(panb)
        self.bfree(panb2)
        return dst

    def l1_ffn(self, xnL1, sL1):
        g = self.g
        TO = g.t_own
        if not hasattr(self, "mones_col"):
            self.mones_col = self.sbuf("mones_col", [128, 1])
            self.memset(self.mones_col[:], -1.0, [("sc", "mones_col")],
                        tag="mones")
        wqfc = self.quantize_w(self.wfctb, 32 * 128, "fc")
        negrow = self.psSC[0:1, 0:1024]
        lnbrow = self.psBC2[0:1, 0:512]
        wqpj = self.quantize_w(
            self.wprojT2, 4096, "pj", lnw=self.sb_lnwc, lnb=self.sb_lnbc,
            colsum_rows=True)
        csneg = self.sbuf("csneg", [128, NCH])
        cslnb = self.sbuf("cslnb", [128, NCH])
        KCS = [("sc", "csneg")]
        KCL = [("sc", "cslnb")]
        rowst = self.abuf("rowst", 1, 1024, BF16)
        self.copy(rowst.ap(0, 0, 1024, dp=0, dn=1), self.psSC[0:1, 0:1024],
                  [("ps", "psSC")], rowst.keys(0, 0, 1024), tag="csc1")
        dbounce = self.dram("csbounce", [2048], BF16)
        self.dma("gpsimd", dbounce[0:1024].rearrange("(o f) -> o f", o=1),
                 rowst.ap(0, 0, 1024, dp=0, dn=1), rowst.keys(0, 0, 1024),
                 [dkey("csbounce")], tag="csd1")
        self.copy(rowst.ap(0, 0, 512, dp=0, dn=1), self.psA[0:1, 0:512],
                  [("ps", "psA")], rowst.keys(0, 0, 1024), tag="csc2")
        self.copy(rowst.ap(0, 512, 1024, dp=0, dn=1), self.psB[0:1, 0:512],
                  [("ps", "psB")], rowst.keys(0, 0, 1024), tag="csc3")
        self.dma("gpsimd", dbounce[1024:2048].rearrange("(o f) -> o f", o=1),
                 rowst.ap(0, 0, 1024, dp=0, dn=1), rowst.keys(0, 0, 1024),
                 [dkey("csbounce")], tag="csd2")
        self.dma("gpsimd", csneg[:],
                 dbounce[0:1024].rearrange("(c p) -> p c", p=128),
                 [dkey("csbounce")], KCS, tag="csd3", slow_ok=True)
        self.dma("gpsimd", cslnb[:],
                 dbounce[1024:2048].rearrange("(c p) -> p c", p=128),
                 [dkey("csbounce")], KCL, tag="csd4", slow_ok=True)
        self.bfree(rowst)

        QT = TO // 4
        hbuf = self.abuf("l1h", 32, QT, BF16)
        wblk2 = [self.abuf(f"l1wb{k}", NCH, 128, BF16) for k in range(2)]
        wpb2 = [self.abuf(f"l1wp{k}", 1, 1024, BF16) for k in range(2)]
        sqp2 = self.abuf("l1hsq", 1, 512, BF16)
        lrow = self.sbuf("l1lrow", [1, 512])
        lrow2 = self.sbuf("l1lrow2", [1, 512])
        KLR = [("sc", "l1lrow")]
        KLR2 = [("sc", "l1lrow2")]
        gp = self.abuf("l1gp", 1, 512)
        g2p = self.abuf("l1g2", 1, 512, BF16)
        xop = self.abuf("l1xo", 1, 512)
        praw = self.abuf("l1praw", NCH, QT, BF16)
        for qt in range(4):
            q0 = qt * QT
            for b in range(32):
                wblk = wblk2[b % 2]
                self.dma("sync", wblk.full_ap(),
                         wqfc[b * 128:(b + 1) * 128, :],
                         [dkey("wq_fc")], wblk.kall(), tag="hwl")
                for sp in range(0, QT, 512):
                    s1 = min(sp + 512, QT)
                    w = s1 - sp
                    psn, ps = ("psA", self.psA) if b % 2 == 0 else ("psB", self.psB)
                    for kc in range(NCH):
                        self.mm(psn, ps[:, 0:w], wblk.ap(kc, 0, 128),
                                xnL1.ap(kc, 1 + q0 + sp, 1 + q0 + s1),
                                kc == 0, kc == NCH - 1,
                                wblk.keys(kc, 0, 128),
                                xnL1.keys(kc, 1 + q0 + sp, 1 + q0 + s1),
                                tag="hmm")
                    self.act(hbuf.ap(b, sp, s1), ps[:, 0:w], AF.Gelu,
                             [("ps", psn)], hbuf.keys(b, sp, s1), tag="hev")
            if qt == 0:
                self.tap("hbuf0", hbuf.full_ap(), [128, 32 * QT],
                         hbuf.kall(), dtype=BF16)
            for sp in range(0, QT, 512):
                s1 = min(sp + 512, QT)
                w = s1 - sp
                for b in range(32):
                    self.act(sqp2.ap(0, 0, w), hbuf.ap(b, sp, s1), AF.Square,
                             hbuf.keys(b, sp, s1), sqp2.keys(0, 0, w),
                             tag="hsq")
                    self.mm("psBC", self.psBC[:, 0:w], self.ones128_b[:],
                            hbuf.ap(b, sp, s1), b == 0, b == 31,
                            [("sc", "ones128_b")], hbuf.keys(b, sp, s1),
                            tag="hst1")
                    self.mm("psBC2", self.psBC2[:, 0:w], self.ones128_b[:],
                            sqp2.ap(0, 0, w), b == 0, b == 31,
                            [("sc", "ones128_b")], sqp2.keys(0, 0, w),
                            tag="hst2")
                DH4 = 4096.0
                self.act(lrow[0:1, sp:s1], self.psBC[0:1, 0:w],
                         AF.Square, [("ps", "psBC")], KLR, tag="hr1")
                self.stt(lrow[0:1, sp:s1], lrow[0:1, sp:s1], -1.0 / DH4,
                         self.psBC2[0:1, 0:w], ALU.mult, ALU.add,
                         KLR + [("ps", "psBC2")], KLR, tag="hr2")
                self.act(lrow[0:1, sp:s1], lrow[0:1, sp:s1], AF.Ln, KLR, KLR,
                         bias=1e-5, scale=1.0 / DH4, tag="hr3")
                self.act(lrow[0:1, sp:s1], lrow[0:1, sp:s1], AF.Exp, KLR, KLR,
                         scale=-0.5, tag="hr4")
                self.ts(lrow2[0:1, sp:s1], self.psBC[0:1, 0:w], 1.0 / DH4,
                        ALU.mult, [("ps", "psBC")], KLR2, tag="hr5")
                self.tt(lrow2[0:1, sp:s1], lrow2[0:1, sp:s1], lrow[0:1, sp:s1],
                        ALU.mult, KLR2 + KLR, KLR2, tag="hr6")
            for b in range(32):
                wpb = wpb2[b % 2]
                self.dma("sync",
                         wpb.ap(0, 0, 1024, dp=0, dn=128),
                         wqpj[b * 128:(b + 1) * 128, :],
                         [dkey("wq_pj")],
                         wpb.keys(0, 0, 1024), tag="pwl")
                for oc in range(NCH):
                    for sp in range(0, QT, 512):
                        s1 = min(sp + 512, QT)
                        bank = ["psA", "psB", "psC", "psD", "psBC", "psBC2",
                                "psSC", "psSC"][oc]
                        ps = {"psA": self.psA, "psB": self.psB, "psC": self.psC,
                              "psD": self.psD, "psBC": self.psBC,
                              "psBC2": self.psBC2, "psSC": self.psSC}[bank]
                        off = 512 if oc == 7 else 0
                        self.mm(bank, ps[:, off:off + (s1 - sp)],
                                wpb.ap(0, oc * 128, oc * 128 + 128, dp=0, dn=128),
                                hbuf.ap(b, sp, s1), b == 0, b == 31,
                                wpb.keys(0, oc * 128, oc * 128 + 128),
                                hbuf.keys(b, sp, s1), tag="pjm")
            for oc in range(NCH):
                for sp in range(0, QT, 512):
                    s1 = min(sp + 512, QT)
                    w = s1 - sp
                    bank = ["psA", "psB", "psC", "psD", "psBC", "psBC2",
                            "psSC", "psSC"][oc]
                    ps = {"psA": self.psA, "psB": self.psB, "psC": self.psC,
                          "psD": self.psD, "psBC": self.psBC,
                          "psBC2": self.psBC2, "psSC": self.psSC}[bank]
                    off = 512 if oc == 7 else 0
                    self.copy(praw.ap(oc, sp, s1), ps[:, off:off + w],
                              [("ps", bank)], praw.keys(oc, sp, s1),
                              eng="scalar", tag="prev")
            for sp in range(0, QT, 512):
                s1 = min(sp + 512, QT)
                w = s1 - sp
                self.mm("psBC", self.psBC[:, 0:w], self.ones128[0:1, :],
                        lrow[0:1, sp:s1], True, True, [("sc", "ones128")],
                        KLR, tag="bcr")
                self.mm("psBC2", self.psBC2[:, 0:w], self.ones128[0:1, :],
                        lrow2[0:1, sp:s1], True, True, [("sc", "ones128")],
                        KLR2, tag="bcm")
                for oc in range(NCH):
                    self.tt(gp.ap(0, 0, w), praw.ap(oc, sp, s1),
                            self.psBC[:, 0:w], ALU.mult,
                            praw.keys(oc, sp, s1) + [("ps", "psBC")],
                            gp.keys(0, 0, w), tag="fold1")
                    self.stt(gp.ap(0, 0, w), self.psBC2[:, 0:w],
                             csneg[:, oc:oc + 1], gp.ap(0, 0, w),
                             ALU.mult, ALU.add,
                             [("ps", "psBC2")] + KCS + gp.keys(0, 0, w),
                             gp.keys(0, 0, w), tag="fold2")
                    self.act(g2p.ap(0, 0, w), gp.ap(0, 0, w), AF.Silu,
                             gp.keys(0, 0, w) + KCL, g2p.keys(0, 0, w),
                             bias=cslnb[:, oc:oc + 1], tag="silu2")
                    self.dma("sync", xop.ap(0, 0, w),
                             self.x1mod_spill[oc * 128:(oc + 1) * 128,
                                              1 + q0 + sp:1 + q0 + s1],
                             [dkey("x1mod_spill", oc, (1 + q0 + sp) // 512),
                              dkey("x1mod_spill", oc, (q0 + s1) // 512)],
                             xop.keys(0, 0, w), tag="cbx")
                    self.tt(g2p.ap(0, 0, w), g2p.ap(0, 0, w),
                            sL1.ap(oc, q0 + sp, q0 + s1), ALU.mult,
                            g2p.keys(0, 0, w) + sL1.keys(oc, q0 + sp, q0 + s1),
                            g2p.keys(0, 0, w), tag="sg2")
                    self.tt(xop.ap(0, 0, w), xop.ap(0, 0, w), g2p.ap(0, 0, w),
                            ALU.add, xop.keys(0, 0, w) + g2p.keys(0, 0, w),
                            xop.keys(0, 0, w), tag="out")
                    self.dma("scalar", self.out_fm[oc * 128:(oc + 1) * 128,
                                                   q0 + sp:q0 + s1],
                             xop.ap(0, 0, w), xop.keys(0, 0, w),
                             [dkey("out_fm", oc, (q0 + sp) // 512)],
                             tag="outw")
        self.bfree(hbuf); self.bfree(sqp2)
        for b_ in wblk2 + wpb2:
            self.bfree(b_)
        self.bfree(gp); self.bfree(g2p); self.bfree(xop); self.bfree(praw)
        self.bfree(xnL1); self.bfree(sL1)

    def finalize(self, final_outputs):
        S = self.S
        final = [(k, S.counts[k]) for k in final_outputs if k in S.counts]
        sems = {}
        semlist = sorted(set(list(S.counts.keys())))
        with ExitStack() as es:
            for k in semlist:
                sems[k] = es.enter_context(self.nc.semaphore("sem_" + k))
            with self.nc.Block() as block:
                for eng in ENGINES:
                    def mk(e_name):
                        def emit_fn(eng_obj):
                            for rec in S.ops[e_name]:
                                for s, v in rec.waits:
                                    eng_obj.wait_ge(sems[s], v)
                                inst = rec.emit(eng_obj)
                                inst.then_inc(sems[rec.sem],
                                              16 if rec.kind == "dma" else 1)
                            for s, v in final:
                                eng_obj.wait_ge(sems[s], v)
                        return emit_fn
                    getattr(block, eng)(mk(eng))
        self.stack.close()
        return self.nc



def _build(geom, taps=()):
    P = Program(geom, taps)
    P.declare()
    P.phase_xn()
    P.phase_l0_attn()
    P.phase_l0_ffn()
    P.phase_combine()
    P.phase_l1()
    nc = P.finalize([k for k in P.S.counts if "dma" in k])
    return nc, P


def kernel(**inputs):
    g = GEOM_FULL
    in_maps = stage_inputs(inputs, g)
    nc, _ = _build(g)
    res = run_bass_kernel_spmd(nc, in_maps, core_ids=list(range(N_CORES)))
    return unstage_output(res.results, g)


# version 06
# speedup vs baseline: 1.3102x; 1.0261x over previous
import math
from contextlib import ExitStack

import numpy as np
import ml_dtypes

import concourse.bass as bass
import concourse.mybir as mybir
from concourse.bass_utils import run_bass_kernel_spmd

F32 = mybir.dt.float32
BF16 = mybir.dt.bfloat16
AF = mybir.ActivationFunctionType
ALU = mybir.AluOpType
AXL = mybir.AxisListType
BF16NP = ml_dtypes.bfloat16

N_CORES = 8
D = 1024
NCH = 8
NEG = -1.0e30


class Geom:
    def __init__(self, t_own=2048, pad_l=1280, pad_r=128):
        assert t_own % 256 == 0 and pad_l % 128 == 0 and pad_r % 128 == 0
        self.t_own = t_own
        self.pad_l = pad_l
        self.pad_r = pad_r
        self.t_buf = pad_l + t_own + pad_r
        self.seq = 4 * t_own
        self.n_tiles = self.t_buf // 128
        self.own_lo = pad_l
        self.own_hi = pad_l + t_own


GEOM_FULL = Geom()


ENGINES = ("tensor", "vector", "scalar", "gpsimd", "sync")


class _OpRec:
    __slots__ = ("engine", "kind", "sem", "val", "waits", "clock", "emit", "tag")

    def __init__(self, engine, kind, sem, val, waits, clock, emit, tag):
        self.engine = engine
        self.kind = kind
        self.sem = sem
        self.val = val
        self.waits = waits
        self.clock = clock
        self.emit = emit
        self.tag = tag


class Sched:

    DMA_ROT = 8

    def __init__(self):
        self.ops = {e: [] for e in ENGINES}
        self.counts = {}
        self.last_writer = {}
        self.readers = {}
        self.eng_clock = {e: {} for e in ENGINES}
        self.dma_rot = {}

    def _sem_of(self, engine, kind):
        if kind == "dma":
            j = self.dma_rot.get(engine, 0)
            self.dma_rot[engine] = j + 1
            return f"{engine}_dma{j % self.DMA_ROT}"
        if kind == "cc":
            return "cc"
        return engine

    def op(self, engine, emit, reads=(), writes=(), kind="compute",
           accum=False, tag=""):
        semkey = self._sem_of(engine, kind)
        inc = 16 if kind == "dma" else 1
        val = self.counts.get(semkey, 0) + inc
        self.counts[semkey] = val

        deps = []
        for r in reads:
            w = self.last_writer.get(r)
            if w is not None:
                deps.append(w)
        for w in writes:
            lw = self.last_writer.get(w)
            if lw is not None and not (accum and lw.engine == engine == "tensor"):
                deps.append(lw)
            deps.extend(self.readers.get(w, ()))

        clk = self.eng_clock[engine]
        wd = {}
        for d in deps:
            if clk.get(d.sem, 0) >= d.val:
                continue
            wd[d.sem] = max(wd.get(d.sem, 0), d.val)
            for k, v in d.clock.items():
                if clk.get(k, 0) < v:
                    clk[k] = v
            if clk.get(d.sem, 0) < d.val:
                clk[d.sem] = d.val
        if kind in ("dma", "cc") and val > inc and clk.get(semkey, 0) < val - inc:
            wd[semkey] = max(wd.get(semkey, 0), val - inc)
            clk[semkey] = max(clk.get(semkey, 0), val - inc)
        waits = sorted(wd.items())

        myclock = dict(clk)
        myclock[semkey] = val
        rec = _OpRec(engine, kind, semkey, val, waits, myclock, emit, tag)
        self.ops[engine].append(rec)

        for w in writes:
            self.last_writer[w] = rec
            self.readers[w] = []
        for r in reads:
            self.readers.setdefault(r, []).append(rec)
        return rec


class Arena:

    GRAIN = 512

    def __init__(self, t, name, n):
        self.t = t
        self.name = name
        self.n = n
        self.free_list = [(0, n)]
        self.hwm = 0

    def alloc(self, n, align=128, side="lo"):
        order = range(len(self.free_list))
        if side == "hi":
            order = reversed(list(order))
        for i in order:
            s, l = self.free_list[i]
            if side == "hi":
                a = (s + l - n) // align * align
                if a < s:
                    continue
            else:
                a = (s + align - 1) // align * align
                if a + n > s + l:
                    continue
            pre = (s, a - s)
            post = (a + n, s + l - (a + n))
            repl = [p for p in (pre, post) if p[1] > 0]
            self.free_list[i:i + 1] = repl
            self.hwm = max(self.hwm, a + n)
            return a
        used = self.n - sum(l for _, l in self.free_list)
        raise MemoryError(
            f"arena {self.name}: cannot alloc {n} (cap {self.n}, in use "
            f"{used}, free_list {self.free_list})")

    def free(self, off, n):
        self.free_list.append((off, n))
        self.free_list.sort()
        out = []
        for s, l in self.free_list:
            if out and out[-1][0] + out[-1][1] == s:
                out[-1] = (out[-1][0], out[-1][1] + l)
            else:
                out.append((s, l))
        self.free_list = [tuple(x) for x in out]


class Buf:

    def __init__(self, arena, base, nch, width, lead=0, name=""):
        self.arena = arena
        self.t = arena.t
        self.base = base
        self.nch = nch
        self.width = width
        self.lead = lead
        self.stride = width + lead
        self.name = name

    def ap(self, c, lo, hi, dp=0, dn=128):
        off = self.base + c * self.stride + self.lead
        return self.t[dp:dp + dn, off + lo: off + hi]

    def keys(self, c, lo, hi):
        g = Arena.GRAIN
        off = self.base + c * self.stride + self.lead
        return [(self.arena.name, i)
                for i in range((off + lo) // g, (off + hi + g - 1) // g)]

    def kall(self):
        g = Arena.GRAIN
        n = self.nch * self.stride
        return [(self.arena.name, i)
                for i in range(self.base // g, (self.base + n + g - 1) // g)]

    def full_ap(self):
        return self.t[:, self.base: self.base + self.nch * self.stride]


def dkey(name, *idx):
    return ("dram:" + name,) + idx



def _chunk_cols(vec, n):
    v = np.asarray(vec, np.float32).reshape(n, 128).T
    return np.ascontiguousarray(v)


def _t2(a):
    return np.ascontiguousarray(np.asarray(a, np.float32).T)


def stage_inputs(inputs, geom):
    g = geom
    x = np.asarray(inputs["L0_attn_x"], np.float32)
    Bb, T, _ = x.shape
    assert T == g.seq and Bb == 2

    shared = {}
    shared["wq1t"] = _t2(inputs["L0_attn_Wq1"])
    shared["wk1t"] = _t2(inputs["L0_attn_Wk1"])
    shared["wk2t"] = _t2(inputs["L0_attn_Wk2"])
    shared["wprojt"] = _t2(inputs["L0_attn_Wproj"])
    shared["bq1c"] = _chunk_cols(inputs["L0_attn_bq1"], NCH)
    shared["bk1c"] = _chunk_cols(inputs["L0_attn_bk1"], NCH)
    shared["bk2c"] = _chunk_cols(inputs["L0_attn_bk2"], NCH)
    shared["bprojc"] = _chunk_cols(inputs["L0_attn_bproj"], NCH)

    ident = np.eye(128, dtype=np.float32)
    shared["identf"] = ident
    shared["identb"] = ident.astype(BF16NP)
    r = np.arange(128)
    tri_first_s = np.where(r[:, None] > r[None, :], NEG, 0.0).astype(np.float32)
    tri_last_s = np.where(r[:, None] < r[None, :], NEG, 0.0).astype(np.float32)
    shared["tri_first_s"] = tri_first_s.astype(BF16NP)
    shared["tri_last_s"] = tri_last_s.astype(BF16NP)
    shared["tri_first_st"] = np.ascontiguousarray(tri_first_s.T).astype(BF16NP)
    shared["tri_last_st"] = np.ascontiguousarray(tri_last_s.T).astype(BF16NP)
    shared["pexprep"] = np.zeros((128, 8), np.float32)

    per_core = []
    for core in range(N_CORES):
        b, j = core // 4, core % 4
        a = j * g.t_own
        lo, hi = a - g.pad_l, a + g.t_own + g.pad_r
        xs = np.zeros((g.t_buf, D), np.float32)
        s_lo, s_hi = max(lo, 0), min(hi, T)
        xs[s_lo - lo: s_hi - lo] = x[b, s_lo:s_hi]
        m = {"xfm": np.ascontiguousarray(xs.T)}

        pm = np.zeros(8, np.float32)
        pexp = np.zeros(8, np.float32)
        for i in range(4):
            if i < j:
                pm[4 * b + i] = 1.0
                pexp[4 * b + i] = float(j - 1 - i)
        m["pmask16"] = np.ascontiguousarray(
            np.broadcast_to(np.repeat(pm, 16)[None, :], (128, 128))).astype(np.float32)
        m["pmask8"] = np.ascontiguousarray(
            np.broadcast_to(np.repeat(pm, 8)[None, :], (128, 64))).astype(np.float32)
        m["pmaskrep"] = np.ascontiguousarray(
            np.broadcast_to(pm[None, :], (128, 8))).astype(np.float32)
        m["pexprep"] = np.ascontiguousarray(
            np.broadcast_to(pexp[None, :], (128, 8))).astype(np.float32)
        m["edge0"] = np.full((128, 1), 0.0 if j == 0 else 1.0, np.float32)
        m["edge3"] = np.full((128, 1), 0.0 if j == 3 else 1.0, np.float32)

        dead = np.zeros((128, g.n_tiles), np.float32)
        for tile in range(g.n_tiles):
            tok0 = lo + tile * 128
            for p in range(128):
                tok = tok0 + p
                if tok < 0 or tok >= T:
                    dead[p, tile] = NEG
        m["deadcol"] = dead
        per_core.append(m)

    HEADS, DH, WIN = 8, 64, 512
    INNER = HEADS * DH
    HID = 2730
    HIDP = 2816
    wqkv = np.asarray(inputs["L0_ffn_wqkv"], np.float32)
    wo_ = np.asarray(inputs["L0_ffn_wo"], np.float32)
    w1_ = np.asarray(inputs["L0_ffn_w1"], np.float32)
    b1_ = np.asarray(inputs["L0_ffn_b1"], np.float32)
    w2_ = np.asarray(inputs["L0_ffn_w2"], np.float32)
    b2_ = np.asarray(inputs["L0_ffn_b2"], np.float32)
    qs_ = np.asarray(inputs["L0_ffn_q_scale"], np.float32)
    ks_ = np.asarray(inputs["L0_ffn_k_scale"], np.float32)
    lg_ = np.asarray(inputs["L0_ffn_ln_g"], np.float32)
    lb_ = np.asarray(inputs["L0_ffn_ln_b"], np.float32)
    for li in range(2):
        shared[f"wqkvqk{li}"] = np.ascontiguousarray(wqkv[li][:, :2 * INNER])
        shared[f"wqkvv{li}"] = np.ascontiguousarray(wqkv[li][:, 2 * INNER:])
        shared[f"wo{li}"] = np.ascontiguousarray(wo_[li])
        w1a = np.zeros((D, HIDP), np.float32)
        w1g = np.zeros((D, HIDP), np.float32)
        w1a[:, :HID] = w1_[li][:, :HID]
        w1g[:, :HID] = w1_[li][:, HID:]
        w1pb = np.zeros((22, D, 256), np.float32)
        for i in range(22):
            w1pb[i, :, :128] = w1a[:, i * 128:(i + 1) * 128]
            w1pb[i, :, 128:] = w1g[:, i * 128:(i + 1) * 128]
        shared[f"w1pb{li}"] = np.ascontiguousarray(
            w1pb.reshape(22, 8, 128, 256).transpose(0, 2, 1, 3)).reshape(
                22 * 128, 2048)
        b1a = np.zeros((HIDP,), np.float32)
        b1g = np.zeros((HIDP,), np.float32)
        b1a[:HID] = b1_[li][:HID]
        b1g[:HID] = b1_[li][HID:]
        shared[f"b1a{li}"] = _chunk_cols(b1a, 22)
        shared[f"b1g{li}"] = _chunk_cols(b1g, 22)
        w2p = np.zeros((HIDP, D), np.float32)
        w2p[:HID] = w2_[li]
        w2pcb = np.zeros((4, HIDP, 256), np.float32)
        for pr in range(4):
            w2pcb[pr] = w2p[:, pr * 256:(pr + 1) * 256]
        shared[f"w2pcb{li}"] = np.ascontiguousarray(
            w2pcb.reshape(4, 22, 128, 256).transpose(0, 2, 1, 3)).reshape(
                512, 5632)
        shared[f"b2c{li}"] = _chunk_cols(b2_[li], NCH)
        qsind = np.zeros((16, 4 * 128), np.float32)
        ksind = np.zeros((16, 4 * 128), np.float32)
        hind = np.zeros((16, 4 * 128), np.float32)
        for c in range(4):
            for p in range(128):
                hq = 2 * c + p // 64
                qsind[hq, c * 128 + p] = 8.0 * qs_[li][p % 64]
                ksind[8 + hq, c * 128 + p] = ks_[li][p % 64]
                hind[hq, c * 128 + p] = 1.0
        shared[f"qsind{li}"] = qsind.astype(BF16NP)
        shared[f"ksind{li}"] = ksind.astype(BF16NP)
        if li == 0:
            shared["hind16"] = hind.astype(BF16NP)
            osel = np.zeros((16, 8 * 128), np.float32)
            for h in range(HEADS):
                osel[h, h * 128:(h + 1) * 128] = 1.0
            shared["osel"] = osel.astype(BF16NP)
            hsel = np.zeros((128, 8 * 16), np.float32)
            for oc in range(8):
                for p in range(128):
                    hh = 2 * (oc % 4) + p // 64
                    col = hh if oc < 4 else 8 + hh
                    hsel[p, oc * 16 + col] = 1.0
            shared["hsel"] = hsel.astype(BF16NP)
        lgb = np.zeros((2, NCH * 128), np.float32)
        lgb[0] = lg_[li]
        lgb[1] = lb_[li]
        shared[f"lgb{li}"] = lgb.astype(BF16NP)
        shared[f"lgc{li}"] = _chunk_cols(lg_[li], NCH)

    shared["wkt"] = _t2(inputs["L1_attn_Wk"])
    shared["wvt"] = _t2(inputs["L1_attn_Wv"])
    shared["wrt"] = _t2(inputs["L1_attn_Wr"])
    shared["wot"] = _t2(inputs["L1_attn_Wo"])
    shared["sdc"] = _chunk_cols(inputs["L1_attn_spatial_decay"], NCH)
    shared["sfc"] = _chunk_cols(inputs["L1_attn_spatial_first"], NCH)
    shared["mixkc"] = _chunk_cols(inputs["L1_attn_mix_k"], NCH)
    shared["mixvc"] = _chunk_cols(inputs["L1_attn_mix_v"], NCH)
    shared["mixrc"] = _chunk_cols(inputs["L1_attn_mix_r"], NCH)
    wfcT = _t2(inputs["L1_ffn_W_fc"])
    shared["wfctb"] = np.ascontiguousarray(
        wfcT.reshape(8, 128, 32, 128).transpose(2, 1, 0, 3)).reshape(
            32 * 128, 1024)
    shared["wprojT2"] = _t2(inputs["L1_ffn_W_proj"])
    shared["lnwc"] = _chunk_cols(inputs["L1_ffn_ln_w"], 32)
    shared["lnbc"] = _chunk_cols(inputs["L1_ffn_ln_b"], 32)

    half = np.arange(0, DH, 2, np.float32)
    inv_freq = 1.0 / 10000.0 ** (half / DH)
    base = (half + 0.4 * DH) / (1.4 * DH)
    seq = g.seq
    for core, m in enumerate(per_core):
        b, j = core // 4, core % 4
        lo = j * g.t_own - g.pad_l
        t_abs = np.clip(np.arange(lo, lo + g.t_buf, dtype=np.float32), 0, seq - 1)
        freqs = t_abs[None, :] * inv_freq[:, None]
        power = (t_abs - seq // 2) / float(WIN // 2)
        scale = base[:, None] ** power[None, :]
        cos = np.cos(freqs)
        sin = np.sin(freqs)
        def expand(tab32, signed=False):
            t64 = np.concatenate([tab32, tab32], axis=0)
            if signed:
                t64 = np.concatenate([-tab32, tab32], axis=0)
            return np.concatenate([t64, t64], axis=0)
        m["rotq_cos"] = np.ascontiguousarray(expand(cos * scale)).astype(BF16NP)
        m["rotq_sin"] = np.ascontiguousarray(
            expand(sin * scale, signed=True)).astype(BF16NP)
        m["rotk_cos"] = np.ascontiguousarray(expand(cos / scale)).astype(BF16NP)
        m["rotk_sin"] = np.ascontiguousarray(
            expand(sin / scale, signed=True)).astype(BF16NP)

    in_maps = [dict(shared, **pc) for pc in per_core]
    return in_maps


def unstage_output(results, geom):
    g = geom
    out = np.empty((2, g.seq, D), np.float32)
    for core, r in enumerate(results):
        b, j = core // 4, core % 4
        out[b, j * g.t_own:(j + 1) * g.t_own] = r["out_fm"].T
    return out



class Program:
    def __init__(self, geom, taps=()):
        self.g = geom
        self.taps = set(taps)
        self.nc = bass.Bass("TRN2", target_bir_lowering=False, debug=False)
        self.S = Sched()
        self.stack = ExitStack()
        self.sbuf_bytes = 0
        self._tap_outputs = []

    def din(self, name, shape, dtype=F32):
        return self.nc.dram_tensor(name, list(shape), dtype,
                                   kind="ExternalInput").ap()

    def dout(self, name, shape, dtype=F32):
        return self.nc.dram_tensor(name, list(shape), dtype,
                                   kind="ExternalOutput").ap()

    def dram(self, name, shape, dtype=F32):
        return self.nc.dram_tensor(name, list(shape), dtype).ap()

    def sbuf(self, name, shape, dtype=F32):
        t = self.stack.enter_context(self.nc.sbuf_tensor(name, list(shape), dtype))
        self.sbuf_bytes += shape[1] * mybir.dt.size(dtype)
        return t

    def mk_arenas(self, f32_elems, bf16_elems):
        self.af32 = Arena(self.sbuf("af32", [128, f32_elems]), "af32", f32_elems)
        self.ab16 = Arena(self.sbuf("ab16", [128, bf16_elems], BF16), "ab16",
                          bf16_elems)

    def abuf(self, name, nch, width, dtype=F32, lead=0, side="lo"):
        ar = self.af32 if dtype == F32 else self.ab16
        base = ar.alloc(nch * (width + lead), side=side)
        return Buf(ar, base, nch, width, lead=lead, name=name)

    def bfree(self, buf):
        buf.arena.free(buf.base, buf.nch * buf.stride)

    def psum(self, name, cols=512):
        return self.stack.enter_context(
            self.nc.psum_tensor(name, [128, cols], F32))

    def dma(self, q, out_ap, in_ap, reads, writes, tag="", slow_ok=False):
        self.S.op(q, lambda e, o=out_ap, i=in_ap, sl=slow_ok:
                      e.dma_start(out=o, in_=i, allow_slow_non_contiguous=sl),
                  reads=reads, writes=writes, kind="dma", tag=tag)

    def mm_group(self, psname, ops, tag=""):
        banks = [c0 // 512 for (_, c0, _, _, _, _) in ops]
        first = {}
        last = {}
        for i, b in enumerate(banks):
            first.setdefault(b, i)
            last[b] = i
        for i, (out_ap, c0, lhsT, rhs, lkeys, rkeys) in enumerate(ops):
            b = banks[i]
            self.mm(psname, out_ap, lhsT, rhs, first[b] == i, last[b] == i,
                    lkeys, rkeys, tag=tag)

    def mm_groupb(self, entries, tag=""):
        banks = [(e[0], e[2] // 512) for e in entries]
        first = {}
        last = {}
        for i, b in enumerate(banks):
            first.setdefault(b, i)
            last[b] = i
        for i, (psn, out_ap, c0, lhsT, rhs, lk, rk) in enumerate(entries):
            b = banks[i]
            self.mm(psn, out_ap, lhsT, rhs, first[b] == i, last[b] == i,
                    lk, rk, tag=tag)

    def mm(self, psname, out_ap, lhsT, rhs, start, stop, lkeys, rkeys, tag=""):
        self.S.op("tensor",
                  lambda e, o=out_ap, l=lhsT, r=rhs, st=start, sp=stop:
                      e.matmul(o, l, r, start=st, stop=sp),
                  reads=list(lkeys) + list(rkeys), writes=[("ps", psname)],
                  accum=not start, tag=tag)

    def cbias(self, value):
        if not hasattr(self, "_cpool"):
            self._cpool = self.sbuf("cpool", [128, 32])
            self._cpool_map = {}
        if value not in self._cpool_map:
            i = len(self._cpool_map)
            assert i < 32
            self.memset(self._cpool[:, i:i + 1], value, [("sc", "cpool")],
                        tag="cpool")
            self._cpool_map[value] = i
        i = self._cpool_map[value]
        return self._cpool[:, i:i + 1]

    def act(self, out_ap, in_ap, func, rkeys, wkeys, bias=0.0, scale=1.0,
            eng="scalar", tag=""):
        if isinstance(bias, float) and bias != 0.0:
            bias = self.cbias(bias)[0:out_ap.shape[0], :]
            rkeys = list(rkeys) + [("sc", "cpool")]
        elif not isinstance(bias, float) and bias.shape[0] != out_ap.shape[0]:
            bias = bias[0:out_ap.shape[0], :]
        self.S.op(eng,
                  lambda e, o=out_ap, i=in_ap, f=func, b=bias, s=scale:
                      e.activation(o, i, f, bias=b, scale=s),
                  reads=rkeys, writes=wkeys, tag=tag)

    def tt(self, out_ap, a_ap, b_ap, op, rkeys, wkeys, eng="vector", tag=""):
        self.S.op(eng,
                  lambda e, o=out_ap, a=a_ap, b=b_ap, p=op:
                      e.tensor_tensor(o, a, b, p),
                  reads=rkeys, writes=wkeys, tag=tag)

    def ts(self, out_ap, in_ap, s1, op0, rkeys, wkeys, s2=None, op1=ALU.bypass,
           eng="vector", tag=""):
        self.S.op(eng,
                  lambda e, o=out_ap, i=in_ap, a=s1, b=s2, p0=op0, p1=op1:
                      e.tensor_scalar(o, i, a, b, p0, p1),
                  reads=rkeys, writes=wkeys, tag=tag)

    def stt(self, out_ap, in0_ap, scalar, in1_ap, op0, op1, rkeys, wkeys,
            eng="vector", tag=""):
        self.S.op(eng,
                  lambda e, o=out_ap, i0=in0_ap, s=scalar, i1=in1_ap, p0=op0,
                         p1=op1:
                      e.scalar_tensor_tensor(o, i0, s, i1, p0, p1),
                  reads=rkeys, writes=wkeys, tag=tag)

    def copy(self, out_ap, in_ap, rkeys, wkeys, eng="vector", tag=""):
        if eng == "scalar":
            self.S.op(eng, lambda e, o=out_ap, i=in_ap: e.copy(o, i),
                      reads=rkeys, writes=wkeys, tag=tag)
        else:
            self.S.op(eng, lambda e, o=out_ap, i=in_ap: e.tensor_copy(o, i),
                      reads=rkeys, writes=wkeys, tag=tag)

    def red(self, out_ap, in_ap, op, rkeys, wkeys, negate=False, eng="vector",
            tag=""):
        self.S.op(eng,
                  lambda e, o=out_ap, i=in_ap, p=op, n=negate:
                      e.tensor_reduce(o, i, AXL.X, p, negate=n),
                  reads=rkeys, writes=wkeys, tag=tag)

    def scan(self, out_ap, d0_ap, d1_ap, init, rkeys, wkeys, op0=ALU.mult,
             op1=ALU.add, tag=""):
        self.S.op("vector",
                  lambda e, o=out_ap, a=d0_ap, b=d1_ap, i=init, p0=op0, p1=op1:
                      e.tensor_tensor_scan(o, a, b, i, p0, p1),
                  reads=rkeys, writes=wkeys, tag=tag)

    def memset(self, ap, const, wkeys, eng="vector", tag=""):
        self.S.op(eng, lambda e, a=ap, c=const: e.memset(a, c),
                  reads=(), writes=wkeys, tag=tag)

    def recip(self, out_ap, in_ap, rkeys, wkeys, tag=""):
        self.S.op("vector", lambda e, o=out_ap, i=in_ap: e.reciprocal(o, i),
                  reads=rkeys, writes=wkeys, tag=tag)

    def transpose(self, out_psname, out_ap, in_ap, rkeys, tag=""):
        self.S.op("tensor",
                  lambda e, o=out_ap, i=in_ap, d=self.sb_identf[:]:
                      e.transpose(o, i, d),
                  reads=list(rkeys) + [("sc", "identf")],
                  writes=[("ps", out_psname)], tag=tag)

    def tap(self, name, ap, shape, rkeys, dtype=F32):
        if name not in self.taps:
            return
        out = self.dout("tap_" + name, list(shape), dtype)
        self.dma("gpsimd", out[:, :], ap, rkeys, [dkey("tap_" + name)],
                 tag="tap")

    def declare(self):
        g = self.g
        if g.t_own <= 512:
            self.mk_arenas(10240, 65536)
        else:
            self.mk_arenas(8448, 72192)
        self.xfm = self.din("xfm", [D, g.t_buf])
        self.wq1t = self.din("wq1t", [D, D])
        self.wk1t = self.din("wk1t", [D, D])
        self.wk2t = self.din("wk2t", [D, D])
        self.wprojt = self.din("wprojt", [D, D])
        self.sb_bias = {}
        for n in ("bq1c", "bk1c", "bk2c", "bprojc"):
            ap = self.din(n, [128, NCH])
            t = self.sbuf("sb_" + n, [128, NCH])
            self.sb_bias[n] = t
            self.dma("sync", t[:], ap[:, :], [dkey(n)], [("sc", n)])
        for n, sh in (("identf", [128, 128]), ("pmask16", [128, 128]),
                      ("pmask8", [128, 64]), ("pmaskrep", [128, 8]),
                      ("pexprep", [128, 8]), ("edge0", [128, 1]),
                      ("edge3", [128, 1]), ("deadcol", [128, g.n_tiles])):
            ap = self.din(n, sh)
            t = self.sbuf("sb_" + n, sh)
            setattr(self, "sb_" + n, t)
            self.dma("sync", t[:], ap[:, :], [dkey(n)], [("sc", n)])

        for li in range(2):
            for n, sh in ((f"wqkvqk{li}", [D, 1024]), (f"wqkvv{li}", [D, 512]),
                          (f"wo{li}", [512, D]), (f"w1pb{li}", [22 * 128, 2048]),
                          (f"w2pcb{li}", [4 * 128, 22 * 256]),
                          (f"lgb{li}", [2, NCH * 128]),
                          (f"lgc{li}", [128, NCH])):
                setattr(self, n, self.din(n, sh,
                                          BF16 if n.startswith("lgb") else F32))
            for n in (f"b1a{li}", f"b1g{li}"):
                ap = self.din(n, [128, 22])
                t = self.sbuf("sb_" + n, [128, 22])
                self.sb_bias[n] = t
                self.dma("sync", t[:], ap[:, :], [dkey(n)], [("sc", n)])
            ap = self.din(f"b2c{li}", [128, NCH])
            t = self.sbuf(f"sb_b2c{li}", [128, NCH])
            self.sb_bias[f"b2c{li}"] = t
            self.dma("sync", t[:], ap[:, :], [dkey(f"b2c{li}")],
                     [("sc", f"b2c{li}")])
            for n in (f"qsind{li}", f"ksind{li}"):
                ap = self.din(n, [16, 512], BF16)
                t = self.sbuf("sb_" + n, [16, 512], BF16)
                setattr(self, "sb_" + n, t)
                self.dma("sync", t[:], ap[:, :], [dkey(n)], [("sc", n)])
        for n, sh in (("hind16", [16, 512]), ("osel", [16, 1024]),
                      ("hsel", [128, 128]), ("tri_first_s", [128, 128]),
                      ("tri_last_s", [128, 128]), ("tri_first_st", [128, 128]),
                      ("tri_last_st", [128, 128]), ("identb", [128, 128])):
            ap = self.din(n, sh, BF16)
            t = self.sbuf("sb_" + n, sh, BF16)
            setattr(self, "sb_" + n, t)
            self.dma("sync", t[:], ap[:, :], [dkey(n)], [("sc", n)])
        for n in ("rotq_cos", "rotq_sin", "rotk_cos", "rotk_sin"):
            setattr(self, n, self.din(n, [128, g.t_buf], BF16))

        self.onesrow_b = self.sbuf("onesrow_b", [1, 128], BF16)
        self.onescol_b = self.sbuf("onescol_b", [128, 1], BF16)
        self.ones128_b = self.sbuf("ones128_b", [128, 128], BF16)
        self.memset(self.onesrow_b[:], 1.0, [("sc", "onesrow_b")])
        self.memset(self.onescol_b[:], 1.0, [("sc", "onescol_b")])
        self.memset(self.ones128_b[:], 1.0, [("sc", "ones128_b")])

        for n in ("wkt", "wvt", "wrt", "wot"):
            setattr(self, n, self.din(n, [D, D]))
        self.wfctb = self.din("wfctb", [32 * 128, 1024])
        self.wprojT2 = self.din("wprojT2", [4096, D])
        for n in ("sdc", "sfc", "mixkc", "mixvc", "mixrc"):
            ap = self.din(n, [128, NCH])
            t = self.sbuf("sb_" + n, [128, NCH])
            setattr(self, "sb_" + n, t)
            self.dma("sync", t[:], ap[:, :], [dkey(n)], [("sc", n)])
        for n in ("lnwc", "lnbc"):
            ap = self.din(n, [128, 32])
            t = self.sbuf("sb_" + n, [128, 32])
            setattr(self, "sb_" + n, t)
            self.dma("sync", t[:], ap[:, :], [dkey(n)], [("sc", n)])

        self.out_fm = self.dout("out_fm", [D, g.t_own])

        self.psA = self.psum("psA")
        self.psB = self.psum("psB")
        self.psC = self.psum("psC")
        self.psD = self.psum("psD")
        self.psBC = self.psum("psBC")
        self.psBC2 = self.psum("psBC2")
        self.psSC = self.psum("psSC", 1024)

        self.ones_col = self.sbuf("ones_col", [128, 1])
        self.ones128 = self.sbuf("ones128", [128, 128])
        self.sc_small = self.sbuf("sc_small", [128, 256])
        self.memset(self.ones_col[:], 1.0, [("sc", "ones_col")])
        self.memset(self.ones128[:], 1.0, [("sc", "ones128")])

    def wfull(self, name, dram_ap, rows=D):
        cols = dram_ap.shape[1]
        nk = rows // 128
        w = self.abuf("w_" + name, nk, cols, BF16)
        self.dma("gpsimd",
                 w.full_ap().rearrange("p (a c) -> p a c", a=nk),
                 dram_ap.rearrange("(a p) c -> p a c", p=128),
                 [dkey(name)], w.kall(), tag="wload_" + name)
        return w

    def mm_fm(self, w, rhs_buf, rhs_lo, rhs_hi, evac, rhs_shift=0,
              psnames=("psA", "psB"), pss=None, nk=None, out_chunks=None,
              tag=""):
        if pss is None:
            pss = (self.psA, self.psB)
        if nk is None:
            nk = w.nch
        if out_chunks is None:
            out_chunks = range(w.width // 128)
        npan = (rhs_hi - rhs_lo + 511) // 512
        i = 0
        for oc in out_chunks:
            for p in range(npan):
                lo = rhs_lo + 512 * p
                hi = min(lo + 512, rhs_hi)
                psn, ps = psnames[i % 2], pss[i % 2]
                i += 1
                for kc in range(nk):
                    self.mm(psn, ps[:, 0:hi - lo],
                            w.ap(kc, oc * 128, oc * 128 + 128),
                            rhs_buf.ap(kc, lo + rhs_shift, hi + rhs_shift),
                            kc == 0, kc == nk - 1,
                            w.keys(kc, oc * 128, oc * 128 + 128),
                            rhs_buf.keys(kc, lo + rhs_shift, hi + rhs_shift),
                            tag=tag + "mm")
                evac(oc, lo, hi, psn, ps)

    def phase_xn(self):
        g = self.g
        self.xn_spill = self.dram("xn_spill", [D, g.t_buf + 1], BF16)
        zc = self.sbuf("zcol", [128, 1], BF16)
        self.memset(zc[:], 0.0, [("sc", "zcol")], tag="zc")
        for c in range(NCH):
            self.dma("scalar", self.xn_spill[c * 128:(c + 1) * 128, 0:1],
                     zc[:], [("sc", "zcol")], [dkey("xn_spill", c, -1)],
                     tag="xnz", slow_ok=True)

        xpan = self.abuf("xpan", NCH, 256)
        xsq = self.abuf("xsq", NCH, 256)
        xnp = self.abuf("xnp", NCH, 256, BF16)
        rstd = self.sbuf("xn_rstd", [128, 512])

        n_pan = (g.t_buf + 255) // 256
        for p in range(n_pan):
            lo = p * 256
            hi = min(lo + 256, g.t_buf)
            w = hi - lo
            for c in range(NCH):
                self.dma("sync", xpan.ap(c, 0, w),
                         self.xfm[c * 128:(c + 1) * 128, lo:hi],
                         [dkey("xfm")], xpan.keys(c, 0, w), tag="xload")
                self.act(xsq.ap(c, 0, w), xpan.ap(c, 0, w), AF.Square,
                         xpan.keys(c, 0, w), xsq.keys(c, 0, w), tag="xsq")
            for c in range(NCH):
                self.mm("psBC", self.psBC[:, 0:w], self.ones128[:],
                        xpan.ap(c, 0, w), c == 0, c == NCH - 1,
                        [("sc", "ones128")], xpan.keys(c, 0, w), tag="xs1")
                self.mm("psBC2", self.psBC2[:, 0:w], self.ones128[:],
                        xsq.ap(c, 0, w), c == 0, c == NCH - 1,
                        [("sc", "ones128")], xsq.keys(c, 0, w), tag="xs2")
            self.act(rstd[:, 0:w], self.psBC[:, 0:w], AF.Square,
                     [("ps", "psBC")], [("sc", "xn_rstd")], tag="s2")
            self.stt(rstd[:, 0:w], rstd[:, 0:w], -1.0 / D, self.psBC2[:, 0:w],
                     ALU.mult, ALU.add, [("sc", "xn_rstd"), ("ps", "psBC2")],
                     [("sc", "xn_rstd")], tag="var")
            self.act(rstd[:, 0:w], rstd[:, 0:w], AF.Ln, [("sc", "xn_rstd")],
                     [("sc", "xn_rstd")], bias=1e-5, scale=1.0 / D, tag="ln")
            self.act(rstd[:, 0:w], rstd[:, 0:w], AF.Exp, [("sc", "xn_rstd")],
                     [("sc", "xn_rstd")], scale=-0.5, tag="exp")
            for c in range(NCH):
                self.stt(xsq.ap(c, 0, w), self.psBC[:, 0:w], -1.0 / D,
                         xpan.ap(c, 0, w), ALU.mult, ALU.add,
                         [("ps", "psBC")] + xpan.keys(c, 0, w),
                         xsq.keys(c, 0, w), tag="xcen")
                self.tt(xnp.ap(c, 0, w), xsq.ap(c, 0, w), rstd[:, 0:w],
                        ALU.mult, xsq.keys(c, 0, w) + [("sc", "xn_rstd")],
                        xnp.keys(c, 0, w), tag="xnorm")
                self.dma("scalar", self.xn_spill[c * 128:(c + 1) * 128,
                                                 1 + lo:1 + hi],
                         xnp.ap(c, 0, w), xnp.keys(c, 0, w),
                         [dkey("xn_spill", c, lo // 512)], tag="xnsp")
        self.bfree(xpan)
        self.bfree(xsq)
        self.bfree(xnp)

    def load_xn(self, buf, c, lo, hi, tag="xnld"):
        self.dma("sync", buf.ap(c, 0, hi - lo),
                 self.xn_spill[c * 128:(c + 1) * 128, 1 + lo:1 + hi],
                 [dkey("xn_spill", c, max(lo, 0) // 512),
                  dkey("xn_spill", c, max(hi - 1, 0) // 512),
                  dkey("xn_spill", c, -1)],
                 buf.keys(c, 0, hi - lo), tag=tag)

    def load_spill(self, buf, c, lo, hi, src_ap, name, off, tag="spld"):
        c0, c1 = off + lo, off + hi
        self.dma("sync", buf.ap(c, 0, hi - lo),
                 src_ap[c * 128:(c + 1) * 128, c0:c1],
                 [dkey(name, c, c0 // 512), dkey(name, c, (c1 - 1) // 512),
                  dkey(name, c, -1)],
                 buf.keys(c, 0, hi - lo), tag=tag)

    def allgather(self, idx, in_sbuf, ncols, rkeys):
        cin = self.dram(f"cc{idx}_in", [128, ncols])
        cout = self.dram(f"cc{idx}_out", [8 * 128, ncols])
        self.dma("gpsimd", cin[:, :], in_sbuf[:, 0:ncols], rkeys,
                 [dkey(f"cc{idx}_in")], tag=f"ag{idx}w")
        self.S.op("gpsimd",
                  lambda e, ci=cin, co=cout: e.collective_compute(
                      "AllGather", ALU.bypass,
                      replica_groups=[list(range(N_CORES))],
                      ins=[ci.opt()], outs=[co.opt()]),
                  reads=[dkey(f"cc{idx}_in")], writes=[dkey(f"cc{idx}_out")],
                  kind="cc", tag=f"ag{idx}")
        agg = self.sbuf(f"agg{idx}", [128, ncols * 8])
        for r in range(8):
            self.dma("gpsimd", agg[:, r * ncols:(r + 1) * ncols],
                     cout[r * 128:(r + 1) * 128, :],
                     [dkey(f"cc{idx}_out")], [("sc", f"agg{idx}")],
                     tag=f"ag{idx}r")
        return agg

    def carry_cols(self, agg, ncols, wname, weights=None):
        carry = self.sbuf(wname + "_c", [128, ncols])
        masked = self.sbuf(wname + "_m", [128, ncols * 8])
        kagg = ("sc", agg.name)
        km = ("sc", wname + "_m")
        if weights is None:
            pmsk = self.sb_pmask16 if ncols == 16 else self.sb_pmask8
            pk = "pmask16" if ncols == 16 else "pmask8"
            assert ncols in (8, 16)
            self.tt(masked[:], agg[:], pmsk[:, 0:ncols * 8],
                    ALU.mult, [kagg, ("sc", pk)], [km], tag=wname)
            self.red(carry[:].rearrange("p (one c) -> p c one", one=1),
                     masked[:].rearrange("p (r c) -> p c r", r=8), ALU.add,
                     [km], [("sc", wname + "_c")], tag=wname)
        else:
            w_ap, wk = weights
            cm = agg[:].rearrange("p (r c) -> p c r", r=8)
            w3 = w_ap.rearrange("p (c r) -> p c r", r=8)
            self.tt(masked[:, 0:64].rearrange("p (c r) -> p c r", r=8),
                    cm[:, 0:8, :], w3, ALU.mult, [kagg, wk], [km], tag=wname)
            self.tt(masked[:, 64:128].rearrange("p (c r) -> p c r", r=8),
                    cm[:, 8:16, :], w3, ALU.mult, [kagg, wk], [km], tag=wname)
            self.red(carry[:].rearrange("p (one c) -> p c one", one=1),
                     masked[:].rearrange("p (c r) -> p c r", r=8), ALU.add,
                     [km], [("sc", wname + "_c")], tag=wname)
        return carry

    def phase_l0_attn(self):
        g = self.g
        SC_LO, SC_HI = g.own_lo, g.t_buf
        NSC = SC_HI - SC_LO
        P2N = g.t_own + 2
        P3N = g.t_own + 1
        NOS = g.t_own + 2

        W2 = g.t_own + 2
        self.k_spill = self.dram("k_spill", [D, W2], BF16)
        K2f = self.abuf("l0_K2f", NCH, W2, BF16)
        Q2f = self.abuf("l0_Q2f", NCH, W2 + 2, BF16)
        self.s_spill = self.dram("s_spill", [D, NOS], BF16)

        PW = 256
        xnps = [self.abuf(f"l0_xnp{k}", NCH, PW + 1, BF16) for k in range(2)]
        xnp = xnps[0]
        fA = self.abuf("l0_fA", NCH, PW + 1)
        fB = self.abuf("l0_fB", NCH, PW + 1)
        fC = self.abuf("l0_fC", NCH, PW + 1)
        Qp = self.abuf("l0_Qp", NCH, PW, BF16)
        Kpans = [self.abuf(f"l0_Kp{k}", NCH, PW, BF16) for k in range(2)]
        Kpan = Kpans[0]
        OARp = self.abuf("l0_OARp", NCH, PW, BF16, lead=1)
        OMAp = self.abuf("l0_OMAp", NCH, PW, BF16)
        sp_pan = self.abuf("l0_sp", NCH, PW, BF16)

        chains = self.sbuf("l0_chains", [128, 64])
        kch = lambda c: chains[:, c:c + 1]
        kvch = lambda c: chains[:, 8 + c:9 + c]
        s2ch = lambda c: chains[:, 16 + c:17 + c]
        oarl = lambda c: chains[:, 24 + c:25 + c]
        runsum = chains[:, 32:48]
        ag2in = chains[:, 48:56]
        KCH = ("sc", "l0_chains")
        qb = self.sbuf("l0_qb", [128, 16])
        KQB = ("sc", "l0_qb")
        self.memset(runsum, 0.0, [KCH], tag="rs0")

        wk1 = self.wfull("wk1t", self.wk1t)
        wk2 = self.wfull("wk2t", self.wk2t)
        self.k2bias = self.sbuf("k2bias", [128, NCH])
        self.ts(self.k2bias[:], self.sb_bias["bk2c"][:], 0.000625, ALU.mult,
                [("sc", "bk2c")], [("sc", "k2bias")], tag="k2b")
        n_own_pan = (g.t_own + PW - 1) // PW
        n_pan1 = (W2 + PW - 1) // PW
        for p in range(n_pan1):
            lo = SC_LO + p * PW
            hi = min(lo + PW, SC_LO + W2)
            w = hi - lo
            xnp = xnps[p % 2]
            Kpan = Kpans[p % 2]
            for c in range(NCH):
                self.load_xn(xnp, c, lo - 1, hi)
            for oc in range(NCH):
                psn, ps = ("psA", self.psA) if oc % 2 == 0 else ("psB", self.psB)
                for kc in range(NCH):
                    self.mm(psn, ps[:, 0:w], wk1.ap(kc, oc * 128, oc * 128 + 128),
                            xnp.ap(kc, 1, 1 + w), kc == 0, kc == NCH - 1,
                            wk1.keys(kc, oc * 128, oc * 128 + 128),
                            xnp.keys(kc, 1, 1 + w), tag="k1mm")
                self.act(Kpan.ap(oc, 0, w), ps[:, 0:w], AF.Exp,
                         [("ps", psn), ("sc", "bk1c")],
                         Kpan.keys(oc, 0, w),
                         bias=self.sb_bias["bk1c"][:, oc:oc + 1], tag="evk")
                i0_ = lo - SC_LO
                self.dma("scalar", self.k_spill[oc * 128:(oc + 1) * 128,
                                                i0_:i0_ + w],
                         Kpan.ap(oc, 0, w), Kpan.keys(oc, 0, w),
                         [dkey("k_spill", oc, i0_ // 512)], tag="ksp",
                         slow_ok=True)
            for oc in range(NCH):
                psn, ps = ("psC", self.psC) if oc % 2 == 0 else ("psD", self.psD)
                for kc in range(NCH):
                    self.mm(psn, ps[:, 0:w], wk2.ap(kc, oc * 128, oc * 128 + 128),
                            xnp.ap(kc, 0, w), kc == 0, kc == NCH - 1,
                            wk2.keys(kc, oc * 128, oc * 128 + 128),
                            xnp.keys(kc, 0, w), tag="k2mm")
                self.act(K2f.ap(oc, lo - SC_LO, hi - SC_LO), ps[:, 0:w],
                         AF.Sigmoid, [("ps", psn), ("sc", "k2bias")],
                         K2f.keys(oc, lo - SC_LO, hi - SC_LO),
                         bias=self.k2bias[:, oc:oc + 1], scale=0.000625,
                         tag="evk2")
            if lo < g.own_hi:
                ws = min(hi, g.own_hi) - lo
                for c in range(NCH):
                    self.tt(fA.ap(c, 0, ws), Kpan.ap(c, 0, ws),
                            xnp.ap(c, 1, 1 + ws), ALU.mult,
                            Kpan.keys(c, 0, ws) +
                            xnp.keys(c, 1, 1 + ws), fA.keys(c, 0, ws), tag="kv1")
                    self.red(fB.ap(c, 0, 1), fA.ap(c, 0, ws), ALU.add,
                             fA.keys(c, 0, ws), fB.keys(c, 0, 1), tag="rkv")
                    self.red(fC.ap(c, 0, 1), Kpan.ap(c, 0, ws),
                             ALU.add, Kpan.keys(c, 0, ws),
                             fC.keys(c, 0, 1), tag="rk")
                    self.tt(runsum[:, c:c + 1], runsum[:, c:c + 1],
                            fC.ap(c, 0, 1), ALU.add, [KCH] + fC.keys(c, 0, 1),
                            [KCH], tag="accK")
                    self.tt(runsum[:, 8 + c:9 + c], runsum[:, 8 + c:9 + c],
                            fB.ap(c, 0, 1), ALU.add, [KCH] + fB.keys(c, 0, 1),
                            [KCH], tag="accKV")
        self.bfree(wk2)
        self.bfree(wk1)

        agg1 = self.allgather(1, runsum, 16, [KCH])
        car1 = self.carry_cols(agg1, 16, "car1")
        k_car1 = ("sc", "car1_c")

        wq1 = self.wfull("wq1t", self.wq1t)
        xnp = xnps[0]
        for c in range(NCH):
            self.load_xn(xnp, c, g.own_lo - 2, g.own_lo)
        psn, ps = "psA", self.psA
        for oc in range(NCH):
            for kc in range(NCH):
                self.mm(psn, ps[:, 0:2], wq1.ap(kc, oc * 128, oc * 128 + 128),
                        xnp.ap(kc, 0, 2), kc == 0, kc == NCH - 1,
                        wq1.keys(kc, oc * 128, oc * 128 + 128),
                        xnp.keys(kc, 0, 2), tag="qbmm")
            self.act(qb[:, 2 * oc:2 * oc + 2], ps[:, 0:2], AF.Sigmoid,
                     [("ps", psn), ("sc", "bq1c")], [KQB],
                     bias=self.sb_bias["bq1c"][:, oc:oc + 1], tag="qbev")
        sc = self.sc_small
        ksc = ("sc", "sc_small")
        for c in range(NCH):
            self.copy(kch(c), car1[:, c:c + 1], [k_car1], [KCH], tag="ci1")
            self.copy(kvch(c), car1[:, 8 + c:9 + c], [k_car1], [KCH], tag="ci2")
            self.memset(s2ch(c), 0.0, [KCH], tag="ci3")
            self.ts(sc[:, c:c + 1], car1[:, c:c + 1], 1e-6, ALU.add,
                    [k_car1], [ksc], tag="rcA")
            self.recip(sc[:, 8 + c:9 + c], sc[:, c:c + 1], [ksc], [ksc],
                       tag="rcB")
            self.tt(sc[:, 16 + c:17 + c], car1[:, 8 + c:9 + c],
                    sc[:, 8 + c:9 + c], ALU.mult, [k_car1, ksc], [ksc],
                    tag="rcC")
            self.tt(oarl(c), sc[:, 16 + c:17 + c], qb[:, 2 * c + 1:2 * c + 2],
                    ALU.mult, [ksc, KQB], [KCH], tag="rcD")
            self.ts(fA.ap(c, 0, 2), qb[:, 2 * c:2 * c + 2], 0.03125, ALU.mult,
                    [KQB], fA.keys(c, 0, 2), tag="q2ba")
            self.stt(fA.ap(c, 0, 2), qb[:, 2 * c:2 * c + 2], 0.000625,
                     fA.ap(c, 0, 2), ALU.mult, ALU.min, [KQB] + fA.keys(c, 0, 2),
                     fA.keys(c, 0, 2), tag="q2bb")
            self.ts(Q2f.ap(c, 0, 2), fA.ap(c, 0, 2), self.sb_edge0[:, 0:1],
                    ALU.mult, fA.keys(c, 0, 2) + [("sc", "edge0")],
                    Q2f.keys(c, 0, 2), tag="q2bz")

        wproj = self.wfull("wprojt", self.wprojt)
        onesb = self.ones_col
        n_pan2 = (P2N + PW - 1) // PW
        for p in range(n_pan2):
            lo = SC_LO + p * PW
            hi = min(lo + PW, SC_LO + P2N)
            w = hi - lo
            i0 = lo - SC_LO
            xnp = xnps[p % 2]
            Kpan = Kpans[p % 2]
            for c in range(NCH):
                self.load_xn(xnp, c, lo - 1, hi)
            for oc in range(NCH):
                psn, ps = ("psA", self.psA) if oc % 2 == 0 else ("psB", self.psB)
                for kc in range(NCH):
                    self.mm(psn, ps[:, 0:w], wq1.ap(kc, oc * 128, oc * 128 + 128),
                            xnp.ap(kc, 1, 1 + w), kc == 0, kc == NCH - 1,
                            wq1.keys(kc, oc * 128, oc * 128 + 128),
                            xnp.keys(kc, 1, 1 + w), tag="qmm")
                self.act(Qp.ap(oc, 0, w), ps[:, 0:w], AF.Sigmoid,
                         [("ps", psn), ("sc", "bq1c")], Qp.keys(oc, 0, w),
                         bias=self.sb_bias["bq1c"][:, oc:oc + 1], tag="evq")
            for c in range(NCH):
                self.dma("sync", Kpan.ap(c, 0, w),
                         self.k_spill[c * 128:(c + 1) * 128, i0:i0 + w],
                         [dkey("k_spill", c, i0 // 512),
                          dkey("k_spill", c, (i0 + w) // 512)],
                         Kpan.keys(c, 0, w), tag="kld", slow_ok=True)
                kKf = Kpan.keys(c, 0, w)
                self.scan(fA.ap(c, 0, w), onesb[:].broadcast_to([128, w]),
                          Kpan.ap(c, 0, w), kch(c),
                          kKf + [KCH, ("sc", "ones_col")], fA.keys(c, 0, w),
                          tag="scK")
                self.copy(kch(c), fA.ap(c, w - 1, w), fA.keys(c, w - 1, w),
                          [KCH], tag="upK")
                self.act(fA.ap(c, 0, w), fA.ap(c, 0, w), AF.Ln,
                         fA.keys(c, 0, w), fA.keys(c, 0, w), bias=1e-6,
                         tag="lnK")
                self.act(fA.ap(c, 0, w), fA.ap(c, 0, w), AF.Exp,
                         fA.keys(c, 0, w), fA.keys(c, 0, w), scale=-1.0,
                         tag="rcK")
                self.tt(fB.ap(c, 0, w), Kpan.ap(c, 0, w),
                        xnp.ap(c, 1, 1 + w), ALU.mult,
                        kKf + xnp.keys(c, 1, 1 + w), fB.keys(c, 0, w),
                        tag="kv")
                self.scan(fB.ap(c, 0, w), onesb[:].broadcast_to([128, w]),
                          fB.ap(c, 0, w), kvch(c),
                          fB.keys(c, 0, w) + [KCH, ("sc", "ones_col")],
                          fB.keys(c, 0, w), tag="scKV")
                self.copy(kvch(c), fB.ap(c, w - 1, w), fB.keys(c, w - 1, w),
                          [KCH], tag="upKV")
                self.tt(fB.ap(c, 0, w), fB.ap(c, 0, w), fA.ap(c, 0, w),
                        ALU.mult, fB.keys(c, 0, w) + fA.keys(c, 0, w),
                        fB.keys(c, 0, w), tag="S")
                self.copy(OARp.ap(c, -1, 0), oarl(c), [KCH],
                          OARp.keys(c, -1, 0), tag="oarl")
                self.tt(OARp.ap(c, 0, w), fB.ap(c, 0, w), Qp.ap(c, 0, w),
                        ALU.mult, fB.keys(c, 0, w) + Qp.keys(c, 0, w),
                        OARp.keys(c, 0, w), tag="oar")
                self.copy(oarl(c), OARp.ap(c, w - 1, w),
                          OARp.keys(c, w - 1, w), [KCH], tag="oarU")
                self.tt(fA.ap(c, 0, w), xnp.ap(c, 1, 1 + w),
                        OARp.ap(c, -1, w - 1), ALU.subtract,
                        xnp.keys(c, 1, 1 + w) + OARp.keys(c, -1, w - 1),
                        fA.keys(c, 0, w), tag="E")
                if p == 0:
                    self.ts(fA.ap(c, 0, 1), fA.ap(c, 0, 1),
                            self.sb_edge0[:, 0:1], ALU.mult,
                            fA.keys(c, 0, 1) + [("sc", "edge0")],
                            fA.keys(c, 0, 1), tag="E0")
                self.tt(fA.ap(c, 0, w), K2f.ap(c, i0, i0 + w), fA.ap(c, 0, w),
                        ALU.mult, K2f.keys(c, i0, i0 + w) + fA.keys(c, 0, w),
                        fA.keys(c, 0, w), tag="k2e")
                self.scan(fC.ap(c, 0, w), onesb[:].broadcast_to([128, w]),
                          fA.ap(c, 0, w), s2ch(c),
                          fA.keys(c, 0, w) + [KCH, ("sc", "ones_col")],
                          fC.keys(c, 0, w), tag="scS2")
                self.copy(s2ch(c), fC.ap(c, w - 1, w), fC.keys(c, w - 1, w),
                          [KCH], tag="upS2")
                self.copy(K2f.ap(c, i0, i0 + w), fC.ap(c, 0, w),
                          fC.keys(c, 0, w), K2f.keys(c, i0, i0 + w), tag="s2w")
                q2lo = 2 + i0
                q2hi = min(2 + i0 + w, W2 + 2)
                if q2lo < q2hi:
                    ww = q2hi - q2lo
                    self.ts(Q2f.ap(c, q2lo, q2hi), Qp.ap(c, 0, ww), 0.03125,
                            ALU.mult, Qp.keys(c, 0, ww), Q2f.keys(c, q2lo, q2hi),
                            tag="q2a")
                    self.stt(Q2f.ap(c, q2lo, q2hi), Qp.ap(c, 0, ww), 0.000625,
                             Q2f.ap(c, q2lo, q2hi), ALU.mult, ALU.min,
                             Qp.keys(c, 0, ww) + Q2f.keys(c, q2lo, q2hi),
                             Q2f.keys(c, q2lo, q2hi), tag="q2b")
            if hi == g.own_hi:
                for c in range(NCH):
                    self.copy(ag2in[:, c:c + 1], s2ch(c), [KCH], [KCH],
                              tag="ag2p")
            for oc in range(NCH):
                psn, ps = ("psC", self.psC) if oc % 2 == 0 else ("psD", self.psD)
                for kc in range(NCH):
                    self.mm(psn, ps[:, 0:w], wproj.ap(kc, oc * 128, oc * 128 + 128),
                            OARp.ap(kc, -1, w - 1), kc == 0, kc == NCH - 1,
                            wproj.keys(kc, oc * 128, oc * 128 + 128),
                            OARp.keys(kc, -1, w - 1), tag="pjmm")
                self.act(sp_pan.ap(oc, 0, w), ps[:, 0:w], AF.Identity,
                         [("ps", psn), ("sc", "bprojc")],
                         sp_pan.keys(oc, 0, w),
                         bias=self.sb_bias["bprojc"][:, oc:oc + 1], tag="evs")
                self.dma("scalar", self.s_spill[oc * 128:(oc + 1) * 128,
                                                i0:i0 + w],
                         sp_pan.ap(oc, 0, w), sp_pan.keys(oc, 0, w),
                         [dkey("s_spill", oc, i0 // 512)], tag="ssp", slow_ok=True)
        self.bfree(wq1)
        self.bfree(Qp)

        agg2 = self.allgather(2, ag2in, 8, [KCH])
        car2 = self.carry_cols(agg2, 8, "car2")
        k_car2 = ("sc", "car2_c")

        n_pan3 = (P3N + PW - 1) // PW
        for p in range(n_pan3):
            lo = SC_LO + p * PW
            hi = min(lo + PW, SC_LO + P3N)
            w = hi - lo
            i0 = lo - SC_LO
            for c in range(NCH):
                self.stt(OMAp.ap(c, 0, w), K2f.ap(c, i0, i0 + w),
                         car2[:, c:c + 1], Q2f.ap(c, i0 + 1, i0 + 1 + w),
                         ALU.add, ALU.mult,
                         K2f.keys(c, i0, i0 + w) + [k_car2] +
                         Q2f.keys(c, i0 + 1, i0 + 1 + w),
                         OMAp.keys(c, 0, w), tag="oma")
            for oc in range(NCH):
                psn, ps = ("psA", self.psA) if oc % 2 == 0 else ("psB", self.psB)
                for kc in range(NCH):
                    self.mm(psn, ps[:, 0:w],
                            wproj.ap(kc, oc * 128, oc * 128 + 128),
                            OMAp.ap(kc, 0, w),
                            kc == 0, kc == NCH - 1,
                            wproj.keys(kc, oc * 128, oc * 128 + 128),
                            OMAp.keys(kc, 0, w), tag="pj2mm")
                self.dma("scalar", sp_pan.ap(oc, 0, w),
                         self.s_spill[oc * 128:(oc + 1) * 128,
                                      i0 + 1:i0 + 1 + w],
                         [dkey("s_spill", oc, i0 // 512),
                          dkey("s_spill", oc, (i0 + w) // 512)],
                         sp_pan.keys(oc, 0, w), tag="sld", slow_ok=True)
                self.tt(sp_pan.ap(oc, 0, w), sp_pan.ap(oc, 0, w),
                        ps[:, 0:w], ALU.add,
                        sp_pan.keys(oc, 0, w) + [("ps", psn)],
                        sp_pan.keys(oc, 0, w), tag="sadd")
                self.dma("scalar", self.s_spill[oc * 128:(oc + 1) * 128,
                                                i0 + 1:i0 + 1 + w],
                         sp_pan.ap(oc, 0, w), sp_pan.keys(oc, 0, w),
                         [dkey("s_spill", oc, i0 // 512),
                          dkey("s_spill", oc, (i0 + w) // 512)], tag="sst", slow_ok=True)
        self.bfree(wproj)
        self.bfree(sp_pan)
        self.bfree(Kpans[0]); self.bfree(Kpans[1])
        self.bfree(K2f)
        self.bfree(Q2f)
        self.bfree(xnps[0]); self.bfree(xnps[1])
        self.bfree(fA)
        self.bfree(fB)
        self.bfree(fC)
        self.bfree(OARp)
        self.bfree(OMAp)

    def row_scatter(self, src_ap, n_rows, dst_rows, row, width, rkeys, wkeys,
                    scname, tag=""):
        d = self.dram(scname, [n_rows * width], BF16)
        self.dma("gpsimd", d.rearrange("(p f) -> p f", p=n_rows), src_ap,
                 rkeys, [dkey(scname)], tag=tag + "a")
        self.dma("gpsimd", dst_rows, d.rearrange("(o f) -> o f", o=1),
                 [dkey(scname)], wkeys, tag=tag + "b")

    def phase_l0_ffn(self):
        g = self.g
        xab0 = self.ffn_layer(0, None)
        QLO0 = g.own_lo - 768
        QN0 = (g.own_hi + 128) - QLO0
        self.xab0_spill = self.dram("xab0_spill", [D, QN0 + 1], BF16)
        for c in range(NCH):
            self.dma("scalar", self.xab0_spill[c * 128:(c + 1) * 128, :],
                     xab0.ap(c, -1, QN0), xab0.keys(c, -1, QN0),
                     [dkey("xab0_spill", c, -1)], tag="xab0sp")
        self.bfree(xab0)
        self.xab1 = self.ffn_layer(1, "spill")

    def ffn_layer(self, li, xin):
        g = self.g
        QLO = g.own_lo - (768 if li == 0 else 128)
        QLO_prev = g.own_lo - 768
        QHI = g.own_hi + 128
        KLO = QLO - 512
        QN = QHI - QLO
        KN = QHI - KLO
        n_qt = QN // 128
        n_kt = KN // 128
        L = str(li)

        wqk = self.wfull(f"wqkvqk{li}", getattr(self, f"wqkvqk{li}"))
        wv = self.wfull(f"wqkvv{li}", getattr(self, f"wqkvv{li}"))
        Vtok = self.abuf("vtok" + L, n_kt, 512, BF16)
        qhat = self.abuf("qhat" + L, 4, QN, BF16)
        khat = self.abuf("khat" + L, 4, KN, BF16)
        rotpan = self.abuf("frot" + L, 1, 512, BF16)
        tabc = self.abuf("ftabc" + L, 1, 512, BF16)
        tabs = self.abuf("ftabs" + L, 1, 512, BF16)
        f1 = self.abuf("ff1" + L, 1, 512, BF16)
        f2 = self.abuf("ff2" + L, 1, 512, BF16)
        rrow = self.abuf("frr" + L, 1, 512, BF16)
        KRS = rrow.keys(0, 0, 512)
        xpans = [self.abuf(f"fxpan{k}" + L, NCH, 513, BF16) for k in range(2)]
        sqpan = self.abuf("fsq" + L, NCH, 512, BF16)

        pan_lo = [0]
        pan_buf = [xpans[0]]
        nld = [0]

        def ld_panel(lo, hi):
            pan_lo[0] = lo
            pan_buf[0] = xpans[nld[0] % 2]
            nld[0] += 1
            xpan = pan_buf[0]
            for c in range(NCH):
                if xin is None:
                    self.load_xn(xpan, c, lo - 1, hi, tag="fxl")
                else:
                    self.load_spill(xpan, c, lo - 1, hi, self.xab0_spill,
                                    "xab0_spill", 1 - QLO_prev, tag="fxl")

        def xap(c, lo, hi):
            return pan_buf[0].ap(c, lo - (pan_lo[0] - 1), hi - (pan_lo[0] - 1))

        def xkeys(c, lo, hi):
            return pan_buf[0].keys(c, lo - (pan_lo[0] - 1),
                                   hi - (pan_lo[0] - 1))

        def qk_panel(oc, lo, hi):
            w = hi - lo
            side = 0 if oc < 4 else 1
            psn, ps = ("psA", self.psA) if oc % 2 == 0 else ("psB", self.psB)
            for kc in range(NCH):
                sh = 0 if kc < 4 else -1
                self.mm(psn, ps[:, 0:w], wqk.ap(kc, oc * 128, oc * 128 + 128),
                        xap(kc, lo + sh, hi + sh), kc == 0, kc == NCH - 1,
                        wqk.keys(kc, oc * 128, oc * 128 + 128),
                        xkeys(kc, lo + sh, hi + sh), tag="qkmm")
            hat = qhat if side == 0 else khat
            base = QLO if side == 0 else KLO
            c = oc % 4
            self.copy(hat.ap(c, lo - base, hi - base), ps[:, 0:w],
                      [("ps", psn)], hat.keys(c, lo - base, hi - base),
                      eng="scalar", tag="qkev")
            self.act(sqpan.ap(oc, 0, w), ps[:, 0:w], AF.Square,
                     [("ps", psn)], sqpan.keys(oc, 0, w), tag="qksq")

        def rot_chunk(oc, lo, hi):
            w = hi - lo
            side = 0 if oc < 4 else 1
            c = oc % 4
            hat = qhat if side == 0 else khat
            base = QLO if side == 0 else KLO
            i0 = lo - base
            sn = ("qsind" if side == 0 else "ksind") + L
            sind = getattr(self, "sb_" + sn)
            ctab = self.rotq_cos if side == 0 else self.rotk_cos
            stab = self.rotq_sin if side == 0 else self.rotk_sin
            ckey = "rotq_cos" if side == 0 else "rotk_cos"
            skey = "rotq_sin" if side == 0 else "rotk_sin"
            self.mm("psBC", self.psBC[:, 0:w], sind[:, c * 128:(c + 1) * 128],
                    rrow.ap(0, 0, w, dp=0, dn=16), True, True,
                    [("sc", sn)], KRS, tag="bcmm")
            self.tt(hat.ap(c, i0, i0 + w), hat.ap(c, i0, i0 + w),
                    self.psBC[:, 0:w], ALU.mult,
                    hat.keys(c, i0, i0 + w) + [("ps", "psBC")],
                    hat.keys(c, i0, i0 + w), tag="qn")
            for b0, b1 in ((0, 32), (32, 0), (64, 96), (96, 64)):
                self.copy(rotpan.ap(0, 0, w, dp=b0, dn=32),
                          hat.ap(c, i0, i0 + w, dp=b1, dn=32),
                          hat.keys(c, i0, i0 + w),
                          rotpan.keys(0, 0, w), eng="scalar", tag="rotc")
            self.dma("sync", tabc.ap(0, 0, w), ctab[:, lo:hi],
                     [dkey(ckey)], tabc.keys(0, 0, w), tag="tabl")
            self.dma("sync", tabs.ap(0, 0, w), stab[:, lo:hi],
                     [dkey(skey)], tabs.keys(0, 0, w), tag="tabl")
            self.tt(f1.ap(0, 0, w), hat.ap(c, i0, i0 + w), tabc.ap(0, 0, w),
                    ALU.mult, hat.keys(c, i0, i0 + w) + tabc.keys(0, 0, w),
                    f1.keys(0, 0, w), tag="rt1")
            self.tt(f2.ap(0, 0, w), rotpan.ap(0, 0, w), tabs.ap(0, 0, w),
                    ALU.mult, rotpan.keys(0, 0, w) + tabs.keys(0, 0, w),
                    f2.keys(0, 0, w), tag="rt2")
            self.tt(hat.ap(c, i0, i0 + w), f1.ap(0, 0, w), f2.ap(0, 0, w),
                    ALU.add, f1.keys(0, 0, w) + f2.keys(0, 0, w),
                    hat.keys(c, i0, i0 + w), tag="rt3")

        n_panA = (KN + 511) // 512
        for p in range(n_panA):
            lo = KLO + p * 512
            hi = min(lo + 512, QHI)
            w = hi - lo
            ld_panel(lo, hi)
            for tt_ in range(w // 128):
                kt = p * 4 + tt_
                psn, ps = ("psC", self.psC) if tt_ % 2 == 0 else ("psD", self.psD)
                t0 = lo + tt_ * 128
                for kc in range(NCH):
                    sh = 0 if kc < 4 else -1
                    self.mm(psn, ps[:, 0:512],
                            xap(kc, t0 + sh, t0 + sh + 128),
                            wv.ap(kc, 0, 512), kc == 0, kc == NCH - 1,
                            xkeys(kc, t0 + sh, t0 + sh + 128),
                            wv.keys(kc, 0, 512), tag="vmm")
                self.copy(Vtok.ap(kt, 0, 512), ps[:, 0:512], [("ps", psn)],
                          Vtok.keys(kt, 0, 512), eng="scalar", tag="vev")
            ocs = [4, 5, 6, 7] + ([0, 1, 2, 3] if lo >= QLO else [])
            for oc in ocs:
                qk_panel(oc, lo, hi)
            for i, oc in enumerate(ocs):
                self.mm("psBC2", self.psBC2[0:16, 0:w],
                        self.sb_hsel[:, oc * 16:(oc + 1) * 16],
                        sqpan.ap(oc, 0, w), i == 0, i == len(ocs) - 1,
                        [("sc", "hsel")], sqpan.keys(oc, 0, w), tag="ssq")
            self.act(rrow.ap(0, 0, w, dp=0, dn=16),
                     self.psBC2[0:16, 0:w], AF.Ln,
                     [("ps", "psBC2")], KRS, bias=1e-24, tag="lnq")
            self.act(rrow.ap(0, 0, w, dp=0, dn=16),
                     rrow.ap(0, 0, w, dp=0, dn=16), AF.Exp,
                     KRS, KRS, scale=-0.5, tag="rsq")
            for oc in ocs:
                rot_chunk(oc, lo, hi)
        self.bfree(wqk)
        self.bfree(wv)
        self.bfree(sqpan)
        self.bfree(rotpan); self.bfree(tabc); self.bfree(tabs)
        self.bfree(f1); self.bfree(f2); self.bfree(rrow)
        self.bfree(xpans[0]); self.bfree(xpans[1])

        self.tap(f"qhat{li}", qhat.full_ap(), [128, 4 * QN], qhat.kall(),
                 dtype=BF16)
        self.tap(f"khat{li}", khat.full_ap(), [128, 4 * KN], khat.kall(),
                 dtype=BF16)
        self.tap(f"vtok{li}", Vtok.full_ap(), [128, n_kt * 512], Vtok.kall(),
                 dtype=BF16)
        negmc = self.abuf("negmc" + L, 1, 256)
        sumsc = self.abuf("sumsc" + L, 1, 256)
        tstage = self.abuf("tstage" + L, 1, 128, BF16, side="hi")
        mrows = self.abuf("mrows" + L, 1, QN, BF16, side="hi")
        rrows = self.abuf("rrows" + L, 1, QN, BF16, side="hi")
        PT = self.abuf("pt" + L, 6, 640, BF16, side="hi")
        obuf = self.abuf("obuf" + L, 4, QN, BF16)
        for p0 in range(0, QN, 512):
            p1 = min(p0 + 512, QN)
            self.memset(mrows.ap(0, p0, p1, dp=0, dn=16), 0.0,
                        mrows.keys(0, p0, p1), tag="mrz")
            self.memset(rrows.ap(0, p0, p1, dp=0, dn=16), 0.0,
                        rrows.keys(0, p0, p1), tag="rrz")
        for sl in range(6):
            self.memset(PT.ap(sl, 0, 640), 0.0, PT.keys(sl, 0, 640),
                        tag="ptz")
        KNM = negmc.keys(0, 0, 256)
        KSM = sumsc.keys(0, 0, 256)
        KTS = tstage.keys(0, 0, 128)

        for h in range(8):
            ch, rb = h // 2, (h % 2) * 64
            for qt in range(n_qt):
                kq = qhat.keys(ch, qt * 128, qt * 128 + 128)

                def sseg(c0, c1):
                    if qt % 2 == 0:
                        return ("psSC", self.psSC[:, c0:c1], c0)
                    if c0 >= 512:
                        return ("psBC2", self.psBC2[:, c0 - 512:c1 - 512],
                                c0 - 512)
                    return ("psBC", self.psBC[:, c0:c1], c0)

                e1 = sseg(0, 512)
                e2 = sseg(512, 640)
                e3 = sseg(0, 128)
                e4 = sseg(512, 640)
                self.mm_groupb([
                    (e1[0], e1[1], e1[2],
                     qhat.ap(ch, qt * 128, qt * 128 + 128, dp=rb, dn=64),
                     khat.ap(ch, qt * 128, qt * 128 + 512, dp=rb, dn=64),
                     kq, khat.keys(ch, qt * 128, qt * 128 + 512)),
                    (e2[0], e2[1], e2[2],
                     qhat.ap(ch, qt * 128, qt * 128 + 128, dp=rb, dn=64),
                     khat.ap(ch, qt * 128 + 512, qt * 128 + 640, dp=rb, dn=64),
                     kq, khat.keys(ch, qt * 128 + 512, qt * 128 + 640)),
                    (e3[0], e3[1], e3[2], self.sb_identb[:],
                     self.sb_tri_first_s[:], [("sc", "identb")],
                     [("sc", "tri_first_s")]),
                    (e4[0], e4[1], e4[2], self.sb_identb[:],
                     self.sb_tri_last_s[:], [("sc", "identb")],
                     [("sc", "tri_last_s")]),
                ], tag="smm")
                if qt % 2 == 0:
                    self.red(negmc.ap(0, qt, qt + 1), self.psSC[:, 0:640],
                             ALU.max, [("ps", "psSC")], KNM, negate=True,
                             tag="smax")
                else:
                    sc = self.sc_small
                    ksc = ("sc", "sc_small")
                    self.red(sc[:, 32:33], self.psBC[:, 0:512],
                             ALU.max, [("ps", "psBC")], [ksc], negate=True,
                             tag="smax")
                    self.red(sc[:, 33:34], self.psBC2[:, 0:128],
                             ALU.max, [("ps", "psBC2")], [ksc], negate=True,
                             tag="smax")
                    self.tt(negmc.ap(0, qt, qt + 1), sc[:, 32:33],
                            sc[:, 33:34], ALU.min, [ksc], KNM, tag="smax")
            self.transpose("psD", self.psD[0:n_qt, 0:128],
                           negmc.ap(0, 0, n_qt), KNM, tag="mtr")
            self.copy(tstage.ap(0, 0, 128, dp=0, dn=n_qt),
                      self.psD[0:n_qt, 0:128], [("ps", "psD")], KTS,
                      eng="vector", tag="mcp")
            self.row_scatter(tstage.ap(0, 0, 128, dp=0, dn=n_qt), n_qt,
                             mrows.ap(0, 0, n_qt * 128, dp=h, dn=1), h,
                             128, KTS, mrows.keys(0, 0, QN),
                             f"mrow_sc{L}_{h}", tag="mrs")

            def do_pv(qt):
                pv_ops = []
                sum_ops = []
                for r in range(5):
                    ktr = qt + r
                    slot = ktr % 6
                    qtlo_r = max(0, ktr - 4)
                    off = (qt - qtlo_r) * 128
                    pv_ops.append((self.psA[0:64, 0:128], 0,
                                   Vtok.ap(ktr, h * 64, h * 64 + 64),
                                   PT.ap(slot, off, off + 128),
                                   Vtok.keys(ktr, h * 64, h * 64 + 64),
                                   PT.keys(slot, off, off + 128)))
                    sum_ops.append((self.psB[:, 0:1], 0,
                                    PT.ap(slot, off, off + 128),
                                    self.onescol_b[:],
                                    PT.keys(slot, off, off + 128),
                                    [("sc", "onescol_b")]))
                self.mm_group("psA", pv_ops, tag="pvmm")
                self.mm_group("psB", sum_ops, tag="summ")
                self.copy(obuf.ap(ch, qt * 128, qt * 128 + 128, dp=rb, dn=64),
                          self.psA[0:64, 0:128], [("ps", "psA")],
                          obuf.keys(ch, qt * 128, qt * 128 + 128),
                          eng="vector", tag="oev")
                self.copy(sumsc.ap(0, qt, qt + 1), self.psB[:, 0:1],
                          [("ps", "psB")], KSM, eng="vector", tag="sev")

            for kt in range(n_kt):
                qtlo = max(0, kt - 4)
                qthi = min(kt + 1, n_qt)
                w_st = (qthi - qtlo) * 128

                def tseg(c0, c1):
                    if kt % 2 == 0:
                        return ("psSC", self.psSC[:, c0:c1], c0)
                    if c0 >= 512:
                        return ("psBC2", self.psBC2[:, c0 - 512:c1 - 512],
                                c0 - 512)
                    return ("psBC", self.psBC[:, c0:c1], c0)

                ops = []
                kk = khat.keys(ch, kt * 128, kt * 128 + 128)
                for c0 in range(0, w_st, 512):
                    c1 = min(c0 + 512, w_st)
                    e = tseg(c0, c1)
                    ops.append((e[0], e[1], e[2],
                                khat.ap(ch, kt * 128, kt * 128 + 128,
                                        dp=rb, dn=64),
                                qhat.ap(ch, qtlo * 128 + c0, qtlo * 128 + c1,
                                        dp=rb, dn=64),
                                kk,
                                qhat.keys(ch, qtlo * 128 + c0,
                                          qtlo * 128 + c1)))
                for c0 in range(0, w_st, 512):
                    c1 = min(c0 + 512, w_st)
                    e = tseg(c0, c1)
                    ops.append((e[0], e[1], e[2],
                                self.sb_osel[:, h * 128:(h + 1) * 128],
                                mrows.ap(0, qtlo * 128 + c0, qtlo * 128 + c1,
                                         dp=0, dn=16),
                                [("sc", "osel")],
                                mrows.keys(0, qtlo * 128 + c0,
                                           qtlo * 128 + c1)))
                if kt < n_qt:
                    off = (kt - qtlo) * 128
                    e = tseg(off, off + 128)
                    ops.append((e[0], e[1], e[2],
                                self.sb_identb[:], self.sb_tri_first_st[:],
                                [("sc", "identb")], [("sc", "tri_first_st")]))
                if 0 <= kt - 4 < n_qt:
                    off = (kt - 4 - qtlo) * 128
                    e = tseg(off, off + 128)
                    ops.append((e[0], e[1], e[2],
                                self.sb_identb[:], self.sb_tri_last_st[:],
                                [("sc", "identb")], [("sc", "tri_last_st")]))
                self.mm_groupb(ops, tag="stmm")
                slot = kt % 6
                dcb = self.sb_deadcol[:, (KLO // 128) + kt:
                                      (KLO // 128) + kt + 1]
                if kt % 2 == 0:
                    self.act(PT.ap(slot, 0, w_st), self.psSC[:, 0:w_st],
                             AF.Exp, [("ps", "psSC"), ("sc", "deadcol")],
                             PT.keys(slot, 0, w_st), bias=dcb, tag="stexp")
                else:
                    wlo = min(w_st, 512)
                    self.act(PT.ap(slot, 0, wlo), self.psBC[:, 0:wlo],
                             AF.Exp, [("ps", "psBC"), ("sc", "deadcol")],
                             PT.keys(slot, 0, wlo), bias=dcb, tag="stexp")
                    if w_st > 512:
                        self.act(PT.ap(slot, 512, w_st),
                                 self.psBC2[:, 0:w_st - 512], AF.Exp,
                                 [("ps", "psBC2"), ("sc", "deadcol")],
                                 PT.keys(slot, 512, w_st), bias=dcb,
                                 tag="stexp")
                if 0 <= kt - 4 < n_qt:
                    do_pv(kt - 4)
            self.ts(sumsc.ap(0, 0, n_qt), sumsc.ap(0, 0, n_qt), 1e-30,
                    ALU.add, KSM, KSM, tag="seps")
            self.recip(sumsc.ap(0, 0, n_qt), sumsc.ap(0, 0, n_qt), KSM, KSM,
                       tag="srec")
            self.transpose("psD", self.psD[0:n_qt, 0:128],
                           sumsc.ap(0, 0, n_qt), KSM, tag="rtr")
            self.copy(tstage.ap(0, 0, 128, dp=0, dn=n_qt),
                      self.psD[0:n_qt, 0:128], [("ps", "psD")], KTS,
                      eng="vector", tag="rcp")
            self.row_scatter(tstage.ap(0, 0, 128, dp=0, dn=n_qt), n_qt,
                             rrows.ap(0, 0, n_qt * 128, dp=h, dn=1), h,
                             128, KTS, rrows.keys(0, 0, QN),
                             f"rrow_sc{L}_{h}", tag="rrs")
            if h % 2 == 1:
                for p0 in range(0, QN, 512):
                    p1 = min(p0 + 512, QN)
                    self.mm("psBC", self.psBC[:, 0:p1 - p0],
                            self.sb_hind16[:, ch * 128:(ch + 1) * 128],
                            rrows.ap(0, p0, p1, dp=0, dn=16), True, True,
                            [("sc", "hind16")], rrows.keys(0, p0, p1),
                            tag="obc")
                    self.tt(obuf.ap(ch, p0, p1), obuf.ap(ch, p0, p1),
                            self.psBC[:, 0:p1 - p0], ALU.mult,
                            obuf.keys(ch, p0, p1) + [("ps", "psBC")],
                            obuf.keys(ch, p0, p1), tag="osc")
        self.bfree(negmc); self.bfree(sumsc); self.bfree(tstage)
        self.bfree(mrows); self.bfree(rrows); self.bfree(PT)
        self.bfree(khat); self.bfree(qhat); self.bfree(Vtok)

        self.tap(f"obuf{li}", obuf.full_ap(), [128, 4 * QN], obuf.kall(),
                 dtype=BF16)
        self.tap(f"mrows{li}", mrows.full_ap(), [128, QN], mrows.kall(),
                 dtype=BF16)
        self.tap(f"rrows{li}", rrows.full_ap(), [128, QN], rrows.kall(),
                 dtype=BF16)
        xpans[0] = self.abuf(f"fxpan0b" + L, NCH, 513, BF16)
        xpans[1] = self.abuf(f"fxpan1b" + L, NCH, 513, BF16)
        wo = self.wfull(f"wo{li}", getattr(self, f"wo{li}"), rows=512)
        xab = self.abuf("xab" + L, NCH, QN, BF16, lead=1, side="hi")
        for c in range(NCH):
            self.memset(xab.ap(c, -1, 0), 0.0, xab.keys(c, -1, 0), tag="xabz")
        for p0 in range(0, QN, 512):
            p1 = min(p0 + 512, QN)
            w = p1 - p0
            ld_panel(QLO + p0, QLO + p1)
            for oc in range(NCH):
                psn, ps = ("psA", self.psA) if oc % 2 == 0 else ("psB", self.psB)
                for kc in range(4):
                    self.mm(psn, ps[:, 0:w], wo.ap(kc, oc * 128, oc * 128 + 128),
                            obuf.ap(kc, p0, p1), kc == 0, kc == 3,
                            wo.keys(kc, oc * 128, oc * 128 + 128),
                            obuf.keys(kc, p0, p1), tag="womm")
                self.tt(xab.ap(oc, p0, p1), xap(oc, QLO + p0, QLO + p1),
                        ps[:, 0:w], ALU.add,
                        xkeys(oc, QLO + p0, QLO + p1) + [("ps", psn)],
                        xab.keys(oc, p0, p1), tag="worid")
        self.bfree(wo)
        self.bfree(obuf)
        for b_ in xpans:
            self.bfree(b_)
        self.tap(f"xa{li}", xab.full_ap(), [128, NCH * (QN + 1)],
                 xab.kall(), dtype=BF16)
        self.ffn_mlp(li, xab, QN)
        if li == 0:
            bi = (g.own_lo - 1) - QLO
            for c in range(NCH):
                self.ts(xab.ap(c, bi, bi + 1), xab.ap(c, bi, bi + 1),
                        self.sb_edge0[:, 0:1], ALU.mult,
                        xab.keys(c, bi, bi + 1) + [("sc", "edge0")],
                        xab.keys(c, bi, bi + 1), tag="xabe0")
        self.tap(f"xab{li}", xab.full_ap(), [128, NCH * (QN + 1)],
                 xab.kall(), dtype=BF16)
        return xab

    def ffn_mlp(self, li, xab, FR):
        g = self.g
        L = str(li)
        assert FR % 4 == 0
        ql = FR // 4
        prodb = self.abuf("prod" + L, 22, ql, BF16)
        w1blk = [self.abuf(f"w1b{k}" + L, NCH, 256, BF16) for k in range(2)]
        w2pair = [self.abuf(f"w2p{k}" + L, 22, 256, BF16) for k in range(2)]
        fgg2 = [self.abuf(f"fgg{k}" + L, 1, 512, BF16) for k in range(2)]
        htil = self.abuf("htil" + L, NCH, ql, BF16)
        stash = self.sbuf(f"ffnstash{li}", [128, 8])
        KST = [("sc", f"ffnstash{li}")]
        lnrowA = self.abuf("lnA" + L, 1, FR, BF16)
        lnrowB = self.abuf("lnB" + L, 1, FR, BF16)
        sqp = self.abuf("msq" + L, NCH, 512, BF16)
        if not hasattr(self, "_mrow1"):
            self._mrow1 = self.sbuf("mrow1", [1, 512])
            self._mrow2 = self.sbuf("mrow2", [1, 512])
        r1, r2 = self._mrow1, self._mrow2
        KR1 = [("sc", "mrow1")]
        KR2 = [("sc", "mrow2")]
        lgb = self.sbuf(f"sb_lgb{li}", [2, NCH * 128], BF16)
        self.dma("sync", lgb[:], getattr(self, f"lgb{li}")[:, :],
                 [dkey(f"lgb{li}")], [("sc", f"lgb{li}")], tag="lgb")
        KLGB = [("sc", f"lgb{li}")]
        if not hasattr(self, "_onerow"):
            self._onerow = self.sbuf("onerow", [1, 512], BF16)
            self.memset(self._onerow[:], 1.0, [("sc", "onerow")], tag="or0")
        for p0 in range(0, FR, 512):
            p1 = min(p0 + 512, FR)
            self.dma("sync", lnrowB.ap(0, p0, p1, dp=1, dn=1),
                     self._onerow[:, 0:p1 - p0], [("sc", "onerow")],
                     lnrowB.keys(0, p0, p1), tag="onerow")
        lgc = self.sbuf(f"sb_lgc{li}", [128, NCH])
        self.dma("sync", lgc[:], getattr(self, f"lgc{li}")[:, :],
                 [dkey(f"lgc{li}")], [("sc", f"lgc{li}")], tag="lgc")

        def shx(c, p0, p1):
            sh = 0 if c < 4 else -1
            return xab.ap(c, p0 + sh, p1 + sh)

        def shxk(c, p0, p1):
            sh = 0 if c < 4 else -1
            return xab.keys(c, p0 + sh, p1 + sh)

        for p0 in range(0, FR, 512):
            p1 = min(p0 + 512, FR)
            w = p1 - p0
            for c in range(NCH):
                self.act(sqp.ap(c, 0, w), shx(c, p0, p1), AF.Square,
                         shxk(c, p0, p1), sqp.keys(c, 0, w),
                         tag="msq")
            for c in range(NCH):
                self.mm("psBC", self.psBC[:, 0:w], self.ones128_b[:],
                        shx(c, p0, p1), c == 0, c == NCH - 1,
                        [("sc", "ones128_b")], shxk(c, p0, p1),
                        tag="ms1")
                self.mm("psBC2", self.psBC2[:, 0:w], self.ones128_b[:],
                        sqp.ap(c, 0, w), c == 0, c == NCH - 1,
                        [("sc", "ones128_b")], sqp.keys(c, 0, w), tag="ms2")
            self.act(r1[:, 0:w], self.psBC[0:1, 0:w], AF.Square,
                     [("ps", "psBC")], KR1, tag="mr1")
            self.stt(r1[:, 0:w], r1[:, 0:w], -1.0 / D, self.psBC2[0:1, 0:w],
                     ALU.mult, ALU.add, KR1 + [("ps", "psBC2")], KR1,
                     tag="mr2")
            self.act(r1[:, 0:w], r1[:, 0:w], AF.Ln, KR1, KR1, bias=1e-5,
                     scale=1.0 / D, tag="mr3")
            self.act(lnrowA.ap(0, p0, p1, dp=0, dn=1), r1[:, 0:w], AF.Exp,
                     KR1, lnrowA.keys(0, p0, p1), scale=-0.5, tag="mr4")
            self.ts(r2[:, 0:w], self.psBC[0:1, 0:w], -1.0 / D, ALU.mult,
                    [("ps", "psBC")], KR2, tag="mr5")
            self.tt(lnrowB.ap(0, p0, p1, dp=0, dn=1), r2[:, 0:w],
                    lnrowA.ap(0, p0, p1, dp=0, dn=1), ALU.mult,
                    KR2 + lnrowA.keys(0, p0, p1), lnrowB.keys(0, p0, p1),
                    tag="mr6")
        self.bfree(sqp)

        w1d = getattr(self, f"w1pb{li}")
        w2d = getattr(self, f"w2pcb{li}")
        w1cast = self.dram(f"w1cast{li}", [22 * 128, 2048], BF16)
        w2cast = self.dram(f"w2cast{li}", [4 * 128, 22 * 256], BF16)
        for qf in range(4):
            H0 = qf * ql
            if qf < 3:
                for c in range(4, NCH):
                    sg = (qf % 2) * 4 + (c - 4)
                    self.copy(stash[:, sg:sg + 1],
                              xab.ap(c, H0 + ql - 1, H0 + ql),
                              xab.keys(c, H0 + ql - 1, H0 + ql), KST,
                              tag="stsh")
            for p0 in range(H0, H0 + ql, 512):
                p1 = min(p0 + 512, H0 + ql)
                w = p1 - p0
                self.mm("psBC", self.psBC[:, 0:w], self.onesrow_b[:],
                        lnrowA.ap(0, p0, p1, dp=0, dn=1), True, True,
                        [("sc", "onesrow_b")], lnrowA.keys(0, p0, p1),
                        tag="hbc1")
                for c in range(NCH):
                    self.tt(htil.ap(c, p0 - H0, p1 - H0),
                            shx(c, p0, p1), self.psBC[:, 0:w],
                            ALU.mult,
                            shxk(c, p0, p1) + [("ps", "psBC")],
                            htil.keys(c, p0 - H0, p1 - H0), tag="ht1")
                    if qf > 0 and p0 == H0 and c >= 4:
                        sg = ((qf - 1) % 2) * 4 + (c - 4)
                        self.tt(htil.ap(c, 0, 1), stash[:, sg:sg + 1],
                                self.psBC[:, 0:1], ALU.mult,
                                KST + [("ps", "psBC")],
                                htil.keys(c, 0, 1), tag="ht1s")
                for c in range(NCH):
                    self.mm("psBC2", self.psBC2[:, 0:w],
                            lgb[0:2, c * 128:(c + 1) * 128],
                            lnrowB.ap(0, p0, p1, dp=0, dn=2), True, True,
                            KLGB, lnrowB.keys(0, p0, p1), tag="hbc2")
                    self.stt(htil.ap(c, p0 - H0, p1 - H0),
                             htil.ap(c, p0 - H0, p1 - H0), lgc[:, c:c + 1],
                             self.psBC2[:, 0:w], ALU.mult, ALU.add,
                             htil.keys(c, p0 - H0, p1 - H0) +
                             [("sc", f"lgc{li}"), ("ps", "psBC2")],
                             htil.keys(c, p0 - H0, p1 - H0), tag="ht2")
            for i in range(22):
                wb = w1blk[i % 2]
                fgg = fgg2[i % 2]
                if qf == 0:
                    self.dma("gpsimd", wb.full_ap(),
                             w1d[i * 128:(i + 1) * 128, :],
                             [dkey(f"w1pb{li}")], wb.kall(), tag="w1l")
                    self.dma("scalar", w1cast[i * 128:(i + 1) * 128, :],
                             wb.full_ap(), wb.kall(),
                             [dkey(f"w1cast{li}", 0, i)], tag="w1sv")
                else:
                    self.dma("sync" if i % 2 == 0 else "scalar",
                             wb.full_ap(), w1cast[i * 128:(i + 1) * 128, :],
                             [dkey(f"w1cast{li}", 0, i)], wb.kall(),
                             tag="w1l")
                psan, psa = ("psA", self.psA) if i % 2 == 0 else \
                    ("psC", self.psC)
                psgn, psg = ("psB", self.psB) if i % 2 == 0 else \
                    ("psD", self.psD)
                for p0 in range(0, ql, 512):
                    p1 = min(p0 + 512, ql)
                    w = p1 - p0
                    for kc in range(NCH):
                        self.mm(psan, psa[:, 0:w], wb.ap(kc, 0, 128),
                                htil.ap(kc, p0, p1), kc == 0, kc == NCH - 1,
                                wb.keys(kc, 0, 128), htil.keys(kc, p0, p1),
                                tag="w1a")
                        self.mm(psgn, psg[:, 0:w], wb.ap(kc, 128, 256),
                                htil.ap(kc, p0, p1), kc == 0, kc == NCH - 1,
                                wb.keys(kc, 128, 256), htil.keys(kc, p0, p1),
                                tag="w1g")
                    self.act(fgg.ap(0, 0, w), psg[:, 0:w], AF.Gelu,
                             [("ps", psgn), ("sc", f"b1g{li}")],
                             fgg.keys(0, 0, w),
                             bias=self.sb_bias[f"b1g{li}"][:, i:i + 1],
                             tag="gev")
                    self.stt(prodb.ap(i, p0, p1), psa[:, 0:w],
                             self.sb_bias[f"b1a{li}"][:, i:i + 1],
                             fgg.ap(0, 0, w), ALU.add, ALU.mult,
                             [("ps", psan), ("sc", f"b1a{li}")] +
                             fgg.keys(0, 0, w),
                             prodb.keys(i, p0, p1), tag="prod")
            npan = (ql + 511) // 512
            assert npan <= 3
            bankset = [("psA", self.psA), ("psB", self.psB),
                       ("psC", self.psC), ("psD", self.psD),
                       ("psBC", self.psBC), ("psBC2", self.psBC2)]
            for pr in range(4):
                wp = w2pair[pr % 2]
                if qf == 0:
                    self.dma("gpsimd", wp.full_ap(),
                             w2d[pr * 128:(pr + 1) * 128, :],
                             [dkey(f"w2pcb{li}")],
                             wp.kall(), tag="w2l")
                    self.dma("scalar", w2cast[pr * 128:(pr + 1) * 128, :],
                             wp.full_ap(), wp.kall(),
                             [dkey(f"w2cast{li}", 0, pr)], tag="w2sv")
                else:
                    self.dma("sync" if pr % 2 == 0 else "scalar",
                             wp.full_ap(), w2cast[pr * 128:(pr + 1) * 128, :],
                             [dkey(f"w2cast{li}", 0, pr)], wp.kall(),
                             tag="w2l")
                for i in range(22):
                    for o2 in range(2):
                        for sp in range(npan):
                            p0 = sp * 512
                            p1 = min(p0 + 512, ql)
                            psn, ps = bankset[o2 * npan + sp]
                            self.mm(psn, ps[:, 0:p1 - p0],
                                    wp.ap(i, o2 * 128, o2 * 128 + 128),
                                    prodb.ap(i, p0, p1), i == 0, i == 21,
                                    wp.keys(i, o2 * 128, o2 * 128 + 128),
                                    prodb.keys(i, p0, p1), tag="w2mm")
                for o2 in range(2):
                    oc = pr * 2 + o2
                    for sp in range(npan):
                        p0 = sp * 512
                        p1 = min(p0 + 512, ql)
                        psn, ps = bankset[o2 * npan + sp]
                        self.stt(xab.ap(oc, H0 + p0, H0 + p1),
                                 ps[:, 0:p1 - p0],
                                 self.sb_bias[f"b2c{li}"][:, oc:oc + 1],
                                 xab.ap(oc, H0 + p0, H0 + p1),
                                 ALU.add, ALU.add,
                                 [("ps", psn), ("sc", f"b2c{li}")] +
                                 xab.keys(oc, H0 + p0, H0 + p1),
                                 xab.keys(oc, H0 + p0, H0 + p1), tag="w2ev")
        self.bfree(lnrowA); self.bfree(lnrowB); self.bfree(htil)
        self.bfree(prodb)
        for b in w1blk + w2pair:
            self.bfree(b)
        self.bfree(fgg2[0]); self.bfree(fgg2[1])

    def phase_combine(self):
        g = self.g
        NOS = g.t_own + 2
        self.x1mod_spill = self.dram("x1mod_spill", [D, NOS], F32)
        self.xnL1 = self.abuf("xnL1", NCH, NOS, BF16)
        xnL1 = self.xnL1
        QLO1 = g.own_lo - 128
        gpan = self.abuf("cgp", NCH, 256, BF16)
        xp = self.abuf("cxp", NCH, 256)
        sp = self.abuf("csp", NCH, 256, BF16)
        ops2 = [self.abuf(f"cop{k}", NCH, 256) for k in range(2)]
        xsq = self.abuf("cxq", NCH, 256)
        rstd = self.sbuf("l1_rstd", [128, 256])
        KRS1 = [("sc", "l1_rstd")]
        for ip, p0 in enumerate(range(0, NOS, 256)):
            p1 = min(p0 + 256, NOS)
            w = p1 - p0
            op = ops2[ip % 2]
            t0 = g.own_lo - 1 + p0
            for c in range(NCH):
                self.act(gpan.ap(c, 0, w), self.xab1.ap(c, t0 - QLO1, t0 - QLO1 + w),
                         AF.Silu, self.xab1.keys(c, t0 - QLO1, t0 - QLO1 + w),
                         gpan.keys(c, 0, w), tag="silu")
                self.dma("sync", xp.ap(c, 0, w),
                         self.xfm[c * 128:(c + 1) * 128, t0:t0 + w],
                         [dkey("xfm")], xp.keys(c, 0, w), tag="cx")
                self.dma("scalar", sp.ap(c, 0, w),
                         self.s_spill[c * 128:(c + 1) * 128, p0:p1],
                         [dkey("s_spill", c)], sp.keys(c, 0, w), tag="cs")
                self.tt(sp.ap(c, 0, w), sp.ap(c, 0, w), gpan.ap(c, 0, w),
                        ALU.mult, sp.keys(c, 0, w) + gpan.keys(c, 0, w),
                        sp.keys(c, 0, w), tag="sg")
                self.tt(op.ap(c, 0, w), xp.ap(c, 0, w), sp.ap(c, 0, w),
                        ALU.add, xp.keys(c, 0, w) + sp.keys(c, 0, w),
                        op.keys(c, 0, w), tag="x1m")
                self.dma("scalar", self.x1mod_spill[c * 128:(c + 1) * 128, p0:p1],
                         op.ap(c, 0, w), op.keys(c, 0, w),
                         [dkey("x1mod_spill", c, p0 // 512)], tag="x1sp")
            for c in range(NCH):
                self.act(xsq.ap(c, 0, w), op.ap(c, 0, w), AF.Square,
                         op.keys(c, 0, w), xsq.keys(c, 0, w), tag="l1sq")
            for c in range(NCH):
                self.mm("psBC", self.psBC[:, 0:w], self.ones128[:],
                        op.ap(c, 0, w), c == 0, c == NCH - 1,
                        [("sc", "ones128")], op.keys(c, 0, w), tag="l1s1")
                self.mm("psBC2", self.psBC2[:, 0:w], self.ones128[:],
                        xsq.ap(c, 0, w), c == 0, c == NCH - 1,
                        [("sc", "ones128")], xsq.keys(c, 0, w), tag="l1s2")
            self.act(rstd[:, 0:w], self.psBC[:, 0:w], AF.Square,
                     [("ps", "psBC")], KRS1, tag="l1v1")
            self.stt(rstd[:, 0:w], rstd[:, 0:w], -1.0 / D, self.psBC2[:, 0:w],
                     ALU.mult, ALU.add, KRS1 + [("ps", "psBC2")], KRS1,
                     tag="l1v2")
            self.act(rstd[:, 0:w], rstd[:, 0:w], AF.Ln, KRS1, KRS1,
                     bias=1e-5, scale=1.0 / D, tag="l1v3")
            self.act(rstd[:, 0:w], rstd[:, 0:w], AF.Exp, KRS1, KRS1,
                     scale=-0.5, tag="l1v4")
            for c in range(NCH):
                self.stt(xsq.ap(c, 0, w), self.psBC[:, 0:w], -1.0 / D,
                         op.ap(c, 0, w), ALU.mult, ALU.add,
                         [("ps", "psBC")] + op.keys(c, 0, w),
                         xsq.keys(c, 0, w), tag="l1c")
                self.tt(xnL1.ap(c, p0, p1), xsq.ap(c, 0, w), rstd[:, 0:w],
                        ALU.mult, xsq.keys(c, 0, w) + KRS1,
                        xnL1.keys(c, p0, p1), tag="l1n")
        self.bfree(gpan); self.bfree(xp); self.bfree(sp)
        self.bfree(ops2[0]); self.bfree(ops2[1]); self.bfree(xsq)
        self.bfree(self.xab1)


    def phase_l1(self):
        g = self.g
        NOS = g.t_own + 2
        TO = g.t_own
        PW1 = 256

        xnL1 = self.xnL1
        for c in range(NCH):
            self.ts(xnL1.ap(c, 0, 1), xnL1.ap(c, 0, 1), self.sb_edge0[:, 0:1],
                    ALU.mult, xnL1.keys(c, 0, 1) + [("sc", "edge0")],
                    xnL1.keys(c, 0, 1), tag="l1e0")
            self.ts(xnL1.ap(c, NOS - 1, NOS), xnL1.ap(c, NOS - 1, NOS),
                    self.sb_edge3[:, 0:1], ALU.mult,
                    xnL1.keys(c, NOS - 1, NOS) + [("sc", "edge3")],
                    xnL1.keys(c, NOS - 1, NOS), tag="l1e3")

        self.tap("xnL1", xnL1.full_ap(), [128, NCH * NOS], xnL1.kall(),
                 dtype=BF16)
        l1c = self.sbuf("l1cols", [128, 48])
        KL1C = [("sc", "l1cols")]
        ewc = lambda c: l1c[:, c:c + 1]
        euc = lambda c: l1c[:, 8 + c:9 + c]
        ach = lambda c: l1c[:, 16 + c:17 + c]
        bch = lambda c: l1c[:, 24 + c:25 + c]
        ag3in = l1c[:, 32:48]
        for c in range(NCH):
            self.act(ewc(c), self.sb_sdc[:, c:c + 1], AF.Exp,
                     [("sc", "sdc")], KL1C, scale=1.0 / g.seq, tag="ew")
            self.act(euc(c), self.sb_sfc[:, c:c + 1], AF.Exp,
                     [("sc", "sfc")], KL1C, scale=1.0 / g.seq, tag="eu")

        m1 = self.sbuf("l1m1", [128, 24])
        KM1 = [("sc", "l1m1")]
        for i, n in enumerate(("mixkc", "mixvc", "mixrc")):
            self.ts(m1[:, i * 8:(i + 1) * 8], getattr(self, "sb_" + n)[:],
                    -1.0, ALU.mult, [("sc", n)], KM1, s2=1.0, op1=ALU.add,
                    tag="m1")

        mixp = self.abuf("l1mix", 4, PW1, BF16)
        ekf = self.abuf("l1ek", NCH, TO, BF16)
        ekvf = self.abuf("l1ekv", NCH, TO, BF16)
        fE = self.abuf("l1fE", NCH, PW1)
        fV = self.abuf("l1fV", NCH, PW1)
        fA2 = self.abuf("l1fA", NCH, PW1, lead=1)
        fB2 = self.abuf("l1fB", NCH, PW1, lead=1)

        def mk_mix(mi, mname, lo, hi, tag):
            w = hi - lo
            mcol = getattr(self, "sb_" + mname)
            for c in range(4):
                sh = -1 if c < 2 else 1
                self.ts(fE.ap(c, 0, w), xnL1.ap(c, lo, hi), mcol[:, c:c + 1],
                        ALU.mult, xnL1.keys(c, lo, hi) + [("sc", mname)],
                        fE.keys(c, 0, w), tag=tag + "a")
                self.stt(mixp.ap(c, 0, w), xnL1.ap(c, lo + sh, hi + sh),
                         m1[:, mi * 8 + c:mi * 8 + c + 1], fE.ap(c, 0, w),
                         ALU.mult, ALU.add,
                         xnL1.keys(c, lo + sh, hi + sh) + KM1 + fE.keys(c, 0, w),
                         mixp.keys(c, 0, w), tag=tag + "b")

        def l1_sweep(wname, mi, mname, evac):
            wd = getattr(self, wname)
            wts = [self.abuf("w_" + wname, NCH, 512, BF16)]
            for ho in (0, 512):
                wt = wts[0]
                self.dma("gpsimd",
                         wt.full_ap().rearrange("p (a c) -> p a c", a=NCH),
                         wd[:, ho:ho + 512].rearrange("(a p) c -> p a c",
                                                      p=128),
                         [dkey(wname)], wt.kall(), tag="wload_" + wname)
                for p0 in range(0, TO, PW1):
                    p1 = min(p0 + PW1, TO)
                    w = p1 - p0
                    mk_mix(mi, mname, 1 + p0, 1 + p1, tag=wname)
                    for oc2 in range(4):
                        oc = ho // 128 + oc2
                        psn, ps = (("psA", self.psA) if oc % 2 == 0
                                   else ("psB", self.psB))
                        for kc in range(NCH):
                            rhs = (mixp.ap(kc, 0, w) if kc < 4
                                   else xnL1.ap(kc, 1 + p0, 1 + p1))
                            rk = (mixp.keys(kc, 0, w) if kc < 4
                                  else xnL1.keys(kc, 1 + p0, 1 + p1))
                            self.mm(psn, ps[:, 0:w],
                                    wt.ap(kc, oc2 * 128, oc2 * 128 + 128),
                                    rhs, kc == 0, kc == NCH - 1,
                                    wt.keys(kc, oc2 * 128, oc2 * 128 + 128),
                                    rk, tag=wname)
                        evac(oc, p0, p1, psn, ps)
            for w_ in wts:
                self.bfree(w_)

        wk = self.wfull("wkt", self.wkt)
        wv = self.wfull("wvt", self.wvt)
        for p0 in range(0, TO, PW1):
            p1 = min(p0 + PW1, TO)
            w = p1 - p0
            mk_mix(0, "mixkc", 1 + p0, 1 + p1, tag="l1k")
            for oc in range(NCH):
                psn, ps = ("psC", self.psC) if oc % 2 == 0 else ("psD", self.psD)
                for kc in range(NCH):
                    rhs = (mixp.ap(kc, 0, w) if kc < 4
                           else xnL1.ap(kc, 1 + p0, 1 + p1))
                    rk = (mixp.keys(kc, 0, w) if kc < 4
                          else xnL1.keys(kc, 1 + p0, 1 + p1))
                    self.mm(psn, ps[:, 0:w],
                            wk.ap(kc, oc * 128, oc * 128 + 128), rhs,
                            kc == 0, kc == NCH - 1,
                            wk.keys(kc, oc * 128, oc * 128 + 128), rk,
                            tag="l1kmm")
                self.act(ekf.ap(oc, p0, p1), ps[:, 0:w], AF.Exp,
                         [("ps", psn)], ekf.keys(oc, p0, p1), tag="evek")
            mk_mix(1, "mixvc", 1 + p0, 1 + p1, tag="l1v")
            for oc in range(NCH):
                psn, ps = ("psC", self.psC) if oc % 2 == 0 else ("psD", self.psD)
                for kc in range(NCH):
                    rhs = (mixp.ap(kc, 0, w) if kc < 4
                           else xnL1.ap(kc, 1 + p0, 1 + p1))
                    rk = (mixp.keys(kc, 0, w) if kc < 4
                          else xnL1.keys(kc, 1 + p0, 1 + p1))
                    self.mm(psn, ps[:, 0:w],
                            wv.ap(kc, oc * 128, oc * 128 + 128), rhs,
                            kc == 0, kc == NCH - 1,
                            wv.keys(kc, oc * 128, oc * 128 + 128), rk,
                            tag="l1vmm")
                self.tt(ekvf.ap(oc, p0, p1), ekf.ap(oc, p0, p1),
                        ps[:, 0:w], ALU.mult,
                        ekf.keys(oc, p0, p1) + [("ps", psn)],
                        ekvf.keys(oc, p0, p1), tag="evekv")
            for oc in range(NCH):
                if p0 == 0:
                    self.memset(ach(oc), 0.0, KL1C, tag="a0")
                    self.memset(bch(oc), 0.0, KL1C, tag="b0")
                self.scan(fV.ap(oc, 0, w), ewc(oc).broadcast_to([128, w]),
                          ekvf.ap(oc, p0, p1), ach(oc),
                          ekvf.keys(oc, p0, p1) + KL1C, fV.keys(oc, 0, w),
                          tag="asc")
                self.copy(ach(oc), fV.ap(oc, w - 1, w), fV.keys(oc, w - 1, w),
                          KL1C, tag="aup")
                self.scan(fE.ap(oc, 0, w), ewc(oc).broadcast_to([128, w]),
                          ekf.ap(oc, p0, p1), bch(oc),
                          ekf.keys(oc, p0, p1) + KL1C, fE.keys(oc, 0, w),
                          tag="bsc")
                self.copy(bch(oc), fE.ap(oc, w - 1, w), fE.keys(oc, w - 1, w),
                          KL1C, tag="bup")
        self.bfree(wk)
        self.bfree(wv)
        for c in range(NCH):
            self.copy(ag3in[:, c:c + 1], ach(c), KL1C, KL1C, tag="p3a")
            self.copy(ag3in[:, 8 + c:9 + c], bch(c), KL1C, KL1C, tag="p3b")
        agg3 = self.allgather(3, ag3in, 16, KL1C)
        wkvw = self.sbuf("wkvw", [128, 64])
        KWW = [("sc", "wkvw")]
        for c in range(NCH):
            self.ts(wkvw[:, c * 8:(c + 1) * 8], self.sb_pexprep[:],
                    self.sb_sdc[:, c:c + 1], ALU.mult,
                    [("sc", "pexprep"), ("sc", "sdc")], KWW, tag="ww1")
            self.act(wkvw[:, c * 8:(c + 1) * 8], wkvw[:, c * 8:(c + 1) * 8],
                     AF.Exp, KWW, KWW, scale=0.25, tag="ww2")
            self.tt(wkvw[:, c * 8:(c + 1) * 8], wkvw[:, c * 8:(c + 1) * 8],
                    self.sb_pmaskrep[:], ALU.mult,
                    KWW + [("sc", "pmaskrep")], KWW, tag="ww3")
        car3 = self.carry_cols(agg3, 16, "car3",
                               weights=(wkvw[:, 0:64], ("sc", "wkvw")))
        k_car3 = ("sc", "car3_c")

        srf = self.abuf("l1sr", NCH, TO, BF16)

        def evr(oc, p0, p1, psn, ps):
            self.act(srf.ap(oc, p0, p1), ps[:, 0:p1 - p0], AF.Sigmoid,
                     [("ps", psn)], srf.keys(oc, p0, p1), tag="evr")

        l1_sweep("wrt", 2, "mixrc", evr)

        for p0 in range(0, TO, PW1):
            p1 = min(p0 + PW1, TO)
            w = p1 - p0
            for oc in range(NCH):
                if p0 == 0:
                    self.copy(ach(oc), car3[:, oc:oc + 1], [k_car3], KL1C,
                              tag="ai")
                    self.copy(bch(oc), car3[:, 8 + oc:9 + oc], [k_car3], KL1C,
                              tag="bi")
                self.copy(fA2.ap(oc, -1, 0), ach(oc), KL1C,
                          fA2.keys(oc, -1, 0), tag="al")
                self.copy(fB2.ap(oc, -1, 0), bch(oc), KL1C,
                          fB2.keys(oc, -1, 0), tag="bl")
                self.scan(fA2.ap(oc, 0, w), ewc(oc).broadcast_to([128, w]),
                          ekvf.ap(oc, p0, p1), ach(oc),
                          ekvf.keys(oc, p0, p1) + KL1C, fA2.keys(oc, 0, w),
                          tag="asc2")
                self.copy(ach(oc), fA2.ap(oc, w - 1, w), fA2.keys(oc, w - 1, w),
                          KL1C, tag="aup2")
                self.scan(fB2.ap(oc, 0, w), ewc(oc).broadcast_to([128, w]),
                          ekf.ap(oc, p0, p1), bch(oc),
                          ekf.keys(oc, p0, p1) + KL1C, fB2.keys(oc, 0, w),
                          tag="bsc2")
                self.copy(bch(oc), fB2.ap(oc, w - 1, w), fB2.keys(oc, w - 1, w),
                          KL1C, tag="bup2")
                self.stt(fE.ap(oc, 0, w), ekf.ap(oc, p0, p1), euc(oc),
                         fB2.ap(oc, -1, w - 1), ALU.mult, ALU.add,
                         ekf.keys(oc, p0, p1) + KL1C + fB2.keys(oc, -1, w - 1),
                         fE.keys(oc, 0, w), tag="den")
                self.stt(fV.ap(oc, 0, w), ekvf.ap(oc, p0, p1), euc(oc),
                         fA2.ap(oc, -1, w - 1), ALU.mult, ALU.add,
                         ekvf.keys(oc, p0, p1) + KL1C + fA2.keys(oc, -1, w - 1),
                         fV.keys(oc, 0, w), tag="num")
                self.act(fE.ap(oc, 0, w), fE.ap(oc, 0, w), AF.Ln,
                         fE.keys(oc, 0, w), fE.keys(oc, 0, w), tag="dln")
                self.act(fE.ap(oc, 0, w), fE.ap(oc, 0, w), AF.Exp,
                         fE.keys(oc, 0, w), fE.keys(oc, 0, w), scale=-1.0,
                         tag="dex")
                self.tt(fV.ap(oc, 0, w), fV.ap(oc, 0, w), fE.ap(oc, 0, w),
                        ALU.mult, fV.keys(oc, 0, w) + fE.keys(oc, 0, w),
                        fV.keys(oc, 0, w), tag="wkv")
                self.tt(srf.ap(oc, p0, p1), srf.ap(oc, p0, p1),
                        fV.ap(oc, 0, w), ALU.mult,
                        srf.keys(oc, p0, p1) + fV.keys(oc, 0, w),
                        srf.keys(oc, p0, p1), tag="srw")
        self.tap("srwkv", srf.full_ap(), [128, NCH * TO], srf.kall(),
                 dtype=BF16)
        self.tap("ekf", ekf.full_ap(), [128, NCH * TO], ekf.kall(),
                 dtype=BF16)
        self.bfree(ekf)
        self.bfree(ekvf)
        self.bfree(fE); self.bfree(fV); self.bfree(fA2); self.bfree(fB2)
        self.bfree(mixp)

        sL1 = self.abuf("l1s", NCH, TO, BF16)
        wo2 = self.wfull("wot", self.wot)
        for p0 in range(0, TO, PW1):
            p1 = min(p0 + PW1, TO)
            w = p1 - p0
            for oc in range(NCH):
                psn, ps = ("psA", self.psA) if oc % 2 == 0 else ("psB", self.psB)
                for kc in range(NCH):
                    self.mm(psn, ps[:, 0:w], wo2.ap(kc, oc * 128, oc * 128 + 128),
                            srf.ap(kc, p0, p1), kc == 0, kc == NCH - 1,
                            wo2.keys(kc, oc * 128, oc * 128 + 128),
                            srf.keys(kc, p0, p1), tag="l1omm")
                self.copy(sL1.ap(oc, p0, p1), ps[:, 0:w], [("ps", psn)],
                          sL1.keys(oc, p0, p1), eng="scalar", tag="evo")
        self.bfree(wo2)
        self.bfree(srf)

        self.tap("sL1", sL1.full_ap(), [128, NCH * TO], sL1.kall(),
                 dtype=BF16)
        self.l1_ffn(xnL1, sL1)

    def quantize_w(self, src_ap, n_rows, name, lnw=None, lnb=None,
                   colsum_rows=None):
        g = self.g
        cols = src_ap.shape[1]
        dst = self.dram("wq_" + name, [n_rows, cols], BF16)
        pan = self.abuf("qz_" + name, 1, 512)
        pan2 = self.abuf("qz2_" + name, 1, 512)
        panb = self.abuf("qzb_" + name, 1, 512, BF16)
        panb2 = self.abuf("qzb2_" + name, 1, 512, BF16)
        kp = pan.keys(0, 0, 512)
        kp2 = pan2.keys(0, 0, 512)
        kpb = panb.keys(0, 0, 512)
        kpb2 = panb2.keys(0, 0, 512)
        if not hasattr(self, "ones_colb"):
            self.ones_colb = self.sbuf("ones_colb", [128, 1], BF16)
            self.mones_colb = self.sbuf("mones_colb", [128, 1], BF16)
            self.memset(self.ones_colb[:], 1.0, [("sc", "ones_colb")])
            self.memset(self.mones_colb[:], -1.0, [("sc", "mones_col")])
        acc = self.sbuf("qacc_" + name, [128, 2])
        KA = [("sc", "qacc_" + name)]
        self.memset(acc[:, 0:1], 0.0, KA, tag="qz0")
        group = max(1, 512 // cols)
        colpan = max(1, cols // 512)
        npan = (n_rows // (128 * group)) * colpan
        assert group == 1 or (lnw is None and colsum_rows is None)

        def panape(p):
            r0 = (p // colpan) * 128 * group
            c0 = (p % colpan) * 512
            wq = min(512, cols - c0) if group == 1 else cols * group
            return r0, c0, wq

        def src_slice(r0, c0, wq):
            if group == 1:
                return src_ap[r0:r0 + 128, c0:c0 + wq]
            return src_ap[r0:r0 + 128 * group, :].rearrange(
                "(a p) c -> p a c", p=128)

        def dst_slice(r0, c0, wq):
            if group == 1:
                return dst[r0:r0 + 128, c0:c0 + wq]
            return dst[r0:r0 + 128 * group, :].rearrange(
                "(a p) c -> p a c", p=128)

        for p in range(npan):
            r0, c0, wq = panape(p)
            self.dma("sync", pan.ap(0, 0, wq), src_slice(r0, c0, wq),
                     [dkey(name)], kp, tag="qzl")
            self.act(pan2.ap(0, 0, wq), pan.ap(0, 0, wq), AF.Abs, kp, kp2,
                     tag="qza")
            self.red(pan2.ap(0, 0, 1), pan2.ap(0, 0, wq), ALU.add, kp2, kp2,
                     tag="qzr")
            self.tt(acc[:, 0:1], acc[:, 0:1], pan2.ap(0, 0, 1), ALU.add,
                    KA + kp2, KA, tag="qzacc")
        self.mm("psBC", self.psBC[0:1, 0:1], acc[:, 0:1], self.ones_col[:],
                True, True, KA, [("sc", "ones_col")], tag="qzs")
        self.copy(acc[0:1, 1:2], self.psBC[0:1, 0:1], [("ps", "psBC")], KA,
                  tag="qzc")
        self.mm("psBC", self.psBC[:, 0:1], self.ones128[0:1, :],
                acc[0:1, 1:2], True, True, [("sc", "ones128")], KA, tag="qzb")
        self.ts(acc[:, 1:2], self.psBC[:, 0:1],
                0.5 / (n_rows * cols), ALU.mult,
                [("ps", "psBC")], KA, s2=5e-7, op1=ALU.add, tag="qzg")
        for p in range(npan):
            r0, c0, wq = panape(p)
            self.dma("sync", pan.ap(0, 0, wq), src_slice(r0, c0, wq),
                     [dkey(name)], kp, tag="qzl2")
            self.act(pan2.ap(0, 0, wq), pan.ap(0, 0, wq), AF.Abs, kp, kp2,
                     tag="qza2")
            self.ts(pan2.ap(0, 0, wq), pan2.ap(0, 0, wq), acc[:, 1:2],
                    ALU.is_gt, kp2 + KA, kp2, tag="qzm")
            self.act(pan.ap(0, 0, wq), pan.ap(0, 0, wq), AF.Sign, kp, kp,
                     tag="qzs2")
            if lnw is None:
                self.tt(panb.ap(0, 0, wq), pan2.ap(0, 0, wq), pan.ap(0, 0, wq),
                        ALU.mult, kp2 + kp, kpb, tag="qzq")
            else:
                self.tt(pan.ap(0, 0, wq), pan2.ap(0, 0, wq), pan.ap(0, 0, wq),
                        ALU.mult, kp2 + kp, kp, tag="qzq1")
                self.ts(panb.ap(0, 0, wq), pan.ap(0, 0, wq),
                        lnw[:, (r0 // 128):(r0 // 128) + 1], ALU.mult,
                        kp + [("sc", "lnwc")], kpb, tag="qzq2")
            self.dma("scalar", dst_slice(r0, c0, wq), panb.ap(0, 0, wq),
                     kpb, [dkey("wq_" + name)], tag="qzw")
            if colsum_rows is not None:
                self.mm("psSC", self.psSC[0:1, c0:c0 + wq],
                        self.mones_colb[:], panb.ap(0, 0, wq),
                        r0 == 0, r0 == n_rows - 128,
                        [("sc", "mones_col")], kpb, tag="qzn")
                self.ts(panb2.ap(0, 0, wq), pan.ap(0, 0, wq),
                        lnb[:, (r0 // 128):(r0 // 128) + 1], ALU.mult,
                        kp + [("sc", "lnbc")], kpb2, tag="qzlb")
                psn2 = "psA" if c0 == 0 else "psB"
                ps2 = self.psA if c0 == 0 else self.psB
                self.mm(psn2, ps2[0:1, 0:wq], self.ones_colb[:],
                        panb2.ap(0, 0, wq), r0 == 0, r0 == n_rows - 128,
                        [("sc", "ones_colb")], kpb2, tag="qzcs")
        self.bfree(pan)
        self.bfree(pan2)
        self.bfree(panb)
        self.bfree(panb2)
        return dst

    def l1_ffn(self, xnL1, sL1):
        g = self.g
        TO = g.t_own
        if not hasattr(self, "mones_col"):
            self.mones_col = self.sbuf("mones_col", [128, 1])
            self.memset(self.mones_col[:], -1.0, [("sc", "mones_col")],
                        tag="mones")
        wqfc = self.quantize_w(self.wfctb, 32 * 128, "fc")
        negrow = self.psSC[0:1, 0:1024]
        lnbrow = self.psBC2[0:1, 0:512]
        wqpj = self.quantize_w(
            self.wprojT2, 4096, "pj", lnw=self.sb_lnwc, lnb=self.sb_lnbc,
            colsum_rows=True)
        csneg = self.sbuf("csneg", [128, NCH])
        cslnb = self.sbuf("cslnb", [128, NCH])
        KCS = [("sc", "csneg")]
        KCL = [("sc", "cslnb")]
        rowst = self.abuf("rowst", 1, 1024, BF16)
        self.copy(rowst.ap(0, 0, 1024, dp=0, dn=1), self.psSC[0:1, 0:1024],
                  [("ps", "psSC")], rowst.keys(0, 0, 1024), tag="csc1")
        dbounce = self.dram("csbounce", [2048], BF16)
        self.dma("gpsimd", dbounce[0:1024].rearrange("(o f) -> o f", o=1),
                 rowst.ap(0, 0, 1024, dp=0, dn=1), rowst.keys(0, 0, 1024),
                 [dkey("csbounce")], tag="csd1")
        self.copy(rowst.ap(0, 0, 512, dp=0, dn=1), self.psA[0:1, 0:512],
                  [("ps", "psA")], rowst.keys(0, 0, 1024), tag="csc2")
        self.copy(rowst.ap(0, 512, 1024, dp=0, dn=1), self.psB[0:1, 0:512],
                  [("ps", "psB")], rowst.keys(0, 0, 1024), tag="csc3")
        self.dma("gpsimd", dbounce[1024:2048].rearrange("(o f) -> o f", o=1),
                 rowst.ap(0, 0, 1024, dp=0, dn=1), rowst.keys(0, 0, 1024),
                 [dkey("csbounce")], tag="csd2")
        self.dma("gpsimd", csneg[:],
                 dbounce[0:1024].rearrange("(c p) -> p c", p=128),
                 [dkey("csbounce")], KCS, tag="csd3", slow_ok=True)
        self.dma("gpsimd", cslnb[:],
                 dbounce[1024:2048].rearrange("(c p) -> p c", p=128),
                 [dkey("csbounce")], KCL, tag="csd4", slow_ok=True)
        self.bfree(rowst)

        QT = TO // 4
        hbuf = self.abuf("l1h", 32, QT, BF16)
        wblk2 = [self.abuf(f"l1wb{k}", NCH, 128, BF16) for k in range(2)]
        wpb2 = [self.abuf(f"l1wp{k}", 1, 1024, BF16) for k in range(2)]
        sqp2 = self.abuf("l1hsq", 1, 512, BF16)
        lrow = self.sbuf("l1lrow", [1, 512])
        lrow2 = self.sbuf("l1lrow2", [1, 512])
        KLR = [("sc", "l1lrow")]
        KLR2 = [("sc", "l1lrow2")]
        gp = self.abuf("l1gp", 1, 512)
        g2p = self.abuf("l1g2", 1, 512, BF16)
        xop = self.abuf("l1xo", 1, 512)
        praw = self.abuf("l1praw", NCH, QT, BF16)
        for qt in range(4):
            q0 = qt * QT
            for b in range(32):
                wblk = wblk2[b % 2]
                self.dma("sync" if b % 2 == 0 else "scalar",
                         wblk.full_ap(), wqfc[b * 128:(b + 1) * 128, :],
                         [dkey("wq_fc")], wblk.kall(), tag="hwl")
                for sp in range(0, QT, 512):
                    s1 = min(sp + 512, QT)
                    w = s1 - sp
                    psn, ps = ("psA", self.psA) if b % 2 == 0 else ("psB", self.psB)
                    for kc in range(NCH):
                        self.mm(psn, ps[:, 0:w], wblk.ap(kc, 0, 128),
                                xnL1.ap(kc, 1 + q0 + sp, 1 + q0 + s1),
                                kc == 0, kc == NCH - 1,
                                wblk.keys(kc, 0, 128),
                                xnL1.keys(kc, 1 + q0 + sp, 1 + q0 + s1),
                                tag="hmm")
                    self.act(hbuf.ap(b, sp, s1), ps[:, 0:w], AF.Gelu,
                             [("ps", psn)], hbuf.keys(b, sp, s1), tag="hev")
            if qt == 0:
                self.tap("hbuf0", hbuf.full_ap(), [128, 32 * QT],
                         hbuf.kall(), dtype=BF16)
            for sp in range(0, QT, 512):
                s1 = min(sp + 512, QT)
                w = s1 - sp
                for b in range(32):
                    self.act(sqp2.ap(0, 0, w), hbuf.ap(b, sp, s1), AF.Square,
                             hbuf.keys(b, sp, s1), sqp2.keys(0, 0, w),
                             tag="hsq")
                    self.mm("psBC", self.psBC[:, 0:w], self.ones128_b[:],
                            hbuf.ap(b, sp, s1), b == 0, b == 31,
                            [("sc", "ones128_b")], hbuf.keys(b, sp, s1),
                            tag="hst1")
                    self.mm("psBC2", self.psBC2[:, 0:w], self.ones128_b[:],
                            sqp2.ap(0, 0, w), b == 0, b == 31,
                            [("sc", "ones128_b")], sqp2.keys(0, 0, w),
                            tag="hst2")
                DH4 = 4096.0
                self.act(lrow[0:1, sp:s1], self.psBC[0:1, 0:w],
                         AF.Square, [("ps", "psBC")], KLR, tag="hr1")
                self.stt(lrow[0:1, sp:s1], lrow[0:1, sp:s1], -1.0 / DH4,
                         self.psBC2[0:1, 0:w], ALU.mult, ALU.add,
                         KLR + [("ps", "psBC2")], KLR, tag="hr2")
                self.act(lrow[0:1, sp:s1], lrow[0:1, sp:s1], AF.Ln, KLR, KLR,
                         bias=1e-5, scale=1.0 / DH4, tag="hr3")
                self.act(lrow[0:1, sp:s1], lrow[0:1, sp:s1], AF.Exp, KLR, KLR,
                         scale=-0.5, tag="hr4")
                self.ts(lrow2[0:1, sp:s1], self.psBC[0:1, 0:w], 1.0 / DH4,
                        ALU.mult, [("ps", "psBC")], KLR2, tag="hr5")
                self.tt(lrow2[0:1, sp:s1], lrow2[0:1, sp:s1], lrow[0:1, sp:s1],
                        ALU.mult, KLR2 + KLR, KLR2, tag="hr6")
            for b in range(32):
                wpb = wpb2[b % 2]
                self.dma("sync" if b % 2 == 0 else "scalar",
                         wpb.ap(0, 0, 1024, dp=0, dn=128),
                         wqpj[b * 128:(b + 1) * 128, :],
                         [dkey("wq_pj")],
                         wpb.keys(0, 0, 1024), tag="pwl")
                for oc in range(NCH):
                    for sp in range(0, QT, 512):
                        s1 = min(sp + 512, QT)
                        bank = ["psA", "psB", "psC", "psD", "psBC", "psBC2",
                                "psSC", "psSC"][oc]
                        ps = {"psA": self.psA, "psB": self.psB, "psC": self.psC,
                              "psD": self.psD, "psBC": self.psBC,
                              "psBC2": self.psBC2, "psSC": self.psSC}[bank]
                        off = 512 if oc == 7 else 0
                        self.mm(bank, ps[:, off:off + (s1 - sp)],
                                wpb.ap(0, oc * 128, oc * 128 + 128, dp=0, dn=128),
                                hbuf.ap(b, sp, s1), b == 0, b == 31,
                                wpb.keys(0, oc * 128, oc * 128 + 128),
                                hbuf.keys(b, sp, s1), tag="pjm")
            for oc in range(NCH):
                for sp in range(0, QT, 512):
                    s1 = min(sp + 512, QT)
                    w = s1 - sp
                    bank = ["psA", "psB", "psC", "psD", "psBC", "psBC2",
                            "psSC", "psSC"][oc]
                    ps = {"psA": self.psA, "psB": self.psB, "psC": self.psC,
                          "psD": self.psD, "psBC": self.psBC,
                          "psBC2": self.psBC2, "psSC": self.psSC}[bank]
                    off = 512 if oc == 7 else 0
                    self.copy(praw.ap(oc, sp, s1), ps[:, off:off + w],
                              [("ps", bank)], praw.keys(oc, sp, s1),
                              eng="scalar", tag="prev")
            for sp in range(0, QT, 512):
                s1 = min(sp + 512, QT)
                w = s1 - sp
                self.mm("psBC", self.psBC[:, 0:w], self.ones128[0:1, :],
                        lrow[0:1, sp:s1], True, True, [("sc", "ones128")],
                        KLR, tag="bcr")
                self.mm("psBC2", self.psBC2[:, 0:w], self.ones128[0:1, :],
                        lrow2[0:1, sp:s1], True, True, [("sc", "ones128")],
                        KLR2, tag="bcm")
                for oc in range(NCH):
                    self.tt(gp.ap(0, 0, w), praw.ap(oc, sp, s1),
                            self.psBC[:, 0:w], ALU.mult,
                            praw.keys(oc, sp, s1) + [("ps", "psBC")],
                            gp.keys(0, 0, w), tag="fold1")
                    self.stt(gp.ap(0, 0, w), self.psBC2[:, 0:w],
                             csneg[:, oc:oc + 1], gp.ap(0, 0, w),
                             ALU.mult, ALU.add,
                             [("ps", "psBC2")] + KCS + gp.keys(0, 0, w),
                             gp.keys(0, 0, w), tag="fold2")
                    self.act(g2p.ap(0, 0, w), gp.ap(0, 0, w), AF.Silu,
                             gp.keys(0, 0, w) + KCL, g2p.keys(0, 0, w),
                             bias=cslnb[:, oc:oc + 1], tag="silu2")
                    self.dma("sync", xop.ap(0, 0, w),
                             self.x1mod_spill[oc * 128:(oc + 1) * 128,
                                              1 + q0 + sp:1 + q0 + s1],
                             [dkey("x1mod_spill", oc, (1 + q0 + sp) // 512),
                              dkey("x1mod_spill", oc, (q0 + s1) // 512)],
                             xop.keys(0, 0, w), tag="cbx")
                    self.tt(g2p.ap(0, 0, w), g2p.ap(0, 0, w),
                            sL1.ap(oc, q0 + sp, q0 + s1), ALU.mult,
                            g2p.keys(0, 0, w) + sL1.keys(oc, q0 + sp, q0 + s1),
                            g2p.keys(0, 0, w), tag="sg2")
                    self.tt(xop.ap(0, 0, w), xop.ap(0, 0, w), g2p.ap(0, 0, w),
                            ALU.add, xop.keys(0, 0, w) + g2p.keys(0, 0, w),
                            xop.keys(0, 0, w), tag="out")
                    self.dma("scalar", self.out_fm[oc * 128:(oc + 1) * 128,
                                                   q0 + sp:q0 + s1],
                             xop.ap(0, 0, w), xop.keys(0, 0, w),
                             [dkey("out_fm", oc, (q0 + sp) // 512)],
                             tag="outw")
        self.bfree(hbuf); self.bfree(sqp2)
        for b_ in wblk2 + wpb2:
            self.bfree(b_)
        self.bfree(gp); self.bfree(g2p); self.bfree(xop); self.bfree(praw)
        self.bfree(xnL1); self.bfree(sL1)

    def finalize(self, final_outputs):
        S = self.S
        final = [(k, S.counts[k]) for k in final_outputs if k in S.counts]
        sems = {}
        semlist = sorted(set(list(S.counts.keys())))
        with ExitStack() as es:
            for k in semlist:
                sems[k] = es.enter_context(self.nc.semaphore("sem_" + k))
            with self.nc.Block() as block:
                for eng in ENGINES:
                    def mk(e_name):
                        def emit_fn(eng_obj):
                            for rec in S.ops[e_name]:
                                for s, v in rec.waits:
                                    eng_obj.wait_ge(sems[s], v)
                                inst = rec.emit(eng_obj)
                                inst.then_inc(sems[rec.sem],
                                              16 if rec.kind == "dma" else 1)
                            for s, v in final:
                                eng_obj.wait_ge(sems[s], v)
                        return emit_fn
                    getattr(block, eng)(mk(eng))
        self.stack.close()
        return self.nc



def _build(geom, taps=()):
    P = Program(geom, taps)
    P.declare()
    P.phase_xn()
    P.phase_l0_attn()
    P.phase_l0_ffn()
    P.phase_combine()
    P.phase_l1()
    nc = P.finalize([k for k in P.S.counts if "dma" in k])
    return nc, P


def kernel(**inputs):
    g = GEOM_FULL
    in_maps = stage_inputs(inputs, g)
    nc, _ = _build(g)
    res = run_bass_kernel_spmd(nc, in_maps, core_ids=list(range(N_CORES)))
    return unstage_output(res.results, g)


# version 10
# speedup vs baseline: 1.3173x; 1.0054x over previous
import math
from contextlib import ExitStack

import numpy as np
import ml_dtypes

import concourse.bass as bass
import concourse.mybir as mybir
from concourse.bass_utils import run_bass_kernel_spmd

F32 = mybir.dt.float32
BF16 = mybir.dt.bfloat16
AF = mybir.ActivationFunctionType
ALU = mybir.AluOpType
AXL = mybir.AxisListType
BF16NP = ml_dtypes.bfloat16

N_CORES = 8
D = 1024
NCH = 8
NEG = -1.0e30


class Geom:
    def __init__(self, t_own=2048, pad_l=1280, pad_r=128):
        assert t_own % 256 == 0 and pad_l % 128 == 0 and pad_r % 128 == 0
        self.t_own = t_own
        self.pad_l = pad_l
        self.pad_r = pad_r
        self.t_buf = pad_l + t_own + pad_r
        self.seq = 4 * t_own
        self.n_tiles = self.t_buf // 128
        self.own_lo = pad_l
        self.own_hi = pad_l + t_own


GEOM_FULL = Geom()


ENGINES = ("tensor", "vector", "scalar", "gpsimd", "sync")


class _OpRec:
    __slots__ = ("engine", "kind", "sem", "val", "waits", "clock", "emit", "tag")

    def __init__(self, engine, kind, sem, val, waits, clock, emit, tag):
        self.engine = engine
        self.kind = kind
        self.sem = sem
        self.val = val
        self.waits = waits
        self.clock = clock
        self.emit = emit
        self.tag = tag


class Sched:

    DMA_ROT = 8

    def __init__(self):
        self.ops = {e: [] for e in ENGINES}
        self.counts = {}
        self.last_writer = {}
        self.readers = {}
        self.eng_clock = {e: {} for e in ENGINES}
        self.dma_rot = {}

    def _sem_of(self, engine, kind):
        if kind == "dma":
            j = self.dma_rot.get(engine, 0)
            self.dma_rot[engine] = j + 1
            return f"{engine}_dma{j % self.DMA_ROT}"
        if kind == "cc":
            return "cc"
        return engine

    def op(self, engine, emit, reads=(), writes=(), kind="compute",
           accum=False, tag=""):
        semkey = self._sem_of(engine, kind)
        inc = 16 if kind == "dma" else 1
        val = self.counts.get(semkey, 0) + inc
        self.counts[semkey] = val

        deps = []
        for r in reads:
            w = self.last_writer.get(r)
            if w is not None:
                deps.append(w)
        for w in writes:
            lw = self.last_writer.get(w)
            if lw is not None and not (accum and lw.engine == engine == "tensor"):
                deps.append(lw)
            deps.extend(self.readers.get(w, ()))

        clk = self.eng_clock[engine]
        wd = {}
        for d in deps:
            if clk.get(d.sem, 0) >= d.val:
                continue
            wd[d.sem] = max(wd.get(d.sem, 0), d.val)
            for k, v in d.clock.items():
                if clk.get(k, 0) < v:
                    clk[k] = v
            if clk.get(d.sem, 0) < d.val:
                clk[d.sem] = d.val
        if kind in ("dma", "cc") and val > inc and clk.get(semkey, 0) < val - inc:
            wd[semkey] = max(wd.get(semkey, 0), val - inc)
            clk[semkey] = max(clk.get(semkey, 0), val - inc)
        waits = sorted(wd.items())

        myclock = dict(clk)
        myclock[semkey] = val
        rec = _OpRec(engine, kind, semkey, val, waits, myclock, emit, tag)
        self.ops[engine].append(rec)

        for w in writes:
            self.last_writer[w] = rec
            self.readers[w] = []
        for r in reads:
            self.readers.setdefault(r, []).append(rec)
        return rec


class Arena:

    GRAIN = 512

    def __init__(self, t, name, n):
        self.t = t
        self.name = name
        self.n = n
        self.free_list = [(0, n)]
        self.hwm = 0

    def alloc(self, n, align=128, side="lo"):
        order = range(len(self.free_list))
        if side == "hi":
            order = reversed(list(order))
        for i in order:
            s, l = self.free_list[i]
            if side == "hi":
                a = (s + l - n) // align * align
                if a < s:
                    continue
            else:
                a = (s + align - 1) // align * align
                if a + n > s + l:
                    continue
            pre = (s, a - s)
            post = (a + n, s + l - (a + n))
            repl = [p for p in (pre, post) if p[1] > 0]
            self.free_list[i:i + 1] = repl
            self.hwm = max(self.hwm, a + n)
            return a
        used = self.n - sum(l for _, l in self.free_list)
        raise MemoryError(
            f"arena {self.name}: cannot alloc {n} (cap {self.n}, in use "
            f"{used}, free_list {self.free_list})")

    def free(self, off, n):
        self.free_list.append((off, n))
        self.free_list.sort()
        out = []
        for s, l in self.free_list:
            if out and out[-1][0] + out[-1][1] == s:
                out[-1] = (out[-1][0], out[-1][1] + l)
            else:
                out.append((s, l))
        self.free_list = [tuple(x) for x in out]


class Buf:

    def __init__(self, arena, base, nch, width, lead=0, name=""):
        self.arena = arena
        self.t = arena.t
        self.base = base
        self.nch = nch
        self.width = width
        self.lead = lead
        self.stride = width + lead
        self.name = name

    def ap(self, c, lo, hi, dp=0, dn=128):
        off = self.base + c * self.stride + self.lead
        return self.t[dp:dp + dn, off + lo: off + hi]

    def keys(self, c, lo, hi):
        g = Arena.GRAIN
        off = self.base + c * self.stride + self.lead
        return [(self.arena.name, i)
                for i in range((off + lo) // g, (off + hi + g - 1) // g)]

    def kall(self):
        g = Arena.GRAIN
        n = self.nch * self.stride
        return [(self.arena.name, i)
                for i in range(self.base // g, (self.base + n + g - 1) // g)]

    def full_ap(self):
        return self.t[:, self.base: self.base + self.nch * self.stride]


def dkey(name, *idx):
    return ("dram:" + name,) + idx



def _chunk_cols(vec, n):
    v = np.asarray(vec, np.float32).reshape(n, 128).T
    return np.ascontiguousarray(v)


def _t2(a):
    return np.ascontiguousarray(np.asarray(a, np.float32).T)


def stage_inputs(inputs, geom):
    g = geom
    x = np.asarray(inputs["L0_attn_x"], np.float32)
    Bb, T, _ = x.shape
    assert T == g.seq and Bb == 2

    shared = {}
    shared["wq1t"] = _t2(inputs["L0_attn_Wq1"])
    shared["wk1t"] = _t2(inputs["L0_attn_Wk1"])
    shared["wk2t"] = _t2(inputs["L0_attn_Wk2"])
    shared["wprojt"] = _t2(inputs["L0_attn_Wproj"])
    shared["bq1c"] = _chunk_cols(inputs["L0_attn_bq1"], NCH)
    shared["bk1c"] = _chunk_cols(inputs["L0_attn_bk1"], NCH)
    shared["bk2c"] = _chunk_cols(inputs["L0_attn_bk2"], NCH)
    shared["bprojc"] = _chunk_cols(inputs["L0_attn_bproj"], NCH)

    ident = np.eye(128, dtype=np.float32)
    shared["identf"] = ident
    shared["identb"] = ident.astype(BF16NP)
    r = np.arange(128)
    tri_first_s = np.where(r[:, None] > r[None, :], NEG, 0.0).astype(np.float32)
    tri_last_s = np.where(r[:, None] < r[None, :], NEG, 0.0).astype(np.float32)
    shared["tri_first_s"] = tri_first_s.astype(BF16NP)
    shared["tri_last_s"] = tri_last_s.astype(BF16NP)
    shared["tri_first_st"] = np.ascontiguousarray(tri_first_s.T).astype(BF16NP)
    shared["tri_last_st"] = np.ascontiguousarray(tri_last_s.T).astype(BF16NP)
    shared["pexprep"] = np.zeros((128, 8), np.float32)

    per_core = []
    for core in range(N_CORES):
        b, j = core // 4, core % 4
        a = j * g.t_own
        lo, hi = a - g.pad_l, a + g.t_own + g.pad_r
        xs = np.zeros((g.t_buf, D), np.float32)
        s_lo, s_hi = max(lo, 0), min(hi, T)
        xs[s_lo - lo: s_hi - lo] = x[b, s_lo:s_hi]
        m = {"xfm": np.ascontiguousarray(xs.T)}

        pm = np.zeros(8, np.float32)
        pexp = np.zeros(8, np.float32)
        for i in range(4):
            if i < j:
                pm[4 * b + i] = 1.0
                pexp[4 * b + i] = float(j - 1 - i)
        m["pmask16"] = np.ascontiguousarray(
            np.broadcast_to(np.repeat(pm, 16)[None, :], (128, 128))).astype(np.float32)
        m["pmask8"] = np.ascontiguousarray(
            np.broadcast_to(np.repeat(pm, 8)[None, :], (128, 64))).astype(np.float32)
        m["pmaskrep"] = np.ascontiguousarray(
            np.broadcast_to(pm[None, :], (128, 8))).astype(np.float32)
        m["pexprep"] = np.ascontiguousarray(
            np.broadcast_to(pexp[None, :], (128, 8))).astype(np.float32)
        m["edge0"] = np.full((128, 1), 0.0 if j == 0 else 1.0, np.float32)
        m["edge3"] = np.full((128, 1), 0.0 if j == 3 else 1.0, np.float32)

        dead = np.zeros((128, g.n_tiles), np.float32)
        for tile in range(g.n_tiles):
            tok0 = lo + tile * 128
            for p in range(128):
                tok = tok0 + p
                if tok < 0 or tok >= T:
                    dead[p, tile] = NEG
        m["deadcol"] = dead
        per_core.append(m)

    HEADS, DH, WIN = 8, 64, 512
    INNER = HEADS * DH
    HID = 2730
    HIDP = 2816
    wqkv = np.asarray(inputs["L0_ffn_wqkv"], np.float32)
    wo_ = np.asarray(inputs["L0_ffn_wo"], np.float32)
    w1_ = np.asarray(inputs["L0_ffn_w1"], np.float32)
    b1_ = np.asarray(inputs["L0_ffn_b1"], np.float32)
    w2_ = np.asarray(inputs["L0_ffn_w2"], np.float32)
    b2_ = np.asarray(inputs["L0_ffn_b2"], np.float32)
    qs_ = np.asarray(inputs["L0_ffn_q_scale"], np.float32)
    ks_ = np.asarray(inputs["L0_ffn_k_scale"], np.float32)
    lg_ = np.asarray(inputs["L0_ffn_ln_g"], np.float32)
    lb_ = np.asarray(inputs["L0_ffn_ln_b"], np.float32)
    for li in range(2):
        shared[f"wqkvqk{li}"] = np.ascontiguousarray(wqkv[li][:, :2 * INNER])
        shared[f"wqkvv{li}"] = np.ascontiguousarray(wqkv[li][:, 2 * INNER:])
        shared[f"wo{li}"] = np.ascontiguousarray(wo_[li])
        w1a = np.zeros((D, HIDP), np.float32)
        w1g = np.zeros((D, HIDP), np.float32)
        w1a[:, :HID] = w1_[li][:, :HID]
        w1g[:, :HID] = w1_[li][:, HID:]
        w1pb = np.zeros((22, D, 256), np.float32)
        for i in range(22):
            w1pb[i, :, :128] = w1a[:, i * 128:(i + 1) * 128]
            w1pb[i, :, 128:] = w1g[:, i * 128:(i + 1) * 128]
        shared[f"w1pb{li}"] = np.ascontiguousarray(
            w1pb.reshape(22, 8, 128, 256).transpose(0, 2, 1, 3)).reshape(
                22 * 128, 2048)
        b1a = np.zeros((HIDP,), np.float32)
        b1g = np.zeros((HIDP,), np.float32)
        b1a[:HID] = b1_[li][:HID]
        b1g[:HID] = b1_[li][HID:]
        shared[f"b1a{li}"] = _chunk_cols(b1a, 22)
        shared[f"b1g{li}"] = _chunk_cols(b1g, 22)
        w2p = np.zeros((HIDP, D), np.float32)
        w2p[:HID] = w2_[li]
        w2pcb = np.zeros((4, HIDP, 256), np.float32)
        for pr in range(4):
            w2pcb[pr] = w2p[:, pr * 256:(pr + 1) * 256]
        shared[f"w2pcb{li}"] = np.ascontiguousarray(
            w2pcb.reshape(4, 22, 128, 256).transpose(0, 2, 1, 3)).reshape(
                512, 5632)
        shared[f"b2c{li}"] = _chunk_cols(b2_[li], NCH)
        qsind = np.zeros((16, 4 * 128), np.float32)
        ksind = np.zeros((16, 4 * 128), np.float32)
        hind = np.zeros((16, 4 * 128), np.float32)
        for c in range(4):
            for p in range(128):
                hq = 2 * c + p // 64
                qsind[hq, c * 128 + p] = 8.0 * qs_[li][p % 64]
                ksind[8 + hq, c * 128 + p] = ks_[li][p % 64]
                hind[hq, c * 128 + p] = 1.0
        shared[f"qsind{li}"] = qsind.astype(BF16NP)
        shared[f"ksind{li}"] = ksind.astype(BF16NP)
        if li == 0:
            shared["hind16"] = hind.astype(BF16NP)
            osel = np.zeros((16, 8 * 128), np.float32)
            for h in range(HEADS):
                osel[h, h * 128:(h + 1) * 128] = 1.0
            shared["osel"] = osel.astype(BF16NP)
            hsel = np.zeros((128, 8 * 16), np.float32)
            for oc in range(8):
                for p in range(128):
                    hh = 2 * (oc % 4) + p // 64
                    col = hh if oc < 4 else 8 + hh
                    hsel[p, oc * 16 + col] = 1.0
            shared["hsel"] = hsel.astype(BF16NP)
        lgb = np.zeros((2, NCH * 128), np.float32)
        lgb[0] = lg_[li]
        lgb[1] = lb_[li]
        shared[f"lgb{li}"] = lgb.astype(BF16NP)
        shared[f"lgc{li}"] = _chunk_cols(lg_[li], NCH)

    shared["wkt"] = _t2(inputs["L1_attn_Wk"])
    shared["wvt"] = _t2(inputs["L1_attn_Wv"])
    shared["wrt"] = _t2(inputs["L1_attn_Wr"])
    shared["wot"] = _t2(inputs["L1_attn_Wo"])
    shared["sdc"] = _chunk_cols(inputs["L1_attn_spatial_decay"], NCH)
    shared["sfc"] = _chunk_cols(inputs["L1_attn_spatial_first"], NCH)
    shared["mixkc"] = _chunk_cols(inputs["L1_attn_mix_k"], NCH)
    shared["mixvc"] = _chunk_cols(inputs["L1_attn_mix_v"], NCH)
    shared["mixrc"] = _chunk_cols(inputs["L1_attn_mix_r"], NCH)
    wfcT = _t2(inputs["L1_ffn_W_fc"])
    shared["wfctb"] = np.ascontiguousarray(
        wfcT.reshape(8, 128, 32, 128).transpose(2, 1, 0, 3)).reshape(
            32 * 128, 1024)
    shared["wprojT2"] = _t2(inputs["L1_ffn_W_proj"])
    shared["lnwc"] = _chunk_cols(inputs["L1_ffn_ln_w"], 32)
    shared["lnbc"] = _chunk_cols(inputs["L1_ffn_ln_b"], 32)

    half = np.arange(0, DH, 2, np.float32)
    inv_freq = 1.0 / 10000.0 ** (half / DH)
    base = (half + 0.4 * DH) / (1.4 * DH)
    seq = g.seq
    for core, m in enumerate(per_core):
        b, j = core // 4, core % 4
        lo = j * g.t_own - g.pad_l
        t_abs = np.clip(np.arange(lo, lo + g.t_buf, dtype=np.float32), 0, seq - 1)
        freqs = t_abs[None, :] * inv_freq[:, None]
        power = (t_abs - seq // 2) / float(WIN // 2)
        scale = base[:, None] ** power[None, :]
        cos = np.cos(freqs)
        sin = np.sin(freqs)
        def expand(tab32, signed=False):
            t64 = np.concatenate([tab32, tab32], axis=0)
            if signed:
                t64 = np.concatenate([-tab32, tab32], axis=0)
            return np.concatenate([t64, t64], axis=0)
        m["rotq_cos"] = np.ascontiguousarray(expand(cos * scale)).astype(BF16NP)
        m["rotq_sin"] = np.ascontiguousarray(
            expand(sin * scale, signed=True)).astype(BF16NP)
        m["rotk_cos"] = np.ascontiguousarray(expand(cos / scale)).astype(BF16NP)
        m["rotk_sin"] = np.ascontiguousarray(
            expand(sin / scale, signed=True)).astype(BF16NP)

    in_maps = [dict(shared, **pc) for pc in per_core]
    return in_maps


def unstage_output(results, geom):
    g = geom
    out = np.empty((2, g.seq, D), np.float32)
    for core, r in enumerate(results):
        b, j = core // 4, core % 4
        out[b, j * g.t_own:(j + 1) * g.t_own] = r["out_fm"].T
    return out



class Program:
    def __init__(self, geom, taps=()):
        self.g = geom
        self.taps = set(taps)
        self.nc = bass.Bass("TRN2", target_bir_lowering=False, debug=False)
        self.S = Sched()
        self.stack = ExitStack()
        self.sbuf_bytes = 0
        self._tap_outputs = []

    def din(self, name, shape, dtype=F32):
        return self.nc.dram_tensor(name, list(shape), dtype,
                                   kind="ExternalInput").ap()

    def dout(self, name, shape, dtype=F32):
        return self.nc.dram_tensor(name, list(shape), dtype,
                                   kind="ExternalOutput").ap()

    def dram(self, name, shape, dtype=F32):
        return self.nc.dram_tensor(name, list(shape), dtype).ap()

    def sbuf(self, name, shape, dtype=F32):
        t = self.stack.enter_context(self.nc.sbuf_tensor(name, list(shape), dtype))
        self.sbuf_bytes += shape[1] * mybir.dt.size(dtype)
        return t

    def mk_arenas(self, f32_elems, bf16_elems):
        self.af32 = Arena(self.sbuf("af32", [128, f32_elems]), "af32", f32_elems)
        self.ab16 = Arena(self.sbuf("ab16", [128, bf16_elems], BF16), "ab16",
                          bf16_elems)

    def abuf(self, name, nch, width, dtype=F32, lead=0, side="lo"):
        ar = self.af32 if dtype == F32 else self.ab16
        base = ar.alloc(nch * (width + lead), side=side)
        return Buf(ar, base, nch, width, lead=lead, name=name)

    def bfree(self, buf):
        buf.arena.free(buf.base, buf.nch * buf.stride)

    def psum(self, name, cols=512):
        return self.stack.enter_context(
            self.nc.psum_tensor(name, [128, cols], F32))

    def dma(self, q, out_ap, in_ap, reads, writes, tag="", slow_ok=False):
        self.S.op(q, lambda e, o=out_ap, i=in_ap, sl=slow_ok:
                      e.dma_start(out=o, in_=i, allow_slow_non_contiguous=sl),
                  reads=reads, writes=writes, kind="dma", tag=tag)

    def mm_group(self, psname, ops, tag=""):
        banks = [c0 // 512 for (_, c0, _, _, _, _) in ops]
        first = {}
        last = {}
        for i, b in enumerate(banks):
            first.setdefault(b, i)
            last[b] = i
        for i, (out_ap, c0, lhsT, rhs, lkeys, rkeys) in enumerate(ops):
            b = banks[i]
            self.mm(psname, out_ap, lhsT, rhs, first[b] == i, last[b] == i,
                    lkeys, rkeys, tag=tag)

    def mm_groupb(self, entries, tag=""):
        banks = [(e[0], e[2] // 512) for e in entries]
        first = {}
        last = {}
        for i, b in enumerate(banks):
            first.setdefault(b, i)
            last[b] = i
        for i, (psn, out_ap, c0, lhsT, rhs, lk, rk) in enumerate(entries):
            b = banks[i]
            self.mm(psn, out_ap, lhsT, rhs, first[b] == i, last[b] == i,
                    lk, rk, tag=tag)

    def mm(self, psname, out_ap, lhsT, rhs, start, stop, lkeys, rkeys, tag=""):
        self.S.op("tensor",
                  lambda e, o=out_ap, l=lhsT, r=rhs, st=start, sp=stop:
                      e.matmul(o, l, r, start=st, stop=sp),
                  reads=list(lkeys) + list(rkeys), writes=[("ps", psname)],
                  accum=not start, tag=tag)

    def cbias(self, value):
        if not hasattr(self, "_cpool"):
            self._cpool = self.sbuf("cpool", [128, 32])
            self._cpool_map = {}
        if value not in self._cpool_map:
            i = len(self._cpool_map)
            assert i < 32
            self.memset(self._cpool[:, i:i + 1], value, [("sc", "cpool")],
                        tag="cpool")
            self._cpool_map[value] = i
        i = self._cpool_map[value]
        return self._cpool[:, i:i + 1]

    def act(self, out_ap, in_ap, func, rkeys, wkeys, bias=0.0, scale=1.0,
            eng="scalar", tag=""):
        if isinstance(bias, float) and bias != 0.0:
            bias = self.cbias(bias)[0:out_ap.shape[0], :]
            rkeys = list(rkeys) + [("sc", "cpool")]
        elif not isinstance(bias, float) and bias.shape[0] != out_ap.shape[0]:
            bias = bias[0:out_ap.shape[0], :]
        self.S.op(eng,
                  lambda e, o=out_ap, i=in_ap, f=func, b=bias, s=scale:
                      e.activation(o, i, f, bias=b, scale=s),
                  reads=rkeys, writes=wkeys, tag=tag)

    def tt(self, out_ap, a_ap, b_ap, op, rkeys, wkeys, eng="vector", tag=""):
        self.S.op(eng,
                  lambda e, o=out_ap, a=a_ap, b=b_ap, p=op:
                      e.tensor_tensor(o, a, b, p),
                  reads=rkeys, writes=wkeys, tag=tag)

    def ts(self, out_ap, in_ap, s1, op0, rkeys, wkeys, s2=None, op1=ALU.bypass,
           eng="vector", tag=""):
        self.S.op(eng,
                  lambda e, o=out_ap, i=in_ap, a=s1, b=s2, p0=op0, p1=op1:
                      e.tensor_scalar(o, i, a, b, p0, p1),
                  reads=rkeys, writes=wkeys, tag=tag)

    def stt(self, out_ap, in0_ap, scalar, in1_ap, op0, op1, rkeys, wkeys,
            eng="vector", tag=""):
        self.S.op(eng,
                  lambda e, o=out_ap, i0=in0_ap, s=scalar, i1=in1_ap, p0=op0,
                         p1=op1:
                      e.scalar_tensor_tensor(o, i0, s, i1, p0, p1),
                  reads=rkeys, writes=wkeys, tag=tag)

    def copy(self, out_ap, in_ap, rkeys, wkeys, eng="vector", tag=""):
        if eng == "scalar":
            self.S.op(eng, lambda e, o=out_ap, i=in_ap: e.copy(o, i),
                      reads=rkeys, writes=wkeys, tag=tag)
        else:
            self.S.op(eng, lambda e, o=out_ap, i=in_ap: e.tensor_copy(o, i),
                      reads=rkeys, writes=wkeys, tag=tag)

    def red(self, out_ap, in_ap, op, rkeys, wkeys, negate=False, eng="vector",
            tag=""):
        self.S.op(eng,
                  lambda e, o=out_ap, i=in_ap, p=op, n=negate:
                      e.tensor_reduce(o, i, AXL.X, p, negate=n),
                  reads=rkeys, writes=wkeys, tag=tag)

    def scan(self, out_ap, d0_ap, d1_ap, init, rkeys, wkeys, op0=ALU.mult,
             op1=ALU.add, eng="vector", tag=""):
        self.S.op(eng,
                  lambda e, o=out_ap, a=d0_ap, b=d1_ap, i=init, p0=op0, p1=op1:
                      e.tensor_tensor_scan(o, a, b, i, p0, p1),
                  reads=rkeys, writes=wkeys, tag=tag)

    def memset(self, ap, const, wkeys, eng="vector", tag=""):
        self.S.op(eng, lambda e, a=ap, c=const: e.memset(a, c),
                  reads=(), writes=wkeys, tag=tag)

    def recip(self, out_ap, in_ap, rkeys, wkeys, tag=""):
        self.S.op("vector", lambda e, o=out_ap, i=in_ap: e.reciprocal(o, i),
                  reads=rkeys, writes=wkeys, tag=tag)

    def transpose(self, out_psname, out_ap, in_ap, rkeys, tag=""):
        self.S.op("tensor",
                  lambda e, o=out_ap, i=in_ap, d=self.sb_identf[:]:
                      e.transpose(o, i, d),
                  reads=list(rkeys) + [("sc", "identf")],
                  writes=[("ps", out_psname)], tag=tag)

    def tap(self, name, ap, shape, rkeys, dtype=F32):
        if name not in self.taps:
            return
        out = self.dout("tap_" + name, list(shape), dtype)
        self.dma("gpsimd", out[:, :], ap, rkeys, [dkey("tap_" + name)],
                 tag="tap")

    def declare(self):
        g = self.g
        if g.t_own <= 512:
            self.mk_arenas(10240, 65536)
        else:
            self.mk_arenas(8448, 72192)
        self.xfm = self.din("xfm", [D, g.t_buf])
        self.wq1t = self.din("wq1t", [D, D])
        self.wk1t = self.din("wk1t", [D, D])
        self.wk2t = self.din("wk2t", [D, D])
        self.wprojt = self.din("wprojt", [D, D])
        self.sb_bias = {}
        for n in ("bq1c", "bk1c", "bk2c", "bprojc"):
            ap = self.din(n, [128, NCH])
            t = self.sbuf("sb_" + n, [128, NCH])
            self.sb_bias[n] = t
            self.dma("sync", t[:], ap[:, :], [dkey(n)], [("sc", n)])
        for n, sh in (("identf", [128, 128]), ("pmask16", [128, 128]),
                      ("pmask8", [128, 64]), ("pmaskrep", [128, 8]),
                      ("pexprep", [128, 8]), ("edge0", [128, 1]),
                      ("edge3", [128, 1]), ("deadcol", [128, g.n_tiles])):
            ap = self.din(n, sh)
            t = self.sbuf("sb_" + n, sh)
            setattr(self, "sb_" + n, t)
            self.dma("sync", t[:], ap[:, :], [dkey(n)], [("sc", n)])

        for li in range(2):
            for n, sh in ((f"wqkvqk{li}", [D, 1024]), (f"wqkvv{li}", [D, 512]),
                          (f"wo{li}", [512, D]), (f"w1pb{li}", [22 * 128, 2048]),
                          (f"w2pcb{li}", [4 * 128, 22 * 256]),
                          (f"lgb{li}", [2, NCH * 128]),
                          (f"lgc{li}", [128, NCH])):
                setattr(self, n, self.din(n, sh,
                                          BF16 if n.startswith("lgb") else F32))
            for n in (f"b1a{li}", f"b1g{li}"):
                ap = self.din(n, [128, 22])
                t = self.sbuf("sb_" + n, [128, 22])
                self.sb_bias[n] = t
                self.dma("sync", t[:], ap[:, :], [dkey(n)], [("sc", n)])
            ap = self.din(f"b2c{li}", [128, NCH])
            t = self.sbuf(f"sb_b2c{li}", [128, NCH])
            self.sb_bias[f"b2c{li}"] = t
            self.dma("sync", t[:], ap[:, :], [dkey(f"b2c{li}")],
                     [("sc", f"b2c{li}")])
            for n in (f"qsind{li}", f"ksind{li}"):
                ap = self.din(n, [16, 512], BF16)
                t = self.sbuf("sb_" + n, [16, 512], BF16)
                setattr(self, "sb_" + n, t)
                self.dma("sync", t[:], ap[:, :], [dkey(n)], [("sc", n)])
        for n, sh in (("hind16", [16, 512]), ("osel", [16, 1024]),
                      ("hsel", [128, 128]), ("tri_first_s", [128, 128]),
                      ("tri_last_s", [128, 128]), ("tri_first_st", [128, 128]),
                      ("tri_last_st", [128, 128]), ("identb", [128, 128])):
            ap = self.din(n, sh, BF16)
            t = self.sbuf("sb_" + n, sh, BF16)
            setattr(self, "sb_" + n, t)
            self.dma("sync", t[:], ap[:, :], [dkey(n)], [("sc", n)])
        for n in ("rotq_cos", "rotq_sin", "rotk_cos", "rotk_sin"):
            setattr(self, n, self.din(n, [128, g.t_buf], BF16))

        self.onesrow_b = self.sbuf("onesrow_b", [1, 128], BF16)
        self.onescol_b = self.sbuf("onescol_b", [128, 1], BF16)
        self.ones128_b = self.sbuf("ones128_b", [128, 128], BF16)
        self.memset(self.onesrow_b[:], 1.0, [("sc", "onesrow_b")])
        self.memset(self.onescol_b[:], 1.0, [("sc", "onescol_b")])
        self.memset(self.ones128_b[:], 1.0, [("sc", "ones128_b")])

        for n in ("wkt", "wvt", "wrt", "wot"):
            setattr(self, n, self.din(n, [D, D]))
        self.wfctb = self.din("wfctb", [32 * 128, 1024])
        self.wprojT2 = self.din("wprojT2", [4096, D])
        for n in ("sdc", "sfc", "mixkc", "mixvc", "mixrc"):
            ap = self.din(n, [128, NCH])
            t = self.sbuf("sb_" + n, [128, NCH])
            setattr(self, "sb_" + n, t)
            self.dma("sync", t[:], ap[:, :], [dkey(n)], [("sc", n)])
        for n in ("lnwc", "lnbc"):
            ap = self.din(n, [128, 32])
            t = self.sbuf("sb_" + n, [128, 32])
            setattr(self, "sb_" + n, t)
            self.dma("sync", t[:], ap[:, :], [dkey(n)], [("sc", n)])

        self.out_fm = self.dout("out_fm", [D, g.t_own])

        self.psA = self.psum("psA")
        self.psB = self.psum("psB")
        self.psC = self.psum("psC")
        self.psD = self.psum("psD")
        self.psBC = self.psum("psBC")
        self.psBC2 = self.psum("psBC2")
        self.psSC = self.psum("psSC", 1024)

        self.ones_col = self.sbuf("ones_col", [128, 1])
        self.ones128 = self.sbuf("ones128", [128, 128])
        self.sc_small = self.sbuf("sc_small", [128, 256])
        self.memset(self.ones_col[:], 1.0, [("sc", "ones_col")])
        self.memset(self.ones128[:], 1.0, [("sc", "ones128")])

    def wfull(self, name, dram_ap, rows=D):
        cols = dram_ap.shape[1]
        nk = rows // 128
        w = self.abuf("w_" + name, nk, cols, BF16)
        self.dma("gpsimd",
                 w.full_ap().rearrange("p (a c) -> p a c", a=nk),
                 dram_ap.rearrange("(a p) c -> p a c", p=128),
                 [dkey(name)], w.kall(), tag="wload_" + name)
        return w

    def mm_fm(self, w, rhs_buf, rhs_lo, rhs_hi, evac, rhs_shift=0,
              psnames=("psA", "psB"), pss=None, nk=None, out_chunks=None,
              tag=""):
        if pss is None:
            pss = (self.psA, self.psB)
        if nk is None:
            nk = w.nch
        if out_chunks is None:
            out_chunks = range(w.width // 128)
        npan = (rhs_hi - rhs_lo + 511) // 512
        i = 0
        for oc in out_chunks:
            for p in range(npan):
                lo = rhs_lo + 512 * p
                hi = min(lo + 512, rhs_hi)
                psn, ps = psnames[i % 2], pss[i % 2]
                i += 1
                for kc in range(nk):
                    self.mm(psn, ps[:, 0:hi - lo],
                            w.ap(kc, oc * 128, oc * 128 + 128),
                            rhs_buf.ap(kc, lo + rhs_shift, hi + rhs_shift),
                            kc == 0, kc == nk - 1,
                            w.keys(kc, oc * 128, oc * 128 + 128),
                            rhs_buf.keys(kc, lo + rhs_shift, hi + rhs_shift),
                            tag=tag + "mm")
                evac(oc, lo, hi, psn, ps)

    def phase_xn(self):
        g = self.g
        self.xn_spill = self.dram("xn_spill", [D, g.t_buf + 1], BF16)
        zc = self.sbuf("zcol", [128, 1], BF16)
        self.memset(zc[:], 0.0, [("sc", "zcol")], tag="zc")
        for c in range(NCH):
            self.dma("scalar", self.xn_spill[c * 128:(c + 1) * 128, 0:1],
                     zc[:], [("sc", "zcol")], [dkey("xn_spill", c, -1)],
                     tag="xnz", slow_ok=True)

        xpan = self.abuf("xpan", NCH, 256)
        xsq = self.abuf("xsq", NCH, 256)
        xnp = self.abuf("xnp", NCH, 256, BF16)
        rstd = self.sbuf("xn_rstd", [128, 512])

        n_pan = (g.t_buf + 255) // 256
        for p in range(n_pan):
            lo = p * 256
            hi = min(lo + 256, g.t_buf)
            w = hi - lo
            for c in range(NCH):
                self.dma("sync", xpan.ap(c, 0, w),
                         self.xfm[c * 128:(c + 1) * 128, lo:hi],
                         [dkey("xfm")], xpan.keys(c, 0, w), tag="xload")
                self.act(xsq.ap(c, 0, w), xpan.ap(c, 0, w), AF.Square,
                         xpan.keys(c, 0, w), xsq.keys(c, 0, w), tag="xsq")
            for c in range(NCH):
                self.mm("psBC", self.psBC[:, 0:w], self.ones128[:],
                        xpan.ap(c, 0, w), c == 0, c == NCH - 1,
                        [("sc", "ones128")], xpan.keys(c, 0, w), tag="xs1")
                self.mm("psBC2", self.psBC2[:, 0:w], self.ones128[:],
                        xsq.ap(c, 0, w), c == 0, c == NCH - 1,
                        [("sc", "ones128")], xsq.keys(c, 0, w), tag="xs2")
            self.act(rstd[:, 0:w], self.psBC[:, 0:w], AF.Square,
                     [("ps", "psBC")], [("sc", "xn_rstd")], tag="s2")
            self.stt(rstd[:, 0:w], rstd[:, 0:w], -1.0 / D, self.psBC2[:, 0:w],
                     ALU.mult, ALU.add, [("sc", "xn_rstd"), ("ps", "psBC2")],
                     [("sc", "xn_rstd")], tag="var")
            self.act(rstd[:, 0:w], rstd[:, 0:w], AF.Ln, [("sc", "xn_rstd")],
                     [("sc", "xn_rstd")], bias=1e-5, scale=1.0 / D, tag="ln")
            self.act(rstd[:, 0:w], rstd[:, 0:w], AF.Exp, [("sc", "xn_rstd")],
                     [("sc", "xn_rstd")], scale=-0.5, tag="exp")
            for c in range(NCH):
                self.stt(xsq.ap(c, 0, w), self.psBC[:, 0:w], -1.0 / D,
                         xpan.ap(c, 0, w), ALU.mult, ALU.add,
                         [("ps", "psBC")] + xpan.keys(c, 0, w),
                         xsq.keys(c, 0, w), tag="xcen")
                self.tt(xnp.ap(c, 0, w), xsq.ap(c, 0, w), rstd[:, 0:w],
                        ALU.mult, xsq.keys(c, 0, w) + [("sc", "xn_rstd")],
                        xnp.keys(c, 0, w), tag="xnorm")
                self.dma("scalar", self.xn_spill[c * 128:(c + 1) * 128,
                                                 1 + lo:1 + hi],
                         xnp.ap(c, 0, w), xnp.keys(c, 0, w),
                         [dkey("xn_spill", c, lo // 512)], tag="xnsp")
        self.bfree(xpan)
        self.bfree(xsq)
        self.bfree(xnp)

    def load_xn(self, buf, c, lo, hi, tag="xnld"):
        self.dma("sync", buf.ap(c, 0, hi - lo),
                 self.xn_spill[c * 128:(c + 1) * 128, 1 + lo:1 + hi],
                 [dkey("xn_spill", c, max(lo, 0) // 512),
                  dkey("xn_spill", c, max(hi - 1, 0) // 512),
                  dkey("xn_spill", c, -1)],
                 buf.keys(c, 0, hi - lo), tag=tag)

    def load_spill(self, buf, c, lo, hi, src_ap, name, off, tag="spld"):
        c0, c1 = off + lo, off + hi
        self.dma("sync", buf.ap(c, 0, hi - lo),
                 src_ap[c * 128:(c + 1) * 128, c0:c1],
                 [dkey(name, c, c0 // 512), dkey(name, c, (c1 - 1) // 512),
                  dkey(name, c, -1)],
                 buf.keys(c, 0, hi - lo), tag=tag)

    def allgather(self, idx, in_sbuf, ncols, rkeys):
        cin = self.dram(f"cc{idx}_in", [128, ncols])
        cout = self.dram(f"cc{idx}_out", [8 * 128, ncols])
        self.dma("gpsimd", cin[:, :], in_sbuf[:, 0:ncols], rkeys,
                 [dkey(f"cc{idx}_in")], tag=f"ag{idx}w")
        self.S.op("gpsimd",
                  lambda e, ci=cin, co=cout: e.collective_compute(
                      "AllGather", ALU.bypass,
                      replica_groups=[list(range(N_CORES))],
                      ins=[ci.opt()], outs=[co.opt()]),
                  reads=[dkey(f"cc{idx}_in")], writes=[dkey(f"cc{idx}_out")],
                  kind="cc", tag=f"ag{idx}")
        agg = self.sbuf(f"agg{idx}", [128, ncols * 8])
        for r in range(8):
            self.dma("gpsimd", agg[:, r * ncols:(r + 1) * ncols],
                     cout[r * 128:(r + 1) * 128, :],
                     [dkey(f"cc{idx}_out")], [("sc", f"agg{idx}")],
                     tag=f"ag{idx}r")
        return agg

    def carry_cols(self, agg, ncols, wname, weights=None):
        carry = self.sbuf(wname + "_c", [128, ncols])
        masked = self.sbuf(wname + "_m", [128, ncols * 8])
        kagg = ("sc", agg.name)
        km = ("sc", wname + "_m")
        if weights is None:
            pmsk = self.sb_pmask16 if ncols == 16 else self.sb_pmask8
            pk = "pmask16" if ncols == 16 else "pmask8"
            assert ncols in (8, 16)
            self.tt(masked[:], agg[:], pmsk[:, 0:ncols * 8],
                    ALU.mult, [kagg, ("sc", pk)], [km], tag=wname)
            self.red(carry[:].rearrange("p (one c) -> p c one", one=1),
                     masked[:].rearrange("p (r c) -> p c r", r=8), ALU.add,
                     [km], [("sc", wname + "_c")], tag=wname)
        else:
            w_ap, wk = weights
            cm = agg[:].rearrange("p (r c) -> p c r", r=8)
            w3 = w_ap.rearrange("p (c r) -> p c r", r=8)
            self.tt(masked[:, 0:64].rearrange("p (c r) -> p c r", r=8),
                    cm[:, 0:8, :], w3, ALU.mult, [kagg, wk], [km], tag=wname)
            self.tt(masked[:, 64:128].rearrange("p (c r) -> p c r", r=8),
                    cm[:, 8:16, :], w3, ALU.mult, [kagg, wk], [km], tag=wname)
            self.red(carry[:].rearrange("p (one c) -> p c one", one=1),
                     masked[:].rearrange("p (c r) -> p c r", r=8), ALU.add,
                     [km], [("sc", wname + "_c")], tag=wname)
        return carry

    def phase_l0_attn(self):
        g = self.g
        SC_LO, SC_HI = g.own_lo, g.t_buf
        NSC = SC_HI - SC_LO
        P2N = g.t_own + 2
        P3N = g.t_own + 1
        NOS = g.t_own + 2

        W2 = g.t_own + 2
        self.k_spill = self.dram("k_spill", [D, W2], BF16)
        K2f = self.abuf("l0_K2f", NCH, W2, BF16)
        Q2f = self.abuf("l0_Q2f", NCH, W2 + 2, BF16)
        self.s_spill = self.dram("s_spill", [D, NOS], BF16)

        PW = 256
        xnps = [self.abuf(f"l0_xnp{k}", NCH, PW + 1, BF16) for k in range(2)]
        xnp = xnps[0]
        fA = self.abuf("l0_fA", NCH, PW + 1)
        fB = self.abuf("l0_fB", NCH, PW + 1)
        fC = self.abuf("l0_fC", NCH, PW + 1)
        Qp = self.abuf("l0_Qp", NCH, PW, BF16)
        Kpans = [self.abuf(f"l0_Kp{k}", NCH, PW, BF16) for k in range(2)]
        Kpan = Kpans[0]
        OARp = self.abuf("l0_OARp", NCH, PW, BF16, lead=1)
        OMAp = self.abuf("l0_OMAp", NCH, PW, BF16)
        sp_pan = self.abuf("l0_sp", NCH, PW, BF16)

        chains = self.sbuf("l0_chains", [128, 64])
        kch = lambda c: chains[:, c:c + 1]
        kvch = lambda c: chains[:, 8 + c:9 + c]
        s2ch = lambda c: chains[:, 16 + c:17 + c]
        oarl = lambda c: chains[:, 24 + c:25 + c]
        runsum = chains[:, 32:48]
        ag2in = chains[:, 48:56]
        KCH = ("sc", "l0_chains")
        qb = self.sbuf("l0_qb", [128, 16])
        KQB = ("sc", "l0_qb")
        self.memset(runsum, 0.0, [KCH], tag="rs0")

        wk1 = self.wfull("wk1t", self.wk1t)
        wk2 = self.wfull("wk2t", self.wk2t)
        self.k2bias = self.sbuf("k2bias", [128, NCH])
        self.ts(self.k2bias[:], self.sb_bias["bk2c"][:], 0.000625, ALU.mult,
                [("sc", "bk2c")], [("sc", "k2bias")], tag="k2b")
        n_own_pan = (g.t_own + PW - 1) // PW
        n_pan1 = (W2 + PW - 1) // PW
        for p in range(n_pan1):
            lo = SC_LO + p * PW
            hi = min(lo + PW, SC_LO + W2)
            w = hi - lo
            xnp = xnps[p % 2]
            Kpan = Kpans[p % 2]
            for c in range(NCH):
                self.load_xn(xnp, c, lo - 1, hi)
            for oc in range(NCH):
                psn, ps = ("psA", self.psA) if oc % 2 == 0 else ("psB", self.psB)
                for kc in range(NCH):
                    self.mm(psn, ps[:, 0:w], wk1.ap(kc, oc * 128, oc * 128 + 128),
                            xnp.ap(kc, 1, 1 + w), kc == 0, kc == NCH - 1,
                            wk1.keys(kc, oc * 128, oc * 128 + 128),
                            xnp.keys(kc, 1, 1 + w), tag="k1mm")
                self.act(Kpan.ap(oc, 0, w), ps[:, 0:w], AF.Exp,
                         [("ps", psn), ("sc", "bk1c")],
                         Kpan.keys(oc, 0, w),
                         bias=self.sb_bias["bk1c"][:, oc:oc + 1], tag="evk")
                i0_ = lo - SC_LO
                self.dma("scalar", self.k_spill[oc * 128:(oc + 1) * 128,
                                                i0_:i0_ + w],
                         Kpan.ap(oc, 0, w), Kpan.keys(oc, 0, w),
                         [dkey("k_spill", oc, i0_ // 512)], tag="ksp",
                         slow_ok=True)
            for oc in range(NCH):
                psn, ps = ("psC", self.psC) if oc % 2 == 0 else ("psD", self.psD)
                for kc in range(NCH):
                    self.mm(psn, ps[:, 0:w], wk2.ap(kc, oc * 128, oc * 128 + 128),
                            xnp.ap(kc, 0, w), kc == 0, kc == NCH - 1,
                            wk2.keys(kc, oc * 128, oc * 128 + 128),
                            xnp.keys(kc, 0, w), tag="k2mm")
                self.act(K2f.ap(oc, lo - SC_LO, hi - SC_LO), ps[:, 0:w],
                         AF.Sigmoid, [("ps", psn), ("sc", "k2bias")],
                         K2f.keys(oc, lo - SC_LO, hi - SC_LO),
                         bias=self.k2bias[:, oc:oc + 1], scale=0.000625,
                         tag="evk2")
            if lo < g.own_hi:
                ws = min(hi, g.own_hi) - lo
                for c in range(NCH):
                    self.tt(fA.ap(c, 0, ws), Kpan.ap(c, 0, ws),
                            xnp.ap(c, 1, 1 + ws), ALU.mult,
                            Kpan.keys(c, 0, ws) +
                            xnp.keys(c, 1, 1 + ws), fA.keys(c, 0, ws), tag="kv1")
                    self.red(fB.ap(c, 0, 1), fA.ap(c, 0, ws), ALU.add,
                             fA.keys(c, 0, ws), fB.keys(c, 0, 1), tag="rkv")
                    self.red(fC.ap(c, 0, 1), Kpan.ap(c, 0, ws),
                             ALU.add, Kpan.keys(c, 0, ws),
                             fC.keys(c, 0, 1), tag="rk")
                    self.tt(runsum[:, c:c + 1], runsum[:, c:c + 1],
                            fC.ap(c, 0, 1), ALU.add, [KCH] + fC.keys(c, 0, 1),
                            [KCH], tag="accK")
                    self.tt(runsum[:, 8 + c:9 + c], runsum[:, 8 + c:9 + c],
                            fB.ap(c, 0, 1), ALU.add, [KCH] + fB.keys(c, 0, 1),
                            [KCH], tag="accKV")
        self.bfree(wk2)
        self.bfree(wk1)

        agg1 = self.allgather(1, runsum, 16, [KCH])
        car1 = self.carry_cols(agg1, 16, "car1")
        k_car1 = ("sc", "car1_c")

        wq1 = self.wfull("wq1t", self.wq1t)
        xnp = xnps[0]
        for c in range(NCH):
            self.load_xn(xnp, c, g.own_lo - 2, g.own_lo)
        psn, ps = "psA", self.psA
        for oc in range(NCH):
            for kc in range(NCH):
                self.mm(psn, ps[:, 0:2], wq1.ap(kc, oc * 128, oc * 128 + 128),
                        xnp.ap(kc, 0, 2), kc == 0, kc == NCH - 1,
                        wq1.keys(kc, oc * 128, oc * 128 + 128),
                        xnp.keys(kc, 0, 2), tag="qbmm")
            self.act(qb[:, 2 * oc:2 * oc + 2], ps[:, 0:2], AF.Sigmoid,
                     [("ps", psn), ("sc", "bq1c")], [KQB],
                     bias=self.sb_bias["bq1c"][:, oc:oc + 1], tag="qbev")
        sc = self.sc_small
        ksc = ("sc", "sc_small")
        for c in range(NCH):
            self.copy(kch(c), car1[:, c:c + 1], [k_car1], [KCH], tag="ci1")
            self.copy(kvch(c), car1[:, 8 + c:9 + c], [k_car1], [KCH], tag="ci2")
            self.memset(s2ch(c), 0.0, [KCH], tag="ci3")
            self.ts(sc[:, c:c + 1], car1[:, c:c + 1], 1e-6, ALU.add,
                    [k_car1], [ksc], tag="rcA")
            self.recip(sc[:, 8 + c:9 + c], sc[:, c:c + 1], [ksc], [ksc],
                       tag="rcB")
            self.tt(sc[:, 16 + c:17 + c], car1[:, 8 + c:9 + c],
                    sc[:, 8 + c:9 + c], ALU.mult, [k_car1, ksc], [ksc],
                    tag="rcC")
            self.tt(oarl(c), sc[:, 16 + c:17 + c], qb[:, 2 * c + 1:2 * c + 2],
                    ALU.mult, [ksc, KQB], [KCH], tag="rcD")
            self.ts(fA.ap(c, 0, 2), qb[:, 2 * c:2 * c + 2], 0.03125, ALU.mult,
                    [KQB], fA.keys(c, 0, 2), tag="q2ba")
            self.stt(fA.ap(c, 0, 2), qb[:, 2 * c:2 * c + 2], 0.000625,
                     fA.ap(c, 0, 2), ALU.mult, ALU.min, [KQB] + fA.keys(c, 0, 2),
                     fA.keys(c, 0, 2), tag="q2bb")
            self.ts(Q2f.ap(c, 0, 2), fA.ap(c, 0, 2), self.sb_edge0[:, 0:1],
                    ALU.mult, fA.keys(c, 0, 2) + [("sc", "edge0")],
                    Q2f.keys(c, 0, 2), tag="q2bz")

        wproj = self.wfull("wprojt", self.wprojt)
        onesb = self.ones_col
        n_pan2 = (P2N + PW - 1) // PW
        for p in range(n_pan2):
            lo = SC_LO + p * PW
            hi = min(lo + PW, SC_LO + P2N)
            w = hi - lo
            i0 = lo - SC_LO
            xnp = xnps[p % 2]
            Kpan = Kpans[p % 2]
            for c in range(NCH):
                self.load_xn(xnp, c, lo - 1, hi)
            for oc in range(NCH):
                psn, ps = ("psA", self.psA) if oc % 2 == 0 else ("psB", self.psB)
                for kc in range(NCH):
                    self.mm(psn, ps[:, 0:w], wq1.ap(kc, oc * 128, oc * 128 + 128),
                            xnp.ap(kc, 1, 1 + w), kc == 0, kc == NCH - 1,
                            wq1.keys(kc, oc * 128, oc * 128 + 128),
                            xnp.keys(kc, 1, 1 + w), tag="qmm")
                self.act(Qp.ap(oc, 0, w), ps[:, 0:w], AF.Sigmoid,
                         [("ps", psn), ("sc", "bq1c")], Qp.keys(oc, 0, w),
                         bias=self.sb_bias["bq1c"][:, oc:oc + 1], tag="evq")
            for c in range(NCH):
                self.dma("sync", Kpan.ap(c, 0, w),
                         self.k_spill[c * 128:(c + 1) * 128, i0:i0 + w],
                         [dkey("k_spill", c, i0 // 512),
                          dkey("k_spill", c, (i0 + w) // 512)],
                         Kpan.keys(c, 0, w), tag="kld", slow_ok=True)
                kKf = Kpan.keys(c, 0, w)
                self.scan(fA.ap(c, 0, w), onesb[:].broadcast_to([128, w]),
                          Kpan.ap(c, 0, w), kch(c),
                          kKf + [KCH, ("sc", "ones_col")], fA.keys(c, 0, w),
                          tag="scK")
                self.copy(kch(c), fA.ap(c, w - 1, w), fA.keys(c, w - 1, w),
                          [KCH], tag="upK")
                self.act(fA.ap(c, 0, w), fA.ap(c, 0, w), AF.Ln,
                         fA.keys(c, 0, w), fA.keys(c, 0, w), bias=1e-6,
                         tag="lnK")
                self.act(fA.ap(c, 0, w), fA.ap(c, 0, w), AF.Exp,
                         fA.keys(c, 0, w), fA.keys(c, 0, w), scale=-1.0,
                         tag="rcK")
                self.tt(fB.ap(c, 0, w), Kpan.ap(c, 0, w),
                        xnp.ap(c, 1, 1 + w), ALU.mult,
                        kKf + xnp.keys(c, 1, 1 + w), fB.keys(c, 0, w),
                        tag="kv")
                self.scan(fB.ap(c, 0, w), onesb[:].broadcast_to([128, w]),
                          fB.ap(c, 0, w), kvch(c),
                          fB.keys(c, 0, w) + [KCH, ("sc", "ones_col")],
                          fB.keys(c, 0, w), tag="scKV")
                self.copy(kvch(c), fB.ap(c, w - 1, w), fB.keys(c, w - 1, w),
                          [KCH], tag="upKV")
                self.tt(fB.ap(c, 0, w), fB.ap(c, 0, w), fA.ap(c, 0, w),
                        ALU.mult, fB.keys(c, 0, w) + fA.keys(c, 0, w),
                        fB.keys(c, 0, w), tag="S")
                self.copy(OARp.ap(c, -1, 0), oarl(c), [KCH],
                          OARp.keys(c, -1, 0), tag="oarl")
                self.tt(OARp.ap(c, 0, w), fB.ap(c, 0, w), Qp.ap(c, 0, w),
                        ALU.mult, fB.keys(c, 0, w) + Qp.keys(c, 0, w),
                        OARp.keys(c, 0, w), tag="oar")
                self.copy(oarl(c), OARp.ap(c, w - 1, w),
                          OARp.keys(c, w - 1, w), [KCH], tag="oarU")
                self.tt(fA.ap(c, 0, w), xnp.ap(c, 1, 1 + w),
                        OARp.ap(c, -1, w - 1), ALU.subtract,
                        xnp.keys(c, 1, 1 + w) + OARp.keys(c, -1, w - 1),
                        fA.keys(c, 0, w), tag="E")
                if p == 0:
                    self.ts(fA.ap(c, 0, 1), fA.ap(c, 0, 1),
                            self.sb_edge0[:, 0:1], ALU.mult,
                            fA.keys(c, 0, 1) + [("sc", "edge0")],
                            fA.keys(c, 0, 1), tag="E0")
                self.tt(fA.ap(c, 0, w), K2f.ap(c, i0, i0 + w), fA.ap(c, 0, w),
                        ALU.mult, K2f.keys(c, i0, i0 + w) + fA.keys(c, 0, w),
                        fA.keys(c, 0, w), tag="k2e")
                self.scan(fC.ap(c, 0, w), onesb[:].broadcast_to([128, w]),
                          fA.ap(c, 0, w), s2ch(c),
                          fA.keys(c, 0, w) + [KCH, ("sc", "ones_col")],
                          fC.keys(c, 0, w), tag="scS2")
                self.copy(s2ch(c), fC.ap(c, w - 1, w), fC.keys(c, w - 1, w),
                          [KCH], tag="upS2")
                self.copy(K2f.ap(c, i0, i0 + w), fC.ap(c, 0, w),
                          fC.keys(c, 0, w), K2f.keys(c, i0, i0 + w), tag="s2w")
                q2lo = 2 + i0
                q2hi = min(2 + i0 + w, W2 + 2)
                if q2lo < q2hi:
                    ww = q2hi - q2lo
                    self.ts(Q2f.ap(c, q2lo, q2hi), Qp.ap(c, 0, ww), 0.03125,
                            ALU.mult, Qp.keys(c, 0, ww), Q2f.keys(c, q2lo, q2hi),
                            tag="q2a")
                    self.stt(Q2f.ap(c, q2lo, q2hi), Qp.ap(c, 0, ww), 0.000625,
                             Q2f.ap(c, q2lo, q2hi), ALU.mult, ALU.min,
                             Qp.keys(c, 0, ww) + Q2f.keys(c, q2lo, q2hi),
                             Q2f.keys(c, q2lo, q2hi), tag="q2b")
            if hi == g.own_hi:
                for c in range(NCH):
                    self.copy(ag2in[:, c:c + 1], s2ch(c), [KCH], [KCH],
                              tag="ag2p")
            for oc in range(NCH):
                psn, ps = ("psC", self.psC) if oc % 2 == 0 else ("psD", self.psD)
                for kc in range(NCH):
                    self.mm(psn, ps[:, 0:w], wproj.ap(kc, oc * 128, oc * 128 + 128),
                            OARp.ap(kc, -1, w - 1), kc == 0, kc == NCH - 1,
                            wproj.keys(kc, oc * 128, oc * 128 + 128),
                            OARp.keys(kc, -1, w - 1), tag="pjmm")
                self.act(sp_pan.ap(oc, 0, w), ps[:, 0:w], AF.Identity,
                         [("ps", psn), ("sc", "bprojc")],
                         sp_pan.keys(oc, 0, w),
                         bias=self.sb_bias["bprojc"][:, oc:oc + 1], tag="evs")
                self.dma("scalar", self.s_spill[oc * 128:(oc + 1) * 128,
                                                i0:i0 + w],
                         sp_pan.ap(oc, 0, w), sp_pan.keys(oc, 0, w),
                         [dkey("s_spill", oc, i0 // 512)], tag="ssp", slow_ok=True)
        self.bfree(wq1)
        self.bfree(Qp)

        agg2 = self.allgather(2, ag2in, 8, [KCH])
        car2 = self.carry_cols(agg2, 8, "car2")
        k_car2 = ("sc", "car2_c")

        n_pan3 = (P3N + PW - 1) // PW
        for p in range(n_pan3):
            lo = SC_LO + p * PW
            hi = min(lo + PW, SC_LO + P3N)
            w = hi - lo
            i0 = lo - SC_LO
            for c in range(NCH):
                self.stt(OMAp.ap(c, 0, w), K2f.ap(c, i0, i0 + w),
                         car2[:, c:c + 1], Q2f.ap(c, i0 + 1, i0 + 1 + w),
                         ALU.add, ALU.mult,
                         K2f.keys(c, i0, i0 + w) + [k_car2] +
                         Q2f.keys(c, i0 + 1, i0 + 1 + w),
                         OMAp.keys(c, 0, w), tag="oma")
            for oc in range(NCH):
                psn, ps = ("psA", self.psA) if oc % 2 == 0 else ("psB", self.psB)
                for kc in range(NCH):
                    self.mm(psn, ps[:, 0:w],
                            wproj.ap(kc, oc * 128, oc * 128 + 128),
                            OMAp.ap(kc, 0, w),
                            kc == 0, kc == NCH - 1,
                            wproj.keys(kc, oc * 128, oc * 128 + 128),
                            OMAp.keys(kc, 0, w), tag="pj2mm")
                self.dma("scalar", sp_pan.ap(oc, 0, w),
                         self.s_spill[oc * 128:(oc + 1) * 128,
                                      i0 + 1:i0 + 1 + w],
                         [dkey("s_spill", oc, i0 // 512),
                          dkey("s_spill", oc, (i0 + w) // 512)],
                         sp_pan.keys(oc, 0, w), tag="sld", slow_ok=True)
                self.tt(sp_pan.ap(oc, 0, w), sp_pan.ap(oc, 0, w),
                        ps[:, 0:w], ALU.add,
                        sp_pan.keys(oc, 0, w) + [("ps", psn)],
                        sp_pan.keys(oc, 0, w), tag="sadd")
                self.dma("scalar", self.s_spill[oc * 128:(oc + 1) * 128,
                                                i0 + 1:i0 + 1 + w],
                         sp_pan.ap(oc, 0, w), sp_pan.keys(oc, 0, w),
                         [dkey("s_spill", oc, i0 // 512),
                          dkey("s_spill", oc, (i0 + w) // 512)], tag="sst", slow_ok=True)
        self.bfree(wproj)
        self.bfree(sp_pan)
        self.bfree(Kpans[0]); self.bfree(Kpans[1])
        self.bfree(K2f)
        self.bfree(Q2f)
        self.bfree(xnps[0]); self.bfree(xnps[1])
        self.bfree(fA)
        self.bfree(fB)
        self.bfree(fC)
        self.bfree(OARp)
        self.bfree(OMAp)

    def row_scatter(self, src_ap, n_rows, dst_rows, row, width, rkeys, wkeys,
                    scname, tag=""):
        d = self.dram(scname, [n_rows * width], BF16)
        self.dma("gpsimd", d.rearrange("(p f) -> p f", p=n_rows), src_ap,
                 rkeys, [dkey(scname)], tag=tag + "a")
        self.dma("gpsimd", dst_rows, d.rearrange("(o f) -> o f", o=1),
                 [dkey(scname)], wkeys, tag=tag + "b")

    def phase_l0_ffn(self):
        g = self.g
        xab0 = self.ffn_layer(0, None)
        QLO0 = g.own_lo - 768
        QN0 = (g.own_hi + 128) - QLO0
        self.xab0_spill = self.dram("xab0_spill", [D, QN0 + 1], BF16)
        for c in range(NCH):
            self.dma("scalar", self.xab0_spill[c * 128:(c + 1) * 128, :],
                     xab0.ap(c, -1, QN0), xab0.keys(c, -1, QN0),
                     [dkey("xab0_spill", c, -1)], tag="xab0sp")
        self.bfree(xab0)
        self.xab1 = self.ffn_layer(1, "spill")

    def ffn_layer(self, li, xin):
        g = self.g
        QLO = g.own_lo - (768 if li == 0 else 128)
        QLO_prev = g.own_lo - 768
        QHI = g.own_hi + 128
        KLO = QLO - 512
        QN = QHI - QLO
        KN = QHI - KLO
        n_qt = QN // 128
        n_kt = KN // 128
        L = str(li)

        wqk = self.wfull(f"wqkvqk{li}", getattr(self, f"wqkvqk{li}"))
        wv = self.wfull(f"wqkvv{li}", getattr(self, f"wqkvv{li}"))
        Vtok = self.abuf("vtok" + L, n_kt, 512, BF16)
        qhat = self.abuf("qhat" + L, 4, QN, BF16)
        khat = self.abuf("khat" + L, 4, KN, BF16)
        rotpan = self.abuf("frot" + L, 1, 512, BF16)
        tabc = self.abuf("ftabc" + L, 1, 512, BF16)
        tabs = self.abuf("ftabs" + L, 1, 512, BF16)
        f1 = self.abuf("ff1" + L, 1, 512, BF16)
        f2 = self.abuf("ff2" + L, 1, 512, BF16)
        rrow = self.abuf("frr" + L, 1, 512, BF16)
        KRS = rrow.keys(0, 0, 512)
        xpans = [self.abuf(f"fxpan{k}" + L, NCH, 513, BF16) for k in range(2)]
        sqpan = self.abuf("fsq" + L, NCH, 512, BF16)

        pan_lo = [0]
        pan_buf = [xpans[0]]
        nld = [0]

        def ld_panel(lo, hi):
            pan_lo[0] = lo
            pan_buf[0] = xpans[nld[0] % 2]
            nld[0] += 1
            xpan = pan_buf[0]
            for c in range(NCH):
                if xin is None:
                    self.load_xn(xpan, c, lo - 1, hi, tag="fxl")
                else:
                    self.load_spill(xpan, c, lo - 1, hi, self.xab0_spill,
                                    "xab0_spill", 1 - QLO_prev, tag="fxl")

        def xap(c, lo, hi):
            return pan_buf[0].ap(c, lo - (pan_lo[0] - 1), hi - (pan_lo[0] - 1))

        def xkeys(c, lo, hi):
            return pan_buf[0].keys(c, lo - (pan_lo[0] - 1),
                                   hi - (pan_lo[0] - 1))

        def qk_panel(oc, lo, hi):
            w = hi - lo
            side = 0 if oc < 4 else 1
            psn, ps = ("psA", self.psA) if oc % 2 == 0 else ("psB", self.psB)
            for kc in range(NCH):
                sh = 0 if kc < 4 else -1
                self.mm(psn, ps[:, 0:w], wqk.ap(kc, oc * 128, oc * 128 + 128),
                        xap(kc, lo + sh, hi + sh), kc == 0, kc == NCH - 1,
                        wqk.keys(kc, oc * 128, oc * 128 + 128),
                        xkeys(kc, lo + sh, hi + sh), tag="qkmm")
            hat = qhat if side == 0 else khat
            base = QLO if side == 0 else KLO
            c = oc % 4
            self.copy(hat.ap(c, lo - base, hi - base), ps[:, 0:w],
                      [("ps", psn)], hat.keys(c, lo - base, hi - base),
                      eng="scalar", tag="qkev")
            self.act(sqpan.ap(oc, 0, w), ps[:, 0:w], AF.Square,
                     [("ps", psn)], sqpan.keys(oc, 0, w), tag="qksq")

        def rot_chunk(oc, lo, hi):
            w = hi - lo
            side = 0 if oc < 4 else 1
            c = oc % 4
            hat = qhat if side == 0 else khat
            base = QLO if side == 0 else KLO
            i0 = lo - base
            sn = ("qsind" if side == 0 else "ksind") + L
            sind = getattr(self, "sb_" + sn)
            ctab = self.rotq_cos if side == 0 else self.rotk_cos
            stab = self.rotq_sin if side == 0 else self.rotk_sin
            ckey = "rotq_cos" if side == 0 else "rotk_cos"
            skey = "rotq_sin" if side == 0 else "rotk_sin"
            self.mm("psBC", self.psBC[:, 0:w], sind[:, c * 128:(c + 1) * 128],
                    rrow.ap(0, 0, w, dp=0, dn=16), True, True,
                    [("sc", sn)], KRS, tag="bcmm")
            self.tt(hat.ap(c, i0, i0 + w), hat.ap(c, i0, i0 + w),
                    self.psBC[:, 0:w], ALU.mult,
                    hat.keys(c, i0, i0 + w) + [("ps", "psBC")],
                    hat.keys(c, i0, i0 + w), tag="qn")
            for b0, b1 in ((0, 32), (32, 0), (64, 96), (96, 64)):
                self.copy(rotpan.ap(0, 0, w, dp=b0, dn=32),
                          hat.ap(c, i0, i0 + w, dp=b1, dn=32),
                          hat.keys(c, i0, i0 + w),
                          rotpan.keys(0, 0, w), eng="scalar", tag="rotc")
            self.dma("sync", tabc.ap(0, 0, w), ctab[:, lo:hi],
                     [dkey(ckey)], tabc.keys(0, 0, w), tag="tabl")
            self.dma("sync", tabs.ap(0, 0, w), stab[:, lo:hi],
                     [dkey(skey)], tabs.keys(0, 0, w), tag="tabl")
            self.tt(f1.ap(0, 0, w), hat.ap(c, i0, i0 + w), tabc.ap(0, 0, w),
                    ALU.mult, hat.keys(c, i0, i0 + w) + tabc.keys(0, 0, w),
                    f1.keys(0, 0, w), tag="rt1")
            self.tt(f2.ap(0, 0, w), rotpan.ap(0, 0, w), tabs.ap(0, 0, w),
                    ALU.mult, rotpan.keys(0, 0, w) + tabs.keys(0, 0, w),
                    f2.keys(0, 0, w), tag="rt2")
            self.tt(hat.ap(c, i0, i0 + w), f1.ap(0, 0, w), f2.ap(0, 0, w),
                    ALU.add, f1.keys(0, 0, w) + f2.keys(0, 0, w),
                    hat.keys(c, i0, i0 + w), tag="rt3")

        n_panA = (KN + 511) // 512
        for p in range(n_panA):
            lo = KLO + p * 512
            hi = min(lo + 512, QHI)
            w = hi - lo
            ld_panel(lo, hi)
            for tt_ in range(w // 128):
                kt = p * 4 + tt_
                psn, ps = ("psC", self.psC) if tt_ % 2 == 0 else ("psD", self.psD)
                t0 = lo + tt_ * 128
                for kc in range(NCH):
                    sh = 0 if kc < 4 else -1
                    self.mm(psn, ps[:, 0:512],
                            xap(kc, t0 + sh, t0 + sh + 128),
                            wv.ap(kc, 0, 512), kc == 0, kc == NCH - 1,
                            xkeys(kc, t0 + sh, t0 + sh + 128),
                            wv.keys(kc, 0, 512), tag="vmm")
                self.copy(Vtok.ap(kt, 0, 512), ps[:, 0:512], [("ps", psn)],
                          Vtok.keys(kt, 0, 512), eng="scalar", tag="vev")
            ocs = [4, 5, 6, 7] + ([0, 1, 2, 3] if lo >= QLO else [])
            for oc in ocs:
                qk_panel(oc, lo, hi)
            for i, oc in enumerate(ocs):
                self.mm("psBC2", self.psBC2[0:16, 0:w],
                        self.sb_hsel[:, oc * 16:(oc + 1) * 16],
                        sqpan.ap(oc, 0, w), i == 0, i == len(ocs) - 1,
                        [("sc", "hsel")], sqpan.keys(oc, 0, w), tag="ssq")
            self.act(rrow.ap(0, 0, w, dp=0, dn=16),
                     self.psBC2[0:16, 0:w], AF.Ln,
                     [("ps", "psBC2")], KRS, bias=1e-24, tag="lnq")
            self.act(rrow.ap(0, 0, w, dp=0, dn=16),
                     rrow.ap(0, 0, w, dp=0, dn=16), AF.Exp,
                     KRS, KRS, scale=-0.5, tag="rsq")
            for oc in ocs:
                rot_chunk(oc, lo, hi)
        self.bfree(wqk)
        self.bfree(wv)
        self.bfree(sqpan)
        self.bfree(rotpan); self.bfree(tabc); self.bfree(tabs)
        self.bfree(f1); self.bfree(f2); self.bfree(rrow)
        self.bfree(xpans[0]); self.bfree(xpans[1])

        self.tap(f"qhat{li}", qhat.full_ap(), [128, 4 * QN], qhat.kall(),
                 dtype=BF16)
        self.tap(f"khat{li}", khat.full_ap(), [128, 4 * KN], khat.kall(),
                 dtype=BF16)
        self.tap(f"vtok{li}", Vtok.full_ap(), [128, n_kt * 512], Vtok.kall(),
                 dtype=BF16)
        negmc = self.abuf("negmc" + L, 1, 256)
        sumsc = self.abuf("sumsc" + L, 1, 256)
        tstage = self.abuf("tstage" + L, 1, 128, BF16, side="hi")
        mrows = self.abuf("mrows" + L, 1, QN, BF16, side="hi")
        rrows = self.abuf("rrows" + L, 1, QN, BF16, side="hi")
        PT = self.abuf("pt" + L, 6, 640, BF16, side="hi")
        obuf = self.abuf("obuf" + L, 4, QN, BF16)
        for p0 in range(0, QN, 512):
            p1 = min(p0 + 512, QN)
            self.memset(mrows.ap(0, p0, p1, dp=0, dn=16), 0.0,
                        mrows.keys(0, p0, p1), tag="mrz")
            self.memset(rrows.ap(0, p0, p1, dp=0, dn=16), 0.0,
                        rrows.keys(0, p0, p1), tag="rrz")
        for sl in range(6):
            self.memset(PT.ap(sl, 0, 640), 0.0, PT.keys(sl, 0, 640),
                        tag="ptz")
        KNM = negmc.keys(0, 0, 256)
        KSM = sumsc.keys(0, 0, 256)
        KTS = tstage.keys(0, 0, 128)

        for h in range(8):
            ch, rb = h // 2, (h % 2) * 64
            for qt in range(n_qt):
                kq = qhat.keys(ch, qt * 128, qt * 128 + 128)

                def sseg(c0, c1):
                    if qt % 2 == 0:
                        return ("psSC", self.psSC[:, c0:c1], c0)
                    if c0 >= 512:
                        return ("psBC2", self.psBC2[:, c0 - 512:c1 - 512],
                                c0 - 512)
                    return ("psBC", self.psBC[:, c0:c1], c0)

                e1 = sseg(0, 512)
                e2 = sseg(512, 640)
                e3 = sseg(0, 128)
                e4 = sseg(512, 640)
                self.mm_groupb([
                    (e1[0], e1[1], e1[2],
                     qhat.ap(ch, qt * 128, qt * 128 + 128, dp=rb, dn=64),
                     khat.ap(ch, qt * 128, qt * 128 + 512, dp=rb, dn=64),
                     kq, khat.keys(ch, qt * 128, qt * 128 + 512)),
                    (e2[0], e2[1], e2[2],
                     qhat.ap(ch, qt * 128, qt * 128 + 128, dp=rb, dn=64),
                     khat.ap(ch, qt * 128 + 512, qt * 128 + 640, dp=rb, dn=64),
                     kq, khat.keys(ch, qt * 128 + 512, qt * 128 + 640)),
                    (e3[0], e3[1], e3[2], self.sb_identb[:],
                     self.sb_tri_first_s[:], [("sc", "identb")],
                     [("sc", "tri_first_s")]),
                    (e4[0], e4[1], e4[2], self.sb_identb[:],
                     self.sb_tri_last_s[:], [("sc", "identb")],
                     [("sc", "tri_last_s")]),
                ], tag="smm")
                if qt % 2 == 0:
                    self.red(negmc.ap(0, qt, qt + 1), self.psSC[:, 0:640],
                             ALU.max, [("ps", "psSC")], KNM, negate=True,
                             tag="smax")
                else:
                    sc = self.sc_small
                    ksc = ("sc", "sc_small")
                    self.red(sc[:, 32:33], self.psBC[:, 0:512],
                             ALU.max, [("ps", "psBC")], [ksc], negate=True,
                             tag="smax")
                    self.red(sc[:, 33:34], self.psBC2[:, 0:128],
                             ALU.max, [("ps", "psBC2")], [ksc], negate=True,
                             tag="smax")
                    self.tt(negmc.ap(0, qt, qt + 1), sc[:, 32:33],
                            sc[:, 33:34], ALU.min, [ksc], KNM, tag="smax")
            self.transpose("psD", self.psD[0:n_qt, 0:128],
                           negmc.ap(0, 0, n_qt), KNM, tag="mtr")
            self.copy(tstage.ap(0, 0, 128, dp=0, dn=n_qt),
                      self.psD[0:n_qt, 0:128], [("ps", "psD")], KTS,
                      eng="vector", tag="mcp")
            self.row_scatter(tstage.ap(0, 0, 128, dp=0, dn=n_qt), n_qt,
                             mrows.ap(0, 0, n_qt * 128, dp=h, dn=1), h,
                             128, KTS, mrows.keys(0, 0, QN),
                             f"mrow_sc{L}_{h}", tag="mrs")

            def do_pv(qt):
                pv_ops = []
                sum_ops = []
                for r in range(5):
                    ktr = qt + r
                    slot = ktr % 6
                    qtlo_r = max(0, ktr - 4)
                    off = (qt - qtlo_r) * 128
                    pv_ops.append((self.psA[0:64, 0:128], 0,
                                   Vtok.ap(ktr, h * 64, h * 64 + 64),
                                   PT.ap(slot, off, off + 128),
                                   Vtok.keys(ktr, h * 64, h * 64 + 64),
                                   PT.keys(slot, off, off + 128)))
                    sum_ops.append((self.psB[:, 0:1], 0,
                                    PT.ap(slot, off, off + 128),
                                    self.onescol_b[:],
                                    PT.keys(slot, off, off + 128),
                                    [("sc", "onescol_b")]))
                self.mm_group("psA", pv_ops, tag="pvmm")
                self.mm_group("psB", sum_ops, tag="summ")
                self.copy(obuf.ap(ch, qt * 128, qt * 128 + 128, dp=rb, dn=64),
                          self.psA[0:64, 0:128], [("ps", "psA")],
                          obuf.keys(ch, qt * 128, qt * 128 + 128),
                          eng="vector", tag="oev")
                self.copy(sumsc.ap(0, qt, qt + 1), self.psB[:, 0:1],
                          [("ps", "psB")], KSM, eng="vector", tag="sev")

            for kt in range(n_kt):
                qtlo = max(0, kt - 4)
                qthi = min(kt + 1, n_qt)
                w_st = (qthi - qtlo) * 128

                def tseg(c0, c1):
                    if kt % 2 == 0:
                        return ("psSC", self.psSC[:, c0:c1], c0)
                    if c0 >= 512:
                        return ("psBC2", self.psBC2[:, c0 - 512:c1 - 512],
                                c0 - 512)
                    return ("psBC", self.psBC[:, c0:c1], c0)

                ops = []
                kk = khat.keys(ch, kt * 128, kt * 128 + 128)
                for c0 in range(0, w_st, 512):
                    c1 = min(c0 + 512, w_st)
                    e = tseg(c0, c1)
                    ops.append((e[0], e[1], e[2],
                                khat.ap(ch, kt * 128, kt * 128 + 128,
                                        dp=rb, dn=64),
                                qhat.ap(ch, qtlo * 128 + c0, qtlo * 128 + c1,
                                        dp=rb, dn=64),
                                kk,
                                qhat.keys(ch, qtlo * 128 + c0,
                                          qtlo * 128 + c1)))
                for c0 in range(0, w_st, 512):
                    c1 = min(c0 + 512, w_st)
                    e = tseg(c0, c1)
                    ops.append((e[0], e[1], e[2],
                                self.sb_osel[:, h * 128:(h + 1) * 128],
                                mrows.ap(0, qtlo * 128 + c0, qtlo * 128 + c1,
                                         dp=0, dn=16),
                                [("sc", "osel")],
                                mrows.keys(0, qtlo * 128 + c0,
                                           qtlo * 128 + c1)))
                if kt < n_qt:
                    off = (kt - qtlo) * 128
                    e = tseg(off, off + 128)
                    ops.append((e[0], e[1], e[2],
                                self.sb_identb[:], self.sb_tri_first_st[:],
                                [("sc", "identb")], [("sc", "tri_first_st")]))
                if 0 <= kt - 4 < n_qt:
                    off = (kt - 4 - qtlo) * 128
                    e = tseg(off, off + 128)
                    ops.append((e[0], e[1], e[2],
                                self.sb_identb[:], self.sb_tri_last_st[:],
                                [("sc", "identb")], [("sc", "tri_last_st")]))
                self.mm_groupb(ops, tag="stmm")
                slot = kt % 6
                dcb = self.sb_deadcol[:, (KLO // 128) + kt:
                                      (KLO // 128) + kt + 1]
                if kt % 2 == 0:
                    self.act(PT.ap(slot, 0, w_st), self.psSC[:, 0:w_st],
                             AF.Exp, [("ps", "psSC"), ("sc", "deadcol")],
                             PT.keys(slot, 0, w_st), bias=dcb, tag="stexp")
                else:
                    wlo = min(w_st, 512)
                    self.act(PT.ap(slot, 0, wlo), self.psBC[:, 0:wlo],
                             AF.Exp, [("ps", "psBC"), ("sc", "deadcol")],
                             PT.keys(slot, 0, wlo), bias=dcb, tag="stexp")
                    if w_st > 512:
                        self.act(PT.ap(slot, 512, w_st),
                                 self.psBC2[:, 0:w_st - 512], AF.Exp,
                                 [("ps", "psBC2"), ("sc", "deadcol")],
                                 PT.keys(slot, 512, w_st), bias=dcb,
                                 tag="stexp")
                if 0 <= kt - 4 < n_qt:
                    do_pv(kt - 4)
            self.ts(sumsc.ap(0, 0, n_qt), sumsc.ap(0, 0, n_qt), 1e-30,
                    ALU.add, KSM, KSM, tag="seps")
            self.recip(sumsc.ap(0, 0, n_qt), sumsc.ap(0, 0, n_qt), KSM, KSM,
                       tag="srec")
            self.transpose("psD", self.psD[0:n_qt, 0:128],
                           sumsc.ap(0, 0, n_qt), KSM, tag="rtr")
            self.copy(tstage.ap(0, 0, 128, dp=0, dn=n_qt),
                      self.psD[0:n_qt, 0:128], [("ps", "psD")], KTS,
                      eng="vector", tag="rcp")
            self.row_scatter(tstage.ap(0, 0, 128, dp=0, dn=n_qt), n_qt,
                             rrows.ap(0, 0, n_qt * 128, dp=h, dn=1), h,
                             128, KTS, rrows.keys(0, 0, QN),
                             f"rrow_sc{L}_{h}", tag="rrs")
            if h % 2 == 1:
                for p0 in range(0, QN, 512):
                    p1 = min(p0 + 512, QN)
                    self.mm("psBC", self.psBC[:, 0:p1 - p0],
                            self.sb_hind16[:, ch * 128:(ch + 1) * 128],
                            rrows.ap(0, p0, p1, dp=0, dn=16), True, True,
                            [("sc", "hind16")], rrows.keys(0, p0, p1),
                            tag="obc")
                    self.tt(obuf.ap(ch, p0, p1), obuf.ap(ch, p0, p1),
                            self.psBC[:, 0:p1 - p0], ALU.mult,
                            obuf.keys(ch, p0, p1) + [("ps", "psBC")],
                            obuf.keys(ch, p0, p1), tag="osc")
        self.bfree(negmc); self.bfree(sumsc); self.bfree(tstage)
        self.bfree(mrows); self.bfree(rrows); self.bfree(PT)
        self.bfree(khat); self.bfree(qhat); self.bfree(Vtok)

        self.tap(f"obuf{li}", obuf.full_ap(), [128, 4 * QN], obuf.kall(),
                 dtype=BF16)
        self.tap(f"mrows{li}", mrows.full_ap(), [128, QN], mrows.kall(),
                 dtype=BF16)
        self.tap(f"rrows{li}", rrows.full_ap(), [128, QN], rrows.kall(),
                 dtype=BF16)
        xpans[0] = self.abuf(f"fxpan0b" + L, NCH, 513, BF16)
        xpans[1] = self.abuf(f"fxpan1b" + L, NCH, 513, BF16)
        wo = self.wfull(f"wo{li}", getattr(self, f"wo{li}"), rows=512)
        xab = self.abuf("xab" + L, NCH, QN, BF16, lead=1, side="hi")
        for c in range(NCH):
            self.memset(xab.ap(c, -1, 0), 0.0, xab.keys(c, -1, 0), tag="xabz")
        for p0 in range(0, QN, 512):
            p1 = min(p0 + 512, QN)
            w = p1 - p0
            ld_panel(QLO + p0, QLO + p1)
            for oc in range(NCH):
                psn, ps = ("psA", self.psA) if oc % 2 == 0 else ("psB", self.psB)
                for kc in range(4):
                    self.mm(psn, ps[:, 0:w], wo.ap(kc, oc * 128, oc * 128 + 128),
                            obuf.ap(kc, p0, p1), kc == 0, kc == 3,
                            wo.keys(kc, oc * 128, oc * 128 + 128),
                            obuf.keys(kc, p0, p1), tag="womm")
                self.tt(xab.ap(oc, p0, p1), xap(oc, QLO + p0, QLO + p1),
                        ps[:, 0:w], ALU.add,
                        xkeys(oc, QLO + p0, QLO + p1) + [("ps", psn)],
                        xab.keys(oc, p0, p1), tag="worid")
        self.bfree(wo)
        self.bfree(obuf)
        for b_ in xpans:
            self.bfree(b_)
        self.tap(f"xa{li}", xab.full_ap(), [128, NCH * (QN + 1)],
                 xab.kall(), dtype=BF16)
        self.ffn_mlp(li, xab, QN)
        if li == 0:
            bi = (g.own_lo - 1) - QLO
            for c in range(NCH):
                self.ts(xab.ap(c, bi, bi + 1), xab.ap(c, bi, bi + 1),
                        self.sb_edge0[:, 0:1], ALU.mult,
                        xab.keys(c, bi, bi + 1) + [("sc", "edge0")],
                        xab.keys(c, bi, bi + 1), tag="xabe0")
        self.tap(f"xab{li}", xab.full_ap(), [128, NCH * (QN + 1)],
                 xab.kall(), dtype=BF16)
        return xab

    def ffn_mlp(self, li, xab, FR):
        g = self.g
        L = str(li)
        assert FR % 4 == 0
        ql = FR // 4
        prodb = self.abuf("prod" + L, 22, ql, BF16)
        w1blk = [self.abuf(f"w1b{k}" + L, NCH, 256, BF16) for k in range(2)]
        w2pair = [self.abuf(f"w2p{k}" + L, 22, 256, BF16) for k in range(2)]
        fgg2 = [self.abuf(f"fgg{k}" + L, 1, 512, BF16) for k in range(2)]
        htil = self.abuf("htil" + L, NCH, ql, BF16)
        stash = self.sbuf(f"ffnstash{li}", [128, 8])
        KST = [("sc", f"ffnstash{li}")]
        lnrowA = self.abuf("lnA" + L, 1, FR, BF16)
        lnrowB = self.abuf("lnB" + L, 1, FR, BF16)
        sqp = self.abuf("msq" + L, NCH, 512, BF16)
        if not hasattr(self, "_mrow1"):
            self._mrow1 = self.sbuf("mrow1", [1, 512])
            self._mrow2 = self.sbuf("mrow2", [1, 512])
        r1, r2 = self._mrow1, self._mrow2
        KR1 = [("sc", "mrow1")]
        KR2 = [("sc", "mrow2")]
        lgb = self.sbuf(f"sb_lgb{li}", [2, NCH * 128], BF16)
        self.dma("sync", lgb[:], getattr(self, f"lgb{li}")[:, :],
                 [dkey(f"lgb{li}")], [("sc", f"lgb{li}")], tag="lgb")
        KLGB = [("sc", f"lgb{li}")]
        if not hasattr(self, "_onerow"):
            self._onerow = self.sbuf("onerow", [1, 512], BF16)
            self.memset(self._onerow[:], 1.0, [("sc", "onerow")], tag="or0")
        for p0 in range(0, FR, 512):
            p1 = min(p0 + 512, FR)
            self.dma("sync", lnrowB.ap(0, p0, p1, dp=1, dn=1),
                     self._onerow[:, 0:p1 - p0], [("sc", "onerow")],
                     lnrowB.keys(0, p0, p1), tag="onerow")
        lgc = self.sbuf(f"sb_lgc{li}", [128, NCH])
        self.dma("sync", lgc[:], getattr(self, f"lgc{li}")[:, :],
                 [dkey(f"lgc{li}")], [("sc", f"lgc{li}")], tag="lgc")

        def shx(c, p0, p1):
            sh = 0 if c < 4 else -1
            return xab.ap(c, p0 + sh, p1 + sh)

        def shxk(c, p0, p1):
            sh = 0 if c < 4 else -1
            return xab.keys(c, p0 + sh, p1 + sh)

        for p0 in range(0, FR, 512):
            p1 = min(p0 + 512, FR)
            w = p1 - p0
            for c in range(NCH):
                self.act(sqp.ap(c, 0, w), shx(c, p0, p1), AF.Square,
                         shxk(c, p0, p1), sqp.keys(c, 0, w),
                         tag="msq")
            for c in range(NCH):
                self.mm("psBC", self.psBC[:, 0:w], self.ones128_b[:],
                        shx(c, p0, p1), c == 0, c == NCH - 1,
                        [("sc", "ones128_b")], shxk(c, p0, p1),
                        tag="ms1")
                self.mm("psBC2", self.psBC2[:, 0:w], self.ones128_b[:],
                        sqp.ap(c, 0, w), c == 0, c == NCH - 1,
                        [("sc", "ones128_b")], sqp.keys(c, 0, w), tag="ms2")
            self.act(r1[:, 0:w], self.psBC[0:1, 0:w], AF.Square,
                     [("ps", "psBC")], KR1, tag="mr1")
            self.stt(r1[:, 0:w], r1[:, 0:w], -1.0 / D, self.psBC2[0:1, 0:w],
                     ALU.mult, ALU.add, KR1 + [("ps", "psBC2")], KR1,
                     tag="mr2")
            self.act(r1[:, 0:w], r1[:, 0:w], AF.Ln, KR1, KR1, bias=1e-5,
                     scale=1.0 / D, tag="mr3")
            self.act(lnrowA.ap(0, p0, p1, dp=0, dn=1), r1[:, 0:w], AF.Exp,
                     KR1, lnrowA.keys(0, p0, p1), scale=-0.5, tag="mr4")
            self.ts(r2[:, 0:w], self.psBC[0:1, 0:w], -1.0 / D, ALU.mult,
                    [("ps", "psBC")], KR2, tag="mr5")
            self.tt(lnrowB.ap(0, p0, p1, dp=0, dn=1), r2[:, 0:w],
                    lnrowA.ap(0, p0, p1, dp=0, dn=1), ALU.mult,
                    KR2 + lnrowA.keys(0, p0, p1), lnrowB.keys(0, p0, p1),
                    tag="mr6")
        self.bfree(sqp)

        w1d = getattr(self, f"w1pb{li}")
        w2d = getattr(self, f"w2pcb{li}")
        w1cast = self.dram(f"w1cast{li}", [22 * 128, 2048], BF16)
        w2cast = self.dram(f"w2cast{li}", [4 * 128, 22 * 256], BF16)
        for qf in range(4):
            H0 = qf * ql
            if qf < 3:
                for c in range(4, NCH):
                    sg = (qf % 2) * 4 + (c - 4)
                    self.copy(stash[:, sg:sg + 1],
                              xab.ap(c, H0 + ql - 1, H0 + ql),
                              xab.keys(c, H0 + ql - 1, H0 + ql), KST,
                              tag="stsh")
            for p0 in range(H0, H0 + ql, 512):
                p1 = min(p0 + 512, H0 + ql)
                w = p1 - p0
                self.mm("psBC", self.psBC[:, 0:w], self.onesrow_b[:],
                        lnrowA.ap(0, p0, p1, dp=0, dn=1), True, True,
                        [("sc", "onesrow_b")], lnrowA.keys(0, p0, p1),
                        tag="hbc1")
                for c in range(NCH):
                    self.tt(htil.ap(c, p0 - H0, p1 - H0),
                            shx(c, p0, p1), self.psBC[:, 0:w],
                            ALU.mult,
                            shxk(c, p0, p1) + [("ps", "psBC")],
                            htil.keys(c, p0 - H0, p1 - H0), tag="ht1")
                    if qf > 0 and p0 == H0 and c >= 4:
                        sg = ((qf - 1) % 2) * 4 + (c - 4)
                        self.tt(htil.ap(c, 0, 1), stash[:, sg:sg + 1],
                                self.psBC[:, 0:1], ALU.mult,
                                KST + [("ps", "psBC")],
                                htil.keys(c, 0, 1), tag="ht1s")
                for c in range(NCH):
                    self.mm("psBC2", self.psBC2[:, 0:w],
                            lgb[0:2, c * 128:(c + 1) * 128],
                            lnrowB.ap(0, p0, p1, dp=0, dn=2), True, True,
                            KLGB, lnrowB.keys(0, p0, p1), tag="hbc2")
                    self.stt(htil.ap(c, p0 - H0, p1 - H0),
                             htil.ap(c, p0 - H0, p1 - H0), lgc[:, c:c + 1],
                             self.psBC2[:, 0:w], ALU.mult, ALU.add,
                             htil.keys(c, p0 - H0, p1 - H0) +
                             [("sc", f"lgc{li}"), ("ps", "psBC2")],
                             htil.keys(c, p0 - H0, p1 - H0), tag="ht2")
            for i in range(22):
                wb = w1blk[i % 2]
                fgg = fgg2[i % 2]
                if qf == 0:
                    self.dma("gpsimd", wb.full_ap(),
                             w1d[i * 128:(i + 1) * 128, :],
                             [dkey(f"w1pb{li}")], wb.kall(), tag="w1l")
                    self.dma("scalar", w1cast[i * 128:(i + 1) * 128, :],
                             wb.full_ap(), wb.kall(),
                             [dkey(f"w1cast{li}", 0, i)], tag="w1sv")
                else:
                    self.dma("sync" if i % 2 == 0 else "scalar",
                             wb.full_ap(), w1cast[i * 128:(i + 1) * 128, :],
                             [dkey(f"w1cast{li}", 0, i)], wb.kall(),
                             tag="w1l")
                psan, psa = ("psA", self.psA) if i % 2 == 0 else \
                    ("psC", self.psC)
                psgn, psg = ("psB", self.psB) if i % 2 == 0 else \
                    ("psD", self.psD)
                for p0 in range(0, ql, 512):
                    p1 = min(p0 + 512, ql)
                    w = p1 - p0
                    for kc in range(NCH):
                        self.mm(psan, psa[:, 0:w], wb.ap(kc, 0, 128),
                                htil.ap(kc, p0, p1), kc == 0, kc == NCH - 1,
                                wb.keys(kc, 0, 128), htil.keys(kc, p0, p1),
                                tag="w1a")
                        self.mm(psgn, psg[:, 0:w], wb.ap(kc, 128, 256),
                                htil.ap(kc, p0, p1), kc == 0, kc == NCH - 1,
                                wb.keys(kc, 128, 256), htil.keys(kc, p0, p1),
                                tag="w1g")
                    self.act(fgg.ap(0, 0, w), psg[:, 0:w], AF.Gelu,
                             [("ps", psgn), ("sc", f"b1g{li}")],
                             fgg.keys(0, 0, w),
                             bias=self.sb_bias[f"b1g{li}"][:, i:i + 1],
                             tag="gev")
                    self.stt(prodb.ap(i, p0, p1), psa[:, 0:w],
                             self.sb_bias[f"b1a{li}"][:, i:i + 1],
                             fgg.ap(0, 0, w), ALU.add, ALU.mult,
                             [("ps", psan), ("sc", f"b1a{li}")] +
                             fgg.keys(0, 0, w),
                             prodb.keys(i, p0, p1), tag="prod")
            npan = (ql + 511) // 512
            assert npan <= 3
            bankset = [("psA", self.psA), ("psB", self.psB),
                       ("psC", self.psC), ("psD", self.psD),
                       ("psBC", self.psBC), ("psBC2", self.psBC2)]
            for pr in range(4):
                wp = w2pair[pr % 2]
                if qf == 0:
                    self.dma("gpsimd", wp.full_ap(),
                             w2d[pr * 128:(pr + 1) * 128, :],
                             [dkey(f"w2pcb{li}")],
                             wp.kall(), tag="w2l")
                    self.dma("scalar", w2cast[pr * 128:(pr + 1) * 128, :],
                             wp.full_ap(), wp.kall(),
                             [dkey(f"w2cast{li}", 0, pr)], tag="w2sv")
                else:
                    self.dma("sync" if pr % 2 == 0 else "scalar",
                             wp.full_ap(), w2cast[pr * 128:(pr + 1) * 128, :],
                             [dkey(f"w2cast{li}", 0, pr)], wp.kall(),
                             tag="w2l")
                for i in range(22):
                    for o2 in range(2):
                        for sp in range(npan):
                            p0 = sp * 512
                            p1 = min(p0 + 512, ql)
                            psn, ps = bankset[o2 * npan + sp]
                            self.mm(psn, ps[:, 0:p1 - p0],
                                    wp.ap(i, o2 * 128, o2 * 128 + 128),
                                    prodb.ap(i, p0, p1), i == 0, i == 21,
                                    wp.keys(i, o2 * 128, o2 * 128 + 128),
                                    prodb.keys(i, p0, p1), tag="w2mm")
                for o2 in range(2):
                    oc = pr * 2 + o2
                    for sp in range(npan):
                        p0 = sp * 512
                        p1 = min(p0 + 512, ql)
                        psn, ps = bankset[o2 * npan + sp]
                        self.stt(xab.ap(oc, H0 + p0, H0 + p1),
                                 ps[:, 0:p1 - p0],
                                 self.sb_bias[f"b2c{li}"][:, oc:oc + 1],
                                 xab.ap(oc, H0 + p0, H0 + p1),
                                 ALU.add, ALU.add,
                                 [("ps", psn), ("sc", f"b2c{li}")] +
                                 xab.keys(oc, H0 + p0, H0 + p1),
                                 xab.keys(oc, H0 + p0, H0 + p1), tag="w2ev")
        self.bfree(lnrowA); self.bfree(lnrowB); self.bfree(htil)
        self.bfree(prodb)
        for b in w1blk + w2pair:
            self.bfree(b)
        self.bfree(fgg2[0]); self.bfree(fgg2[1])

    def phase_combine(self):
        g = self.g
        NOS = g.t_own + 2
        self.x1mod_spill = self.dram("x1mod_spill", [D, NOS], F32)
        self.xnL1 = self.abuf("xnL1", NCH, NOS, BF16)
        xnL1 = self.xnL1
        QLO1 = g.own_lo - 128
        gpan = self.abuf("cgp", NCH, 256, BF16)
        xp = self.abuf("cxp", NCH, 256)
        sp = self.abuf("csp", NCH, 256, BF16)
        ops2 = [self.abuf(f"cop{k}", NCH, 256) for k in range(2)]
        xsq = self.abuf("cxq", NCH, 256)
        rstd = self.sbuf("l1_rstd", [128, 256])
        KRS1 = [("sc", "l1_rstd")]
        for ip, p0 in enumerate(range(0, NOS, 256)):
            p1 = min(p0 + 256, NOS)
            w = p1 - p0
            op = ops2[ip % 2]
            t0 = g.own_lo - 1 + p0
            for c in range(NCH):
                self.act(gpan.ap(c, 0, w), self.xab1.ap(c, t0 - QLO1, t0 - QLO1 + w),
                         AF.Silu, self.xab1.keys(c, t0 - QLO1, t0 - QLO1 + w),
                         gpan.keys(c, 0, w), tag="silu")
                self.dma("sync", xp.ap(c, 0, w),
                         self.xfm[c * 128:(c + 1) * 128, t0:t0 + w],
                         [dkey("xfm")], xp.keys(c, 0, w), tag="cx")
                self.dma("scalar", sp.ap(c, 0, w),
                         self.s_spill[c * 128:(c + 1) * 128, p0:p1],
                         [dkey("s_spill", c)], sp.keys(c, 0, w), tag="cs")
                self.tt(sp.ap(c, 0, w), sp.ap(c, 0, w), gpan.ap(c, 0, w),
                        ALU.mult, sp.keys(c, 0, w) + gpan.keys(c, 0, w),
                        sp.keys(c, 0, w), tag="sg")
                self.tt(op.ap(c, 0, w), xp.ap(c, 0, w), sp.ap(c, 0, w),
                        ALU.add, xp.keys(c, 0, w) + sp.keys(c, 0, w),
                        op.keys(c, 0, w), tag="x1m")
                self.dma("scalar", self.x1mod_spill[c * 128:(c + 1) * 128, p0:p1],
                         op.ap(c, 0, w), op.keys(c, 0, w),
                         [dkey("x1mod_spill", c, p0 // 512)], tag="x1sp")
            for c in range(NCH):
                self.act(xsq.ap(c, 0, w), op.ap(c, 0, w), AF.Square,
                         op.keys(c, 0, w), xsq.keys(c, 0, w), tag="l1sq")
            for c in range(NCH):
                self.mm("psBC", self.psBC[:, 0:w], self.ones128[:],
                        op.ap(c, 0, w), c == 0, c == NCH - 1,
                        [("sc", "ones128")], op.keys(c, 0, w), tag="l1s1")
                self.mm("psBC2", self.psBC2[:, 0:w], self.ones128[:],
                        xsq.ap(c, 0, w), c == 0, c == NCH - 1,
                        [("sc", "ones128")], xsq.keys(c, 0, w), tag="l1s2")
            self.act(rstd[:, 0:w], self.psBC[:, 0:w], AF.Square,
                     [("ps", "psBC")], KRS1, tag="l1v1")
            self.stt(rstd[:, 0:w], rstd[:, 0:w], -1.0 / D, self.psBC2[:, 0:w],
                     ALU.mult, ALU.add, KRS1 + [("ps", "psBC2")], KRS1,
                     tag="l1v2")
            self.act(rstd[:, 0:w], rstd[:, 0:w], AF.Ln, KRS1, KRS1,
                     bias=1e-5, scale=1.0 / D, tag="l1v3")
            self.act(rstd[:, 0:w], rstd[:, 0:w], AF.Exp, KRS1, KRS1,
                     scale=-0.5, tag="l1v4")
            for c in range(NCH):
                self.stt(xsq.ap(c, 0, w), self.psBC[:, 0:w], -1.0 / D,
                         op.ap(c, 0, w), ALU.mult, ALU.add,
                         [("ps", "psBC")] + op.keys(c, 0, w),
                         xsq.keys(c, 0, w), tag="l1c")
                self.tt(xnL1.ap(c, p0, p1), xsq.ap(c, 0, w), rstd[:, 0:w],
                        ALU.mult, xsq.keys(c, 0, w) + KRS1,
                        xnL1.keys(c, p0, p1), tag="l1n")
        self.bfree(gpan); self.bfree(xp); self.bfree(sp)
        self.bfree(ops2[0]); self.bfree(ops2[1]); self.bfree(xsq)
        self.bfree(self.xab1)


    def phase_l1(self):
        g = self.g
        NOS = g.t_own + 2
        TO = g.t_own
        PW1 = 256

        xnL1 = self.xnL1
        for c in range(NCH):
            self.ts(xnL1.ap(c, 0, 1), xnL1.ap(c, 0, 1), self.sb_edge0[:, 0:1],
                    ALU.mult, xnL1.keys(c, 0, 1) + [("sc", "edge0")],
                    xnL1.keys(c, 0, 1), tag="l1e0")
            self.ts(xnL1.ap(c, NOS - 1, NOS), xnL1.ap(c, NOS - 1, NOS),
                    self.sb_edge3[:, 0:1], ALU.mult,
                    xnL1.keys(c, NOS - 1, NOS) + [("sc", "edge3")],
                    xnL1.keys(c, NOS - 1, NOS), tag="l1e3")

        self.tap("xnL1", xnL1.full_ap(), [128, NCH * NOS], xnL1.kall(),
                 dtype=BF16)
        l1c = self.sbuf("l1cols", [128, 48])
        KL1C = [("sc", "l1cols")]
        ewc = lambda c: l1c[:, c:c + 1]
        euc = lambda c: l1c[:, 8 + c:9 + c]
        ach = lambda c: l1c[:, 16 + c:17 + c]
        bch = lambda c: l1c[:, 24 + c:25 + c]
        ag3in = l1c[:, 32:48]
        for c in range(NCH):
            self.act(ewc(c), self.sb_sdc[:, c:c + 1], AF.Exp,
                     [("sc", "sdc")], KL1C, scale=1.0 / g.seq, tag="ew")
            self.act(euc(c), self.sb_sfc[:, c:c + 1], AF.Exp,
                     [("sc", "sfc")], KL1C, scale=1.0 / g.seq, tag="eu")

        m1 = self.sbuf("l1m1", [128, 24])
        KM1 = [("sc", "l1m1")]
        for i, n in enumerate(("mixkc", "mixvc", "mixrc")):
            self.ts(m1[:, i * 8:(i + 1) * 8], getattr(self, "sb_" + n)[:],
                    -1.0, ALU.mult, [("sc", n)], KM1, s2=1.0, op1=ALU.add,
                    tag="m1")

        mixp = self.abuf("l1mix", 4, PW1, BF16)
        ekf = self.abuf("l1ek", NCH, TO, BF16)
        ekvf = self.abuf("l1ekv", NCH, TO, BF16)
        fE = self.abuf("l1fE", NCH, PW1)
        fV = self.abuf("l1fV", NCH, PW1)
        fA2 = self.abuf("l1fA", NCH, PW1, lead=1)
        fB2 = self.abuf("l1fB", NCH, PW1, lead=1)

        def mk_mix(mi, mname, lo, hi, tag):
            w = hi - lo
            mcol = getattr(self, "sb_" + mname)
            for c in range(4):
                sh = -1 if c < 2 else 1
                self.ts(fE.ap(c, 0, w), xnL1.ap(c, lo, hi), mcol[:, c:c + 1],
                        ALU.mult, xnL1.keys(c, lo, hi) + [("sc", mname)],
                        fE.keys(c, 0, w), tag=tag + "a")
                self.stt(mixp.ap(c, 0, w), xnL1.ap(c, lo + sh, hi + sh),
                         m1[:, mi * 8 + c:mi * 8 + c + 1], fE.ap(c, 0, w),
                         ALU.mult, ALU.add,
                         xnL1.keys(c, lo + sh, hi + sh) + KM1 + fE.keys(c, 0, w),
                         mixp.keys(c, 0, w), tag=tag + "b")

        def l1_sweep(wname, mi, mname, evac):
            wd = getattr(self, wname)
            wts = [self.abuf("w_" + wname, NCH, 512, BF16)]
            for ho in (0, 512):
                wt = wts[0]
                self.dma("gpsimd",
                         wt.full_ap().rearrange("p (a c) -> p a c", a=NCH),
                         wd[:, ho:ho + 512].rearrange("(a p) c -> p a c",
                                                      p=128),
                         [dkey(wname)], wt.kall(), tag="wload_" + wname)
                for p0 in range(0, TO, PW1):
                    p1 = min(p0 + PW1, TO)
                    w = p1 - p0
                    mk_mix(mi, mname, 1 + p0, 1 + p1, tag=wname)
                    for oc2 in range(4):
                        oc = ho // 128 + oc2
                        psn, ps = (("psA", self.psA) if oc % 2 == 0
                                   else ("psB", self.psB))
                        for kc in range(NCH):
                            rhs = (mixp.ap(kc, 0, w) if kc < 4
                                   else xnL1.ap(kc, 1 + p0, 1 + p1))
                            rk = (mixp.keys(kc, 0, w) if kc < 4
                                  else xnL1.keys(kc, 1 + p0, 1 + p1))
                            self.mm(psn, ps[:, 0:w],
                                    wt.ap(kc, oc2 * 128, oc2 * 128 + 128),
                                    rhs, kc == 0, kc == NCH - 1,
                                    wt.keys(kc, oc2 * 128, oc2 * 128 + 128),
                                    rk, tag=wname)
                        evac(oc, p0, p1, psn, ps)
            for w_ in wts:
                self.bfree(w_)

        wk = self.wfull("wkt", self.wkt)
        wv = self.wfull("wvt", self.wvt)
        for p0 in range(0, TO, PW1):
            p1 = min(p0 + PW1, TO)
            w = p1 - p0
            mk_mix(0, "mixkc", 1 + p0, 1 + p1, tag="l1k")
            for oc in range(NCH):
                psn, ps = ("psC", self.psC) if oc % 2 == 0 else ("psD", self.psD)
                for kc in range(NCH):
                    rhs = (mixp.ap(kc, 0, w) if kc < 4
                           else xnL1.ap(kc, 1 + p0, 1 + p1))
                    rk = (mixp.keys(kc, 0, w) if kc < 4
                          else xnL1.keys(kc, 1 + p0, 1 + p1))
                    self.mm(psn, ps[:, 0:w],
                            wk.ap(kc, oc * 128, oc * 128 + 128), rhs,
                            kc == 0, kc == NCH - 1,
                            wk.keys(kc, oc * 128, oc * 128 + 128), rk,
                            tag="l1kmm")
                self.act(ekf.ap(oc, p0, p1), ps[:, 0:w], AF.Exp,
                         [("ps", psn)], ekf.keys(oc, p0, p1), tag="evek")
            mk_mix(1, "mixvc", 1 + p0, 1 + p1, tag="l1v")
            for oc in range(NCH):
                psn, ps = ("psC", self.psC) if oc % 2 == 0 else ("psD", self.psD)
                for kc in range(NCH):
                    rhs = (mixp.ap(kc, 0, w) if kc < 4
                           else xnL1.ap(kc, 1 + p0, 1 + p1))
                    rk = (mixp.keys(kc, 0, w) if kc < 4
                          else xnL1.keys(kc, 1 + p0, 1 + p1))
                    self.mm(psn, ps[:, 0:w],
                            wv.ap(kc, oc * 128, oc * 128 + 128), rhs,
                            kc == 0, kc == NCH - 1,
                            wv.keys(kc, oc * 128, oc * 128 + 128), rk,
                            tag="l1vmm")
                self.tt(ekvf.ap(oc, p0, p1), ekf.ap(oc, p0, p1),
                        ps[:, 0:w], ALU.mult,
                        ekf.keys(oc, p0, p1) + [("ps", psn)],
                        ekvf.keys(oc, p0, p1), tag="evekv")
            for oc in range(NCH):
                en = "vector" if oc % 2 == 0 else "gpsimd"
                if p0 == 0:
                    self.memset(ach(oc), 0.0, KL1C, tag="a0")
                    self.memset(bch(oc), 0.0, KL1C, tag="b0")
                self.scan(fV.ap(oc, 0, w), ewc(oc).broadcast_to([128, w]),
                          ekvf.ap(oc, p0, p1), ach(oc),
                          ekvf.keys(oc, p0, p1) + KL1C, fV.keys(oc, 0, w),
                          tag="asc")
                self.copy(ach(oc), fV.ap(oc, w - 1, w), fV.keys(oc, w - 1, w),
                          KL1C, tag="aup")
                self.scan(fE.ap(oc, 0, w), ewc(oc).broadcast_to([128, w]),
                          ekf.ap(oc, p0, p1), bch(oc),
                          ekf.keys(oc, p0, p1) + KL1C, fE.keys(oc, 0, w),
                          tag="bsc")
                self.copy(bch(oc), fE.ap(oc, w - 1, w), fE.keys(oc, w - 1, w),
                          KL1C, tag="bup")
        self.bfree(wk)
        self.bfree(wv)
        for c in range(NCH):
            self.copy(ag3in[:, c:c + 1], ach(c), KL1C, KL1C, tag="p3a")
            self.copy(ag3in[:, 8 + c:9 + c], bch(c), KL1C, KL1C, tag="p3b")
        agg3 = self.allgather(3, ag3in, 16, KL1C)
        wkvw = self.sbuf("wkvw", [128, 64])
        KWW = [("sc", "wkvw")]
        for c in range(NCH):
            self.ts(wkvw[:, c * 8:(c + 1) * 8], self.sb_pexprep[:],
                    self.sb_sdc[:, c:c + 1], ALU.mult,
                    [("sc", "pexprep"), ("sc", "sdc")], KWW, tag="ww1")
            self.act(wkvw[:, c * 8:(c + 1) * 8], wkvw[:, c * 8:(c + 1) * 8],
                     AF.Exp, KWW, KWW, scale=0.25, tag="ww2")
            self.tt(wkvw[:, c * 8:(c + 1) * 8], wkvw[:, c * 8:(c + 1) * 8],
                    self.sb_pmaskrep[:], ALU.mult,
                    KWW + [("sc", "pmaskrep")], KWW, tag="ww3")
        car3 = self.carry_cols(agg3, 16, "car3",
                               weights=(wkvw[:, 0:64], ("sc", "wkvw")))
        k_car3 = ("sc", "car3_c")

        srf = self.abuf("l1sr", NCH, TO, BF16)

        def evr(oc, p0, p1, psn, ps):
            self.act(srf.ap(oc, p0, p1), ps[:, 0:p1 - p0], AF.Sigmoid,
                     [("ps", psn)], srf.keys(oc, p0, p1), tag="evr")

        l1_sweep("wrt", 2, "mixrc", evr)

        for p0 in range(0, TO, PW1):
            p1 = min(p0 + PW1, TO)
            w = p1 - p0
            for oc in range(NCH):
                en = "vector" if oc % 2 == 0 else "gpsimd"
                if p0 == 0:
                    self.copy(ach(oc), car3[:, oc:oc + 1], [k_car3], KL1C,
                              tag="ai")
                    self.copy(bch(oc), car3[:, 8 + oc:9 + oc], [k_car3], KL1C,
                              tag="bi")
                self.copy(fA2.ap(oc, -1, 0), ach(oc), KL1C,
                          fA2.keys(oc, -1, 0), tag="al")
                self.copy(fB2.ap(oc, -1, 0), bch(oc), KL1C,
                          fB2.keys(oc, -1, 0), tag="bl")
                self.scan(fA2.ap(oc, 0, w), ewc(oc).broadcast_to([128, w]),
                          ekvf.ap(oc, p0, p1), ach(oc),
                          ekvf.keys(oc, p0, p1) + KL1C, fA2.keys(oc, 0, w),
                          tag="asc2")
                self.copy(ach(oc), fA2.ap(oc, w - 1, w), fA2.keys(oc, w - 1, w),
                          KL1C, tag="aup2")
                self.scan(fB2.ap(oc, 0, w), ewc(oc).broadcast_to([128, w]),
                          ekf.ap(oc, p0, p1), bch(oc),
                          ekf.keys(oc, p0, p1) + KL1C, fB2.keys(oc, 0, w),
                          tag="bsc2")
                self.copy(bch(oc), fB2.ap(oc, w - 1, w), fB2.keys(oc, w - 1, w),
                          KL1C, tag="bup2")
                self.stt(fE.ap(oc, 0, w), ekf.ap(oc, p0, p1), euc(oc),
                         fB2.ap(oc, -1, w - 1), ALU.mult, ALU.add,
                         ekf.keys(oc, p0, p1) + KL1C + fB2.keys(oc, -1, w - 1),
                         fE.keys(oc, 0, w), tag="den")
                self.stt(fV.ap(oc, 0, w), ekvf.ap(oc, p0, p1), euc(oc),
                         fA2.ap(oc, -1, w - 1), ALU.mult, ALU.add,
                         ekvf.keys(oc, p0, p1) + KL1C + fA2.keys(oc, -1, w - 1),
                         fV.keys(oc, 0, w), tag="num")
                self.act(fE.ap(oc, 0, w), fE.ap(oc, 0, w), AF.Ln,
                         fE.keys(oc, 0, w), fE.keys(oc, 0, w), tag="dln")
                self.act(fE.ap(oc, 0, w), fE.ap(oc, 0, w), AF.Exp,
                         fE.keys(oc, 0, w), fE.keys(oc, 0, w), scale=-1.0,
                         tag="dex")
                self.tt(fV.ap(oc, 0, w), fV.ap(oc, 0, w), fE.ap(oc, 0, w),
                        ALU.mult, fV.keys(oc, 0, w) + fE.keys(oc, 0, w),
                        fV.keys(oc, 0, w), eng=en, tag="wkv")
                self.tt(srf.ap(oc, p0, p1), srf.ap(oc, p0, p1),
                        fV.ap(oc, 0, w), ALU.mult,
                        srf.keys(oc, p0, p1) + fV.keys(oc, 0, w),
                        srf.keys(oc, p0, p1), eng=en, tag="srw")
        self.tap("srwkv", srf.full_ap(), [128, NCH * TO], srf.kall(),
                 dtype=BF16)
        self.tap("ekf", ekf.full_ap(), [128, NCH * TO], ekf.kall(),
                 dtype=BF16)
        self.bfree(ekf)
        self.bfree(ekvf)
        self.bfree(fE); self.bfree(fV); self.bfree(fA2); self.bfree(fB2)
        self.bfree(mixp)

        sL1 = self.abuf("l1s", NCH, TO, BF16)
        wo2 = self.wfull("wot", self.wot)
        for p0 in range(0, TO, PW1):
            p1 = min(p0 + PW1, TO)
            w = p1 - p0
            for oc in range(NCH):
                psn, ps = ("psA", self.psA) if oc % 2 == 0 else ("psB", self.psB)
                for kc in range(NCH):
                    self.mm(psn, ps[:, 0:w], wo2.ap(kc, oc * 128, oc * 128 + 128),
                            srf.ap(kc, p0, p1), kc == 0, kc == NCH - 1,
                            wo2.keys(kc, oc * 128, oc * 128 + 128),
                            srf.keys(kc, p0, p1), tag="l1omm")
                self.copy(sL1.ap(oc, p0, p1), ps[:, 0:w], [("ps", psn)],
                          sL1.keys(oc, p0, p1), eng="scalar", tag="evo")
        self.bfree(wo2)
        self.bfree(srf)

        self.tap("sL1", sL1.full_ap(), [128, NCH * TO], sL1.kall(),
                 dtype=BF16)
        self.l1_ffn(xnL1, sL1)

    def quantize_w(self, src_ap, n_rows, name, lnw=None, lnb=None,
                   colsum_rows=None):
        g = self.g
        cols = src_ap.shape[1]
        dst = self.dram("wq_" + name, [n_rows, cols], BF16)
        pan = self.abuf("qz_" + name, 1, 512)
        pan2 = self.abuf("qz2_" + name, 1, 512)
        panb = self.abuf("qzb_" + name, 1, 512, BF16)
        panb2 = self.abuf("qzb2_" + name, 1, 512, BF16)
        kp = pan.keys(0, 0, 512)
        kp2 = pan2.keys(0, 0, 512)
        kpb = panb.keys(0, 0, 512)
        kpb2 = panb2.keys(0, 0, 512)
        if not hasattr(self, "ones_colb"):
            self.ones_colb = self.sbuf("ones_colb", [128, 1], BF16)
            self.mones_colb = self.sbuf("mones_colb", [128, 1], BF16)
            self.memset(self.ones_colb[:], 1.0, [("sc", "ones_colb")])
            self.memset(self.mones_colb[:], -1.0, [("sc", "mones_col")])
        acc = self.sbuf("qacc_" + name, [128, 2])
        KA = [("sc", "qacc_" + name)]
        self.memset(acc[:, 0:1], 0.0, KA, tag="qz0")
        group = max(1, 512 // cols)
        colpan = max(1, cols // 512)
        npan = (n_rows // (128 * group)) * colpan
        assert group == 1 or (lnw is None and colsum_rows is None)

        def panape(p):
            r0 = (p // colpan) * 128 * group
            c0 = (p % colpan) * 512
            wq = min(512, cols - c0) if group == 1 else cols * group
            return r0, c0, wq

        def src_slice(r0, c0, wq):
            if group == 1:
                return src_ap[r0:r0 + 128, c0:c0 + wq]
            return src_ap[r0:r0 + 128 * group, :].rearrange(
                "(a p) c -> p a c", p=128)

        def dst_slice(r0, c0, wq):
            if group == 1:
                return dst[r0:r0 + 128, c0:c0 + wq]
            return dst[r0:r0 + 128 * group, :].rearrange(
                "(a p) c -> p a c", p=128)

        for p in range(npan):
            r0, c0, wq = panape(p)
            self.dma("sync", pan.ap(0, 0, wq), src_slice(r0, c0, wq),
                     [dkey(name)], kp, tag="qzl")
            self.act(pan2.ap(0, 0, wq), pan.ap(0, 0, wq), AF.Abs, kp, kp2,
                     tag="qza")
            self.red(pan2.ap(0, 0, 1), pan2.ap(0, 0, wq), ALU.add, kp2, kp2,
                     tag="qzr")
            self.tt(acc[:, 0:1], acc[:, 0:1], pan2.ap(0, 0, 1), ALU.add,
                    KA + kp2, KA, tag="qzacc")
        self.mm("psBC", self.psBC[0:1, 0:1], acc[:, 0:1], self.ones_col[:],
                True, True, KA, [("sc", "ones_col")], tag="qzs")
        self.copy(acc[0:1, 1:2], self.psBC[0:1, 0:1], [("ps", "psBC")], KA,
                  tag="qzc")
        self.mm("psBC", self.psBC[:, 0:1], self.ones128[0:1, :],
                acc[0:1, 1:2], True, True, [("sc", "ones128")], KA, tag="qzb")
        self.ts(acc[:, 1:2], self.psBC[:, 0:1],
                0.5 / (n_rows * cols), ALU.mult,
                [("ps", "psBC")], KA, s2=5e-7, op1=ALU.add, tag="qzg")
        for p in range(npan):
            r0, c0, wq = panape(p)
            self.dma("sync", pan.ap(0, 0, wq), src_slice(r0, c0, wq),
                     [dkey(name)], kp, tag="qzl2")
            self.act(pan2.ap(0, 0, wq), pan.ap(0, 0, wq), AF.Abs, kp, kp2,
                     tag="qza2")
            self.ts(pan2.ap(0, 0, wq), pan2.ap(0, 0, wq), acc[:, 1:2],
                    ALU.is_gt, kp2 + KA, kp2, tag="qzm")
            self.act(pan.ap(0, 0, wq), pan.ap(0, 0, wq), AF.Sign, kp, kp,
                     tag="qzs2")
            if lnw is None:
                self.tt(panb.ap(0, 0, wq), pan2.ap(0, 0, wq), pan.ap(0, 0, wq),
                        ALU.mult, kp2 + kp, kpb, tag="qzq")
            else:
                self.tt(pan.ap(0, 0, wq), pan2.ap(0, 0, wq), pan.ap(0, 0, wq),
                        ALU.mult, kp2 + kp, kp, tag="qzq1")
                self.ts(panb.ap(0, 0, wq), pan.ap(0, 0, wq),
                        lnw[:, (r0 // 128):(r0 // 128) + 1], ALU.mult,
                        kp + [("sc", "lnwc")], kpb, tag="qzq2")
            self.dma("scalar", dst_slice(r0, c0, wq), panb.ap(0, 0, wq),
                     kpb, [dkey("wq_" + name)], tag="qzw")
            if colsum_rows is not None:
                self.mm("psSC", self.psSC[0:1, c0:c0 + wq],
                        self.mones_colb[:], panb.ap(0, 0, wq),
                        r0 == 0, r0 == n_rows - 128,
                        [("sc", "mones_col")], kpb, tag="qzn")
                self.ts(panb2.ap(0, 0, wq), pan.ap(0, 0, wq),
                        lnb[:, (r0 // 128):(r0 // 128) + 1], ALU.mult,
                        kp + [("sc", "lnbc")], kpb2, tag="qzlb")
                psn2 = "psA" if c0 == 0 else "psB"
                ps2 = self.psA if c0 == 0 else self.psB
                self.mm(psn2, ps2[0:1, 0:wq], self.ones_colb[:],
                        panb2.ap(0, 0, wq), r0 == 0, r0 == n_rows - 128,
                        [("sc", "ones_colb")], kpb2, tag="qzcs")
        self.bfree(pan)
        self.bfree(pan2)
        self.bfree(panb)
        self.bfree(panb2)
        return dst

    def l1_ffn(self, xnL1, sL1):
        g = self.g
        TO = g.t_own
        if not hasattr(self, "mones_col"):
            self.mones_col = self.sbuf("mones_col", [128, 1])
            self.memset(self.mones_col[:], -1.0, [("sc", "mones_col")],
                        tag="mones")
        wqfc = self.quantize_w(self.wfctb, 32 * 128, "fc")
        negrow = self.psSC[0:1, 0:1024]
        lnbrow = self.psBC2[0:1, 0:512]
        wqpj = self.quantize_w(
            self.wprojT2, 4096, "pj", lnw=self.sb_lnwc, lnb=self.sb_lnbc,
            colsum_rows=True)
        csneg = self.sbuf("csneg", [128, NCH])
        cslnb = self.sbuf("cslnb", [128, NCH])
        KCS = [("sc", "csneg")]
        KCL = [("sc", "cslnb")]
        rowst = self.abuf("rowst", 1, 1024, BF16)
        self.copy(rowst.ap(0, 0, 1024, dp=0, dn=1), self.psSC[0:1, 0:1024],
                  [("ps", "psSC")], rowst.keys(0, 0, 1024), tag="csc1")
        dbounce = self.dram("csbounce", [2048], BF16)
        self.dma("gpsimd", dbounce[0:1024].rearrange("(o f) -> o f", o=1),
                 rowst.ap(0, 0, 1024, dp=0, dn=1), rowst.keys(0, 0, 1024),
                 [dkey("csbounce")], tag="csd1")
        self.copy(rowst.ap(0, 0, 512, dp=0, dn=1), self.psA[0:1, 0:512],
                  [("ps", "psA")], rowst.keys(0, 0, 1024), tag="csc2")
        self.copy(rowst.ap(0, 512, 1024, dp=0, dn=1), self.psB[0:1, 0:512],
                  [("ps", "psB")], rowst.keys(0, 0, 1024), tag="csc3")
        self.dma("gpsimd", dbounce[1024:2048].rearrange("(o f) -> o f", o=1),
                 rowst.ap(0, 0, 1024, dp=0, dn=1), rowst.keys(0, 0, 1024),
                 [dkey("csbounce")], tag="csd2")
        self.dma("gpsimd", csneg[:],
                 dbounce[0:1024].rearrange("(c p) -> p c", p=128),
                 [dkey("csbounce")], KCS, tag="csd3", slow_ok=True)
        self.dma("gpsimd", cslnb[:],
                 dbounce[1024:2048].rearrange("(c p) -> p c", p=128),
                 [dkey("csbounce")], KCL, tag="csd4", slow_ok=True)
        self.bfree(rowst)

        QT = TO // 4
        hbuf = self.abuf("l1h", 32, QT, BF16)
        wblk2 = [self.abuf(f"l1wb{k}", NCH, 128, BF16) for k in range(2)]
        wpb2 = [self.abuf(f"l1wp{k}", 1, 1024, BF16) for k in range(2)]
        sqp2 = self.abuf("l1hsq", 1, 512, BF16)
        lrow = self.sbuf("l1lrow", [1, 512])
        lrow2 = self.sbuf("l1lrow2", [1, 512])
        KLR = [("sc", "l1lrow")]
        KLR2 = [("sc", "l1lrow2")]
        gps2 = [self.abuf(f"l1gp{k}", 1, 512) for k in range(2)]
        g2ps2 = [self.abuf(f"l1g2{k}", 1, 512, BF16) for k in range(2)]
        xops2 = [self.abuf(f"l1xo{k}", 1, 512) for k in range(2)]
        praw = self.abuf("l1praw", NCH, QT, BF16)
        for qt in range(4):
            q0 = qt * QT
            for b in range(32):
                wblk = wblk2[b % 2]
                self.dma("sync" if b % 2 == 0 else "scalar",
                         wblk.full_ap(), wqfc[b * 128:(b + 1) * 128, :],
                         [dkey("wq_fc")], wblk.kall(), tag="hwl")
                for sp in range(0, QT, 512):
                    s1 = min(sp + 512, QT)
                    w = s1 - sp
                    psn, ps = ("psA", self.psA) if b % 2 == 0 else ("psB", self.psB)
                    for kc in range(NCH):
                        self.mm(psn, ps[:, 0:w], wblk.ap(kc, 0, 128),
                                xnL1.ap(kc, 1 + q0 + sp, 1 + q0 + s1),
                                kc == 0, kc == NCH - 1,
                                wblk.keys(kc, 0, 128),
                                xnL1.keys(kc, 1 + q0 + sp, 1 + q0 + s1),
                                tag="hmm")
                    self.act(hbuf.ap(b, sp, s1), ps[:, 0:w], AF.Gelu,
                             [("ps", psn)], hbuf.keys(b, sp, s1), tag="hev")
            if qt == 0:
                self.tap("hbuf0", hbuf.full_ap(), [128, 32 * QT],
                         hbuf.kall(), dtype=BF16)
            for sp in range(0, QT, 512):
                s1 = min(sp + 512, QT)
                w = s1 - sp
                for b in range(32):
                    self.act(sqp2.ap(0, 0, w), hbuf.ap(b, sp, s1), AF.Square,
                             hbuf.keys(b, sp, s1), sqp2.keys(0, 0, w),
                             tag="hsq")
                    self.mm("psBC", self.psBC[:, 0:w], self.ones128_b[:],
                            hbuf.ap(b, sp, s1), b == 0, b == 31,
                            [("sc", "ones128_b")], hbuf.keys(b, sp, s1),
                            tag="hst1")
                    self.mm("psBC2", self.psBC2[:, 0:w], self.ones128_b[:],
                            sqp2.ap(0, 0, w), b == 0, b == 31,
                            [("sc", "ones128_b")], sqp2.keys(0, 0, w),
                            tag="hst2")
                DH4 = 4096.0
                self.act(lrow[0:1, sp:s1], self.psBC[0:1, 0:w],
                         AF.Square, [("ps", "psBC")], KLR, tag="hr1")
                self.stt(lrow[0:1, sp:s1], lrow[0:1, sp:s1], -1.0 / DH4,
                         self.psBC2[0:1, 0:w], ALU.mult, ALU.add,
                         KLR + [("ps", "psBC2")], KLR, tag="hr2")
                self.act(lrow[0:1, sp:s1], lrow[0:1, sp:s1], AF.Ln, KLR, KLR,
                         bias=1e-5, scale=1.0 / DH4, tag="hr3")
                self.act(lrow[0:1, sp:s1], lrow[0:1, sp:s1], AF.Exp, KLR, KLR,
                         scale=-0.5, tag="hr4")
                self.ts(lrow2[0:1, sp:s1], self.psBC[0:1, 0:w], 1.0 / DH4,
                        ALU.mult, [("ps", "psBC")], KLR2, tag="hr5")
                self.tt(lrow2[0:1, sp:s1], lrow2[0:1, sp:s1], lrow[0:1, sp:s1],
                        ALU.mult, KLR2 + KLR, KLR2, tag="hr6")
            for b in range(32):
                wpb = wpb2[b % 2]
                self.dma("sync" if b % 2 == 0 else "scalar",
                         wpb.ap(0, 0, 1024, dp=0, dn=128),
                         wqpj[b * 128:(b + 1) * 128, :],
                         [dkey("wq_pj")],
                         wpb.keys(0, 0, 1024), tag="pwl")
                for oc in range(NCH):
                    for sp in range(0, QT, 512):
                        s1 = min(sp + 512, QT)
                        bank = ["psA", "psB", "psC", "psD", "psBC", "psBC2",
                                "psSC", "psSC"][oc]
                        ps = {"psA": self.psA, "psB": self.psB, "psC": self.psC,
                              "psD": self.psD, "psBC": self.psBC,
                              "psBC2": self.psBC2, "psSC": self.psSC}[bank]
                        off = 512 if oc == 7 else 0
                        self.mm(bank, ps[:, off:off + (s1 - sp)],
                                wpb.ap(0, oc * 128, oc * 128 + 128, dp=0, dn=128),
                                hbuf.ap(b, sp, s1), b == 0, b == 31,
                                wpb.keys(0, oc * 128, oc * 128 + 128),
                                hbuf.keys(b, sp, s1), tag="pjm")
            for oc in range(NCH):
                for sp in range(0, QT, 512):
                    s1 = min(sp + 512, QT)
                    w = s1 - sp
                    bank = ["psA", "psB", "psC", "psD", "psBC", "psBC2",
                            "psSC", "psSC"][oc]
                    ps = {"psA": self.psA, "psB": self.psB, "psC": self.psC,
                          "psD": self.psD, "psBC": self.psBC,
                          "psBC2": self.psBC2, "psSC": self.psSC}[bank]
                    off = 512 if oc == 7 else 0
                    self.copy(praw.ap(oc, sp, s1), ps[:, off:off + w],
                              [("ps", bank)], praw.keys(oc, sp, s1),
                              eng="scalar", tag="prev")
            for sp in range(0, QT, 512):
                s1 = min(sp + 512, QT)
                w = s1 - sp
                self.mm("psBC", self.psBC[:, 0:w], self.ones128[0:1, :],
                        lrow[0:1, sp:s1], True, True, [("sc", "ones128")],
                        KLR, tag="bcr")
                self.mm("psBC2", self.psBC2[:, 0:w], self.ones128[0:1, :],
                        lrow2[0:1, sp:s1], True, True, [("sc", "ones128")],
                        KLR2, tag="bcm")
                for oc in range(NCH):
                    en = "vector" if oc % 2 == 0 else "gpsimd"
                    gp_ = gps2[oc % 2]
                    g2p_ = g2ps2[oc % 2]
                    xop_ = xops2[oc % 2]
                    self.tt(gp_.ap(0, 0, w), praw.ap(oc, sp, s1),
                            self.psBC[:, 0:w], ALU.mult,
                            praw.keys(oc, sp, s1) + [("ps", "psBC")],
                            gp_.keys(0, 0, w), tag="fold1")
                    self.stt(gp_.ap(0, 0, w), self.psBC2[:, 0:w],
                             csneg[:, oc:oc + 1], gp_.ap(0, 0, w),
                             ALU.mult, ALU.add,
                             [("ps", "psBC2")] + KCS + gp_.keys(0, 0, w),
                             gp_.keys(0, 0, w), tag="fold2")
                    self.act(g2p_.ap(0, 0, w), gp_.ap(0, 0, w), AF.Silu,
                             gp_.keys(0, 0, w) + KCL, g2p_.keys(0, 0, w),
                             bias=cslnb[:, oc:oc + 1], tag="silu2")
                    self.dma("sync", xop_.ap(0, 0, w),
                             self.x1mod_spill[oc * 128:(oc + 1) * 128,
                                              1 + q0 + sp:1 + q0 + s1],
                             [dkey("x1mod_spill", oc, (1 + q0 + sp) // 512),
                              dkey("x1mod_spill", oc, (q0 + s1) // 512)],
                             xop_.keys(0, 0, w), tag="cbx")
                    self.tt(g2p_.ap(0, 0, w), g2p_.ap(0, 0, w),
                            sL1.ap(oc, q0 + sp, q0 + s1), ALU.mult,
                            g2p_.keys(0, 0, w) + sL1.keys(oc, q0 + sp, q0 + s1),
                            g2p_.keys(0, 0, w), eng=en, tag="sg2")
                    self.tt(xop_.ap(0, 0, w), xop_.ap(0, 0, w), g2p_.ap(0, 0, w),
                            ALU.add, xop_.keys(0, 0, w) + g2p_.keys(0, 0, w),
                            xop_.keys(0, 0, w), eng=en, tag="out")
                    self.dma("scalar", self.out_fm[oc * 128:(oc + 1) * 128,
                                                   q0 + sp:q0 + s1],
                             xop_.ap(0, 0, w), xop_.keys(0, 0, w),
                             [dkey("out_fm", oc, (q0 + sp) // 512)],
                             tag="outw")
        self.bfree(hbuf); self.bfree(sqp2)
        for b_ in wblk2 + wpb2:
            self.bfree(b_)
        for b_ in gps2 + g2ps2 + xops2:
            self.bfree(b_)
        self.bfree(praw)
        self.bfree(xnL1); self.bfree(sL1)

    def finalize(self, final_outputs):
        S = self.S
        final = [(k, S.counts[k]) for k in final_outputs if k in S.counts]
        sems = {}
        semlist = sorted(set(list(S.counts.keys())))
        with ExitStack() as es:
            for k in semlist:
                sems[k] = es.enter_context(self.nc.semaphore("sem_" + k))
            with self.nc.Block() as block:
                for eng in ENGINES:
                    def mk(e_name):
                        def emit_fn(eng_obj):
                            for rec in S.ops[e_name]:
                                for s, v in rec.waits:
                                    eng_obj.wait_ge(sems[s], v)
                                inst = rec.emit(eng_obj)
                                inst.then_inc(sems[rec.sem],
                                              16 if rec.kind == "dma" else 1)
                            for s, v in final:
                                eng_obj.wait_ge(sems[s], v)
                        return emit_fn
                    getattr(block, eng)(mk(eng))
        self.stack.close()
        return self.nc



def _build(geom, taps=()):
    P = Program(geom, taps)
    P.declare()
    P.phase_xn()
    P.phase_l0_attn()
    P.phase_l0_ffn()
    P.phase_combine()
    P.phase_l1()
    nc = P.finalize([k for k in P.S.counts if "dma" in k])
    return nc, P


def kernel(**inputs):
    g = GEOM_FULL
    in_maps = stage_inputs(inputs, g)
    nc, _ = _build(g)
    res = run_bass_kernel_spmd(nc, in_maps, core_ids=list(range(N_CORES)))
    return unstage_output(res.results, g)
